# version 03
# baseline (speedup 1.0000x reference)
import contextlib
import numpy as np
import ml_dtypes
import concourse.bass as bass
import concourse.mybir as mybir
from concourse.bass_utils import run_bass_kernel_spmd

F32 = mybir.dt.float32
BF16 = mybir.dt.bfloat16
I16 = mybir.dt.int16
AF = mybir.ActivationFunctionType
ALU = mybir.AluOpType

D = 4096
DI = 14336
NKC = D // 128
NOC = DI // 128
NH = 32
S = 8192
NOWN = 1024
HALO = 256
NEXT = NOWN + 2 * HALO
SCALE = 1.0 / float(np.sqrt(128.0))
EPS = 1e-5

WIN_Q = [(0, 256), (0, 512), (256, 512), (512, 512), (768, 256)]
WIN_WOFF = [0, 256, 768, 1280, 1792]

_ENG_ALL = list(mybir.ALL_ENGINES)


class SemC:
    def __init__(self, stk, nc, name):
        self.h = stk.enter_context(nc.semaphore(name))
        self.n = 0


class Rg:
    def __init__(self, ps, eng, name, unroll):
        self.unroll = unroll
        self.eng = eng
        if not unroll:
            self.r = ps.enter_context(eng.register(name))
            eng.reg_mov(self.r, 0)
            self.cur0 = 0

    def to(self, v):
        if self.unroll:
            return int(v)
        d = int(v) - self.cur0
        if d > 0:
            self.eng.reg_add(self.r, self.r, d)
        elif d < 0:
            self.eng.reg_sub(self.r, self.r, -d)
        self.cur0 = int(v)
        return self.r


@contextlib.contextmanager
def raw_fori(nc, n, tag):
    regs = nc.alloc_registers(f"fl{tag}_i", engines=_ENG_ALL)
    nc.regs_mov(regs, 0)
    lbl, end = f"fl{tag}_L", f"fl{tag}_E"
    nc.br(lbl, engines=_ENG_ALL)
    with nc.body(lbl, valid_engines=_ENG_ALL):
        yield
        nc.regs_alu(regs, regs, 1, op=ALU.add)
        nc.br_lt(regs, n, on_true=lbl, on_false=end, engines=_ENG_ALL)
    nc.switch_bb(end)
    for hdl in regs.handles:
        nc.free_register(hdl)


def _free_lowering_temps(nc, eng_enum, n0):
    from bass_rust import RegisterHandle
    for nm in nc.inst_names()[n0:]:
        inst = nc.find_inst_by_name(nm)
        if inst is None:
            continue
        if type(inst).__name__ in ("InstRegisterMove", "InstRegisterAlu"):
            for a in list(inst.outs):
                r = getattr(a, "regref", None)
                if isinstance(r, str) and ("_snap_" in r or "_tmp_" in r):
                    try:
                        nc.free_register(RegisterHandle(r, eng_enum))
                    except Exception:
                        pass


@contextlib.contextmanager
def free_temps(nc, eng_enum):
    n0 = nc.n_instructions()
    yield
    _free_lowering_temps(nc, eng_enum, n0)


def drive(nc, n, unroll, bumps, body, tag):
    if unroll:
        for j in range(n):
            body(j)
    else:
        entry = [rg.cur0 for rg, _ in bumps]
        with raw_fori(nc, n, tag):
            body(0)
            for (rg, rate), e in zip(bumps, entry):
                rg.to(e + rate)
        for (rg, rate), e in zip(bumps, entry):
            rg.cur0 = e + rate * n


def build_nc(debug=False, unroll=False):
    nc = bass.Bass("TRN2", target_bir_lowering=False,
                   detect_race_conditions=bool(unroll))

    xe = nc.dram_tensor("xe", [D, NEXT], F32, kind="ExternalInput").ap()
    gw = nc.dram_tensor("gw", [DI, D], I16, kind="ExternalInput").ap()
    uw = nc.dram_tensor("uw", [DI, D], I16, kind="ExternalInput").ap()
    dw = nc.dram_tensor("dw", [D, DI], I16, kind="ExternalInput").ap()
    wq = nc.dram_tensor("wq", [D, D], F32, kind="ExternalInput").ap()
    wk = nc.dram_tensor("wk", [128, D], F32, kind="ExternalInput").ap()
    wv = nc.dram_tensor("wv", [128, D], F32, kind="ExternalInput").ap()
    wo = nc.dram_tensor("wo", [D, D], F32, kind="ExternalInput").ap()
    gusI = nc.dram_tensor("gus", [128, 2 * NOC], F32, kind="ExternalInput").ap()
    dsI = nc.dram_tensor("ds", [128, NKC], F32, kind="ExternalInput").ap()
    oneI = nc.dram_tensor("onev", [128, NKC], F32, kind="ExternalInput").ap()
    cosI = nc.dram_tensor("cosE", [128, NEXT], F32, kind="ExternalInput").ap()
    sinI = nc.dram_tensor("sinE", [128, NEXT], F32, kind="ExternalInput").ap()
    wgtI = nc.dram_tensor("wgt", [128, 2048], F32, kind="ExternalInput").ap()
    onesmI = nc.dram_tensor("onesm", [128, 12 * 128], BF16, kind="ExternalInput").ap()
    identI = nc.dram_tensor("ident", [128, 128], BF16, kind="ExternalInput").ap()

    yT = nc.dram_tensor("yT", [D, NOWN], F32, kind="ExternalOutput").ap()

    SCK = "ExternalOutput" if debug else "Internal"
    SCK2 = "ExternalOutput" if debug == 2 else "Internal"
    x1sc = nc.dram_tensor("x1sc", [D, NEXT], F32, kind=SCK).ap()
    x2sc = nc.dram_tensor("x2sc", [D, NOWN], F32, kind=SCK).ap()
    guT_sc = nc.dram_tensor("guT_sc", [2 * NOC * 128, D], BF16, kind=SCK2).ap()
    qT_sc = nc.dram_tensor("qT_sc", [NH * 128, D], BF16, kind=SCK2).ap()
    dT_sc = nc.dram_tensor("dT_sc", [DI, D], BF16, kind=SCK2).ap()
    oT_sc = nc.dram_tensor("oT_sc", [D, D], BF16, kind=SCK2).ap()
    hsc = nc.dram_tensor("hsc", [NOC * 128, NEXT], BF16, kind=SCK2).ap()
    attnsc = nc.dram_tensor("attnsc", [NH * 128, NOWN], BF16, kind=SCK2).ap()

    stk = contextlib.ExitStack()
    with stk:
        sT = SemC(stk, nc, "sT")
        sA = SemC(stk, nc, "sA")
        sV = SemC(stk, nc, "sV")
        sP = SemC(stk, nc, "sP")
        _ctr = [0]

        def fresh_dma_sems():
            _ctr[0] += 1
            return SemC(stk, nc, f"dL{_ctr[0]}"), SemC(stk, nc, f"dS{_ctr[0]}")

        def uid():
            _ctr[0] += 1
            return _ctr[0]

        cosE = stk.enter_context(nc.sbuf_tensor("cosE_s", [128, NEXT], F32))
        sinE = stk.enter_context(nc.sbuf_tensor("sinE_s", [128, NEXT], F32))
        wgt = stk.enter_context(nc.sbuf_tensor("wgt_s", [128, 2048], F32))
        onesm = stk.enter_context(nc.sbuf_tensor("onesm_s", [128, 12 * 128], BF16))
        ident = stk.enter_context(nc.sbuf_tensor("ident_s", [128, 128], BF16))
        dshS = stk.enter_context(nc.sbuf_tensor("dsh_s", [128, NKC], F32))
        onebf = stk.enter_context(nc.sbuf_tensor("onebf_s", [128, 128], BF16))
        dsh_dram = nc.dram_tensor("dsh_dram", [128, NKC], F32, kind="Internal").ap()

        cL0, cS0 = fresh_dma_sems()
        with nc.Block() as b0:
            @b0.sync
            def _(s):
                for dst_, src_ in ((cosE, cosI), (sinE, sinI), (wgt, wgtI), (onesm, onesmI),
                                   (ident, identI), (dshS, dsI)):
                    s.dma_start(out=dst_[:, :], in_=src_[:, :]).then_inc(cL0.h, 16)
                s.wait_ge(cL0.h, 96)

            @b0.vector
            def _(v):
                v.wait_ge(cL0.h, 96)
                v.tensor_scalar_mul(dshS[:, :], dshS[:, :], 0.5).then_inc(sV.h, 1)
                v.wait_ge(sV.h, sV.n + 1)

            @b0.scalar
            def _(sc):
                sc.wait_ge(sV.h, sV.n + 1)
                sc.dma_start(out=dsh_dram[:, :], in_=dshS[:, :]).then_inc(cS0.h, 16)
                sc.wait_ge(cS0.h, 16)

            @b0.gpsimd
            def _(g):
                g.memset(onebf[:, :], 1.0).then_inc(sP.h, 1)
        cL0.n += 96
        cS0.n += 16
        sV.n += 1
        sP.n += 1
        nc.all_engine_barrier()

        def prolog_slab(src, n_o, dst, src_f32, slab_base=0, slab_step=1):
            dL0, dS0 = fresh_dma_sems()
            dL1, dS1 = fresh_dma_sems()
            dLp, dSp = [dL0, dL1], [dS0, dS1]
            sdt = F32 if src_f32 else I16
            u = uid()
            W = D
            with contextlib.ExitStack() as ps:
                NI = [ps.enter_context(nc.sbuf_tensor(f"pni{h}_{u}", [128, W], sdt)) for h in range(2)]
                CB = [ps.enter_context(nc.sbuf_tensor(f"pcb{h}_{u}", [128, W], BF16)) for h in range(2)]
                WT = [ps.enter_context(nc.sbuf_tensor(f"pwt{h}_{u}", [128, W], BF16)) for h in range(2)]
                PT = [ps.enter_context(nc.psum_tensor(f"ppt{h}_{u}", [128, 2048], BF16)) for h in range(4)]
                bP0, bT0, bA0 = sP.n, sT.n, sA.n
                nc.gpsimd.sem_inc(sP.h, 2)
                nc.tensor.sem_inc(sT.h, 4)
                nc.scalar.sem_inc(sA.h, 4)
                nc.scalar.sem_inc(dS0.h, 16)
                nc.scalar.sem_inc(dS1.h, 16)
                dS0.n += 16
                dS1.n += 16
                gLd2 = [Rg(ps, nc.gpsimd, f"gLd2{h}_{u}", unroll) for h in range(2)]
                aSs2 = [Rg(ps, nc.scalar, f"aSs2{h}_{u}", unroll) for h in range(2)]
                ySp = Rg(ps, nc.sync, f"ySp{u}", unroll)
                ySo = Rg(ps, nc.sync, f"ySo{u}", unroll)
                gTt = Rg(ps, nc.gpsimd, f"gTt{u}", unroll)
                tPc = Rg(ps, nc.tensor, f"tPc{u}", unroll)
                tAe = Rg(ps, nc.tensor, f"tAe{u}", unroll)
                aTt = Rg(ps, nc.scalar, f"aTt{u}", unroll)
                aAo = Rg(ps, nc.scalar, f"aAo{u}", unroll)
                aOf = Rg(ps, nc.scalar, f"aOf{u}", unroll)

                def body(j):
                    for h in range(2):
                        o = 2 * j + h
                        pt0, pt1 = PT[2 * h], PT[2 * h + 1]
                        nc.sync.wait_ge(sP.h, ySp.to(bP0 + 2 + o - 1))
                        with free_temps(nc, mybir.EngineType.SP):
                            nc.sync.dma_start(out=NI[h][:, :],
                                              in_=bass.AP(src.tensor, ySo.to(o * 128 * W),
                                                          [[W, 128], [1, W]])).then_inc(dLp[h].h, 16)
                        nc.gpsimd.wait_ge(dLp[h].h, gLd2[h].to(16 * j + 16))
                        nc.gpsimd.wait_ge(sT.h, gTt.to(bT0 + 4 + 2 * o - 2))
                        nc.gpsimd.tensor_copy(CB[h][:, :], NI[h][:, :]).then_inc(sP.h, 1)
                        nc.tensor.wait_ge(sP.h, tPc.to(bP0 + 2 + o + 1))
                        nc.tensor.wait_ge(sA.h, tAe.to(bA0 + 4 + 2 * o - 2))
                        for k in range(NKC):
                            pt = pt0 if k < 16 else pt1
                            i = nc.tensor.transpose(pt[:, (k % 16) * 128:(k % 16 + 1) * 128],
                                                    CB[h][:, k * 128:(k + 1) * 128], ident[:, :])
                            if k in (15, 31):
                                i.then_inc(sT.h, 1)
                        nc.scalar.wait_ge(dSp[h].h, aSs2[h].to(16 * j))
                        nc.scalar.wait_ge(sT.h, aTt.to(bT0 + 4 + 2 * o + 1))
                        nc.scalar.copy(WT[h][:, 0:2048], pt0[:, :]).then_inc(sA.h, 1)
                        nc.scalar.wait_ge(sT.h, aTt.to(bT0 + 4 + 2 * o + 2))
                        nc.scalar.copy(WT[h][:, 2048:4096], pt1[:, :]).then_inc(sA.h, 1)
                        nc.scalar.wait_ge(sA.h, aAo.to(bA0 + 4 + 2 * o + 2))
                        with free_temps(nc, mybir.EngineType.Activation):
                            nc.scalar.dma_start(out=bass.AP(dst.tensor,
                                                            aOf.to((slab_base + slab_step * o) * 128 * W),
                                                            [[W, 128], [1, W]]),
                                                in_=WT[h][:, :]).then_inc(dSp[h].h, 16)

                drive(nc, n_o // 2, unroll,
                      [(ySp, 2), (ySo, 2 * 128 * W), (gLd2[0], 16), (gLd2[1], 16),
                       (gTt, 4), (tPc, 2), (tAe, 4), (aSs2[0], 16), (aSs2[1], 16),
                       (aTt, 4), (aAo, 4), (aOf, slab_step * 2 * 128 * W)],
                      body, f"ps{u}")
                nit = n_o // 2
                dL0.n += 16 * nit
                dL1.n += 16 * nit
                sP.n += 2 + 2 * nit
                sT.n += 4 + 4 * nit
                sA.n += 4 + 4 * nit
                dS0.n += 16 * nit
                dS1.n += 16 * nit
                nc.scalar.wait_ge(dS0.h, dS0.n)
                nc.scalar.wait_ge(dS1.h, dS1.n)
                nc.sync.wait_ge(dL0.h, dL0.n)
                nc.sync.wait_ge(dL1.h, dL1.n)
            nc.all_engine_barrier()

        def prolog_rowmajor(src, n_or, n_kc, dst, src_f32):
            dL0, dS = fresh_dma_sems()
            dL1, _u2 = fresh_dma_sems()
            dLp = [dL0, dL1]
            sdt = F32 if src_f32 else I16
            W = n_kc * 128
            M = n_or * 128
            nsub = (n_kc + 15) // 16
            u = uid()
            with contextlib.ExitStack() as ps:
                NI = [ps.enter_context(nc.sbuf_tensor(f"rni{h}_{u}", [128, W], sdt)) for h in range(2)]
                CB = [ps.enter_context(nc.sbuf_tensor(f"rcb{h}_{u}", [128, W], BF16)) for h in range(2)]
                DT = ps.enter_context(nc.sbuf_tensor(f"rdt_{u}", [128, W], BF16))
                PT = [ps.enter_context(nc.psum_tensor(f"rpt{h}_{u}", [128, 2048], BF16)) for h in range(2)]
                bP0, bT0, bA0, bS0 = sP.n, sT.n, sA.n, dS.n
                nc.gpsimd.sem_inc(sP.h, 2)
                nc.tensor.sem_inc(sT.h, 2 * nsub)
                nc.scalar.sem_inc(sA.h, 2 * nsub)
                nc.scalar.sem_inc(dS.h, 16)
                dS.n += 16
                bS1 = dS.n
                ySp = Rg(ps, nc.sync, f"ySp{u}", unroll)
                ySo = Rg(ps, nc.sync, f"ySo{u}", unroll)
                gLd2 = [Rg(ps, nc.gpsimd, f"gLd2{h}_{u}", unroll) for h in range(2)]
                gTt = Rg(ps, nc.gpsimd, f"gTt{u}", unroll)
                tPc = Rg(ps, nc.tensor, f"tPc{u}", unroll)
                tAe = Rg(ps, nc.tensor, f"tAe{u}", unroll)
                aSs = Rg(ps, nc.scalar, f"aSs{u}", unroll)
                aTt = Rg(ps, nc.scalar, f"aTt{u}", unroll)
                aAo = Rg(ps, nc.scalar, f"aAo{u}", unroll)
                aOf = Rg(ps, nc.scalar, f"aOf{u}", unroll)

                def body(j):
                    for h in range(2):
                        o = 2 * j + h
                        nc.sync.wait_ge(sP.h, ySp.to(bP0 + 2 + o - 1))
                        with free_temps(nc, mybir.EngineType.SP):
                            nc.sync.dma_start(out=NI[h][:, :],
                                              in_=bass.AP(src.tensor, ySo.to(o * 128 * W),
                                                          [[W, 128], [1, W]])).then_inc(dLp[h].h, 16)
                        nc.gpsimd.wait_ge(dLp[h].h, gLd2[h].to(16 * j + 16))
                        nc.gpsimd.wait_ge(sT.h, gTt.to(bT0 + 2 * nsub + nsub * (o - 1)))
                        nc.gpsimd.tensor_copy(CB[h][:, :], NI[h][:, :]).then_inc(sP.h, 1)
                        for sb in range(nsub):
                            k0, k1 = sb * 16, min((sb + 1) * 16, n_kc)
                            pt = PT[sb % 2]
                            if sb < 2:
                                nc.tensor.wait_ge(sP.h, tPc.to(bP0 + 2 + o + 1))
                            nc.tensor.wait_ge(sA.h, tAe.to(bA0 + 2 * nsub + nsub * o + max(sb - 1, 0)))
                            for k in range(k0, k1):
                                i = nc.tensor.transpose(pt[:, (k - k0) * 128:(k - k0 + 1) * 128],
                                                        CB[h][:, k * 128:(k + 1) * 128], ident[:, :])
                            i.then_inc(sT.h, 1)
                            if sb == 0:
                                nc.scalar.wait_ge(dS.h, aSs.to(bS1 + 16 * o))
                            nc.scalar.wait_ge(sT.h, aTt.to(bT0 + 2 * nsub + nsub * o + sb + 1))
                            nc.scalar.copy(DT[:, k0 * 128:k1 * 128],
                                           pt[:, 0:(k1 - k0) * 128]).then_inc(sA.h, 1)
                        nc.scalar.wait_ge(sA.h, aAo.to(bA0 + 2 * nsub + nsub * o + nsub))
                        with free_temps(nc, mybir.EngineType.Activation):
                            nc.scalar.dma_start(
                                out=bass.AP(dst.tensor, aOf.to(o * 128),
                                            [[M, 128], [128 * M, n_kc], [1, 128]]),
                                in_=DT[:, :].rearrange("p (k c) -> p k c", c=128),
                            ).then_inc(dS.h, 16)

                drive(nc, n_or // 2, unroll,
                      [(ySp, 2), (ySo, 2 * 128 * W), (gLd2[0], 16), (gLd2[1], 16),
                       (gTt, 2 * nsub), (tPc, 2), (tAe, 2 * nsub), (aSs, 32),
                       (aTt, 2 * nsub), (aAo, 2 * nsub), (aOf, 2 * 128)],
                      body, f"pr{u}")
                nit = n_or // 2
                dL0.n += 16 * nit
                dL1.n += 16 * nit
                sP.n += 2 + 2 * nit
                sT.n += 2 * nsub + 2 * nsub * nit
                sA.n += 2 * nsub + 2 * nsub * nit
                dS.n += 32 * nit
                nc.scalar.wait_ge(dS.h, dS.n)
                nc.sync.wait_ge(dL0.h, dL0.n)
                nc.sync.wait_ge(dL1.h, dL1.n)
            nc.all_engine_barrier()

        prolog_slab(gw, NOC, guT_sc, False, slab_base=0, slab_step=2)
        prolog_slab(uw, NOC, guT_sc, False, slab_base=1, slab_step=2)
        prolog_slab(wq, NH, qT_sc, True)
        prolog_rowmajor(dw, NKC, NOC, dT_sc, False)
        prolog_rowmajor(wo, NKC, NKC, oT_sc, True)

        XN = stk.enter_context(nc.sbuf_tensor("XN", [128, NKC * NEXT], BF16))

        def emit_ln(src, nt, xn_stride):
            dL0, _u1 = fresh_dma_sems()
            dL1, _u2 = fresh_dma_sems()
            dLp = [dL0, dL1]
            srcv = src.rearrange("(g p) n -> p g n", p=128)
            ntt = nt // 512
            u = uid()
            rL, rT, rA, rV = 16 * 8, 8, 7, 71
            with contextlib.ExitStack() as ps:
                XF = [ps.enter_context(nc.sbuf_tensor(f"lxf{h}_{u}", [128, 4096], F32)) for h in range(2)]
                XB = ps.enter_context(nc.sbuf_tensor(f"lxb_{u}", [128, 4096], BF16))
                XQ = ps.enter_context(nc.sbuf_tensor(f"lxq_{u}", [128, 4096], BF16))
                MB = ps.enter_context(nc.sbuf_tensor(f"lmb_{u}", [128, 512], F32))
                T2 = ps.enter_context(nc.sbuf_tensor(f"lt2_{u}", [128, 512], F32))
                VV = ps.enter_context(nc.sbuf_tensor(f"lvv_{u}", [128, 512], F32))
                RS = ps.enter_context(nc.sbuf_tensor(f"lrs_{u}", [128, 512], F32))
                DDb = [ps.enter_context(nc.sbuf_tensor(f"ldd{h}_{u}", [128, 512], F32)) for h in range(2)]
                PM = ps.enter_context(nc.psum_tensor(f"lpm_{u}", [128, 512], F32))
                PQ = ps.enter_context(nc.psum_tensor(f"lpq_{u}", [128, 512], F32))
                bT0, bA0, bV0 = sT.n, sA.n, sV.n
                for tt in range(ntt):
                    cv = ca = ct = 0
                    for cg in range(4):
                        par = cg % 2
                        nc.sync.wait_ge(sV.h, bV0 + rV * tt)
                        if cg >= 2:
                            nc.sync.wait_ge(sV.h, bV0 + rV * tt + cg - 1)
                            nc.sync.wait_ge(sA.h, bA0 + rA * tt + cg - 1)
                        nc.sync.dma_start(
                            out=XF[par][:, :].rearrange("p (g n) -> p g n", n=512),
                            in_=srcv[:, cg * 8:(cg + 1) * 8, tt * 512:tt * 512 + 512],
                        ).then_inc(dLp[par].h, 16)
                        ldv = 64 * tt + 16 * (cg // 2 + 1)
                        nc.vector.wait_ge(dLp[par].h, ldv)
                        nc.vector.wait_ge(sT.h, bT0 + rT * tt + 2 * cg)
                        nc.vector.tensor_copy(XB[:, :], XF[par][:, :]).then_inc(sV.h, 1); cv += 1
                        nc.scalar.wait_ge(dLp[par].h, ldv)
                        nc.scalar.wait_ge(sT.h, bT0 + rT * tt + 2 * cg)
                        nc.scalar.activation(XQ[:, :], XF[par][:, :], AF.Square).then_inc(sA.h, 1); ca += 1
                        nc.tensor.wait_ge(sV.h, bV0 + rV * tt + cv)
                        for c in range(8):
                            i = nc.tensor.matmul(PM[:, :], onebf[:, :], XB[:, c * 512:(c + 1) * 512],
                                                 start=(cg == 0 and c == 0), stop=(cg == 3 and c == 7),
                                                 skip_group_check=True)
                        i.then_inc(sT.h, 1); ct += 1
                        nc.tensor.wait_ge(sA.h, bA0 + rA * tt + ca)
                        for c in range(8):
                            i = nc.tensor.matmul(PQ[:, :], onebf[:, :], XQ[:, c * 512:(c + 1) * 512],
                                                 start=(cg == 0 and c == 0), stop=(cg == 3 and c == 7),
                                                 skip_group_check=True)
                        i.then_inc(sT.h, 1); ct += 1
                    nc.vector.wait_ge(sT.h, bT0 + rT * tt + 8)
                    nc.vector.tensor_scalar_mul(MB[:, :], PM[:, :], 1.0 / D).then_inc(sV.h, 1); cv += 1
                    nc.scalar.wait_ge(sT.h, bT0 + rT * tt + 8)
                    nc.scalar.mul(T2[:, :], PQ[:, :], 1.0 / D).then_inc(sA.h, 1); ca += 1
                    nc.vector.wait_ge(sV.h, bV0 + rV * tt + cv)
                    nc.vector.scalar_tensor_tensor(VV[:, :], MB[:, :], -1.0, MB[:, :],
                                                   ALU.mult, ALU.mult).then_inc(sV.h, 1); cv += 1
                    nc.vector.wait_ge(sV.h, bV0 + rV * tt + cv)
                    nc.vector.wait_ge(sA.h, bA0 + rA * tt + ca)
                    nc.vector.scalar_tensor_tensor(VV[:, :], VV[:, :], EPS, T2[:, :],
                                                   ALU.add, ALU.add).then_inc(sV.h, 1); cv += 1
                    nc.scalar.wait_ge(sV.h, bV0 + rV * tt + cv)
                    nc.scalar.activation(RS[:, :], VV[:, :], AF.Ln).then_inc(sA.h, 1); ca += 1
                    nc.scalar.wait_ge(sA.h, bA0 + rA * tt + ca)
                    nc.scalar.activation(RS[:, :], RS[:, :], AF.Exp, scale=-0.5).then_inc(sA.h, 1); ca += 1
                    assert ca == rA and ct == rT
                    for cg in range(4):
                        par = cg % 2
                        nc.sync.wait_ge(sV.h, bV0 + rV * tt + 7 + 16 * cg)
                        nc.sync.dma_start(
                            out=XF[par][:, :].rearrange("p (g n) -> p g n", n=512),
                            in_=srcv[:, cg * 8:(cg + 1) * 8, tt * 512:tt * 512 + 512],
                        ).then_inc(dLp[par].h, 16)
                        nc.vector.wait_ge(dLp[par].h, 64 * tt + 32 + 16 * (cg // 2 + 1))
                        nc.vector.wait_ge(sA.h, bA0 + rA * tt + rA)
                        for c2 in range(4):
                            nc.vector.wait_ge(sV.h, bV0 + rV * tt + cv)
                            for c, dp in ((2 * c2, 0), (2 * c2 + 1, 1)):
                                nc.vector.tensor_sub(DDb[dp][:, :], XF[par][:, c * 512:(c + 1) * 512],
                                                     MB[:, :]).then_inc(sV.h, 1); cv += 1
                            nc.vector.wait_ge(sV.h, bV0 + rV * tt + cv)
                            for c, dp in ((2 * c2, 0), (2 * c2 + 1, 1)):
                                cc = cg * 8 + c
                                nc.vector.tensor_mul(
                                    XN[:, cc * xn_stride + tt * 512:cc * xn_stride + tt * 512 + 512],
                                    DDb[dp][:, :], RS[:, :]).then_inc(sV.h, 1); cv += 1
                    assert cv == rV, (cv, rV)
                dL0.n += 64 * ntt
                dL1.n += 64 * ntt
                sT.n += rT * ntt
                sA.n += rA * ntt
                sV.n += rV * ntt
                nc.sync.wait_ge(dL0.h, dL0.n)
                nc.sync.wait_ge(dL1.h, dL1.n)
            nc.all_engine_barrier()

        def emit_sweep1(nt, xn_stride):
            ntt = nt // 512
            dL0, dS0 = fresh_dma_sems()
            dL1, dS1 = fresh_dma_sems()
            dLp, dSp = [dL0, dL1], [dS0, dS1]
            u = uid()
            with contextlib.ExitStack() as ps:
                GUB = [ps.enter_context(nc.sbuf_tensor(f"s1gu{h}_{u}", [128, 2 * D], BF16)) for h in range(2)]
                GUC = [ps.enter_context(nc.sbuf_tensor(f"s1gc{h}_{u}", [128, 2], F32)) for h in range(2)]
                SG = ps.enter_context(nc.sbuf_tensor(f"s1sg_{u}", [128, ntt * 512], BF16))
                HH = [ps.enter_context(nc.sbuf_tensor(f"s1h{h}_{u}", [128, ntt * 512], BF16)) for h in range(2)]
                PG = [ps.enter_context(nc.psum_tensor(f"s1pg{t}_{u}", [128, 512], F32)) for t in range(ntt)]
                PU = [ps.enter_context(nc.psum_tensor(f"s1pu{t}_{u}", [128, 512], F32)) for t in range(ntt)]
                bT0, bA0, bV0 = sT.n, sA.n, sV.n
                nc.tensor.sem_inc(sT.h, 4 * ntt)
                nc.scalar.sem_inc(sA.h, 2 * ntt)
                nc.vector.sem_inc(sV.h, 2 * ntt)
                nc.scalar.sem_inc(dS0.h, 16)
                nc.scalar.sem_inc(dS1.h, 16)
                dS0.n += 16
                dS1.n += 16
                tLd2 = [Rg(ps, nc.tensor, f"tLd2{h}_{u}", unroll) for h in range(2)]
                aLd2 = [Rg(ps, nc.scalar, f"aLd2{h}_{u}", unroll) for h in range(2)]
                vLd2 = [Rg(ps, nc.vector, f"vLd2{h}_{u}", unroll) for h in range(2)]
                vSs2 = [Rg(ps, nc.vector, f"vSs2{h}_{u}", unroll) for h in range(2)]
                yTm = Rg(ps, nc.sync, f"yTm{u}", unroll)
                ySv = Rg(ps, nc.sync, f"ySv{u}", unroll)
                yOf = Rg(ps, nc.sync, f"yOf{u}", unroll)
                yGc = Rg(ps, nc.sync, f"yGc{u}", unroll)
                tAs = Rg(ps, nc.tensor, f"tAs{u}", unroll)
                tVh = Rg(ps, nc.tensor, f"tVh{u}", unroll)
                aTg = Rg(ps, nc.scalar, f"aTg{u}", unroll)
                aVh = Rg(ps, nc.scalar, f"aVh{u}", unroll)
                aHo = Rg(ps, nc.scalar, f"aHo{u}", unroll)
                vTu = Rg(ps, nc.vector, f"vTu{u}", unroll)
                vAs = Rg(ps, nc.vector, f"vAs{u}", unroll)
                vL2 = Rg(ps, nc.vector, f"vL2{u}", unroll)

                def body(j):
                    for h in range(2):
                        o = 2 * j + h
                        nc.sync.wait_ge(sT.h, yTm.to(bT0 + 4 * ntt + 2 * ntt * (o - 1)))
                        nc.sync.wait_ge(sV.h, ySv.to(bV0 + 2 * ntt + ntt * (o - 1)))
                        with free_temps(nc, mybir.EngineType.SP):
                            nc.sync.dma_start(
                                out=GUB[h][:, :].rearrange("p (g n) -> p g n", n=D),
                                in_=bass.AP(guT_sc.tensor, yOf.to(2 * o * 128 * D),
                                            [[D, 128], [128 * D, 2], [1, D]])).then_inc(dLp[h].h, 16)
                        with free_temps(nc, mybir.EngineType.SP):
                            nc.sync.dma_start(
                                out=GUC[h][:, :],
                                in_=bass.AP(gusI.tensor, yGc.to(2 * o),
                                            [[2 * NOC, 128], [1, 2]])).then_inc(dLp[h].h, 16)
                        nc.tensor.wait_ge(dLp[h].h, tLd2[h].to(32 * j + 32))
                        nc.tensor.wait_ge(sA.h, tAs.to(bA0 + 2 * ntt + ntt * o))
                        for t in range(ntt):
                            for k in range(NKC):
                                i = nc.tensor.matmul(
                                    PG[t][:, :], GUB[h][:, k * 128:(k + 1) * 128],
                                    XN[:, k * xn_stride + t * 512:k * xn_stride + t * 512 + 512],
                                    start=(k == 0), stop=(k == NKC - 1), skip_group_check=True)
                            i.then_inc(sT.h, 1)
                        nc.tensor.wait_ge(sV.h, tVh.to(bV0 + 2 * ntt + ntt * o))
                        for t in range(ntt):
                            for k in range(NKC):
                                i = nc.tensor.matmul(
                                    PU[t][:, :], GUB[h][:, D + k * 128:D + (k + 1) * 128],
                                    XN[:, k * xn_stride + t * 512:k * xn_stride + t * 512 + 512],
                                    start=(k == 0), stop=(k == NKC - 1), skip_group_check=True)
                            i.then_inc(sT.h, 1)
                        nc.scalar.wait_ge(sT.h, aTg.to(bT0 + 4 * ntt + 2 * ntt * o + ntt))
                        nc.scalar.wait_ge(sV.h, aVh.to(bV0 + 2 * ntt + ntt * o))
                        nc.scalar.wait_ge(dLp[h].h, aLd2[h].to(32 * j + 32))
                        for t in range(ntt):
                            nc.scalar.activation(SG[:, t * 512:(t + 1) * 512], PG[t][:, :], AF.Silu,
                                                 scale=GUC[h][:, 0:1]).then_inc(sA.h, 1)
                        nc.vector.wait_ge(sT.h, vTu.to(bT0 + 4 * ntt + 2 * ntt * o + 2 * ntt))
                        nc.vector.wait_ge(sA.h, vAs.to(bA0 + 2 * ntt + ntt * o + ntt))
                        nc.vector.wait_ge(dSp[h].h, vSs2[h].to(16 * j))
                        nc.vector.wait_ge(dLp[h].h, vLd2[h].to(32 * j + 32))
                        for t in range(ntt):
                            nc.vector.scalar_tensor_tensor(
                                HH[h][:, t * 512:(t + 1) * 512], PU[t][:, :], GUC[h][:, 1:2],
                                SG[:, t * 512:(t + 1) * 512], ALU.mult, ALU.mult).then_inc(sV.h, 1)
                        nc.scalar.wait_ge(sV.h, aVh.to(bV0 + 2 * ntt + ntt * o + ntt))
                        with free_temps(nc, mybir.EngineType.Activation):
                            nc.scalar.dma_start(out=bass.AP(hsc.tensor, aHo.to(o * 128 * NEXT),
                                                            [[NEXT, 128], [1, nt]]),
                                                in_=HH[h][:, :]).then_inc(dSp[h].h, 16)

                drive(nc, NOC // 2, unroll,
                      [(yTm, 4 * ntt), (ySv, 2 * ntt), (yOf, 4 * 128 * D), (yGc, 4),
                       (tLd2[0], 32), (tLd2[1], 32), (tAs, 2 * ntt), (tVh, 2 * ntt),
                       (aTg, 4 * ntt), (aVh, 2 * ntt), (aLd2[0], 32), (aLd2[1], 32),
                       (aHo, 2 * 128 * NEXT), (vTu, 4 * ntt), (vAs, 2 * ntt),
                       (vSs2[0], 16), (vSs2[1], 16), (vLd2[0], 32), (vLd2[1], 32)],
                      body, f"s1_{u}")
                nit = NOC // 2
                dL0.n += 32 * nit
                dL1.n += 32 * nit
                sT.n += 4 * ntt + 4 * ntt * nit
                sA.n += 2 * ntt + 2 * ntt * nit
                sV.n += 2 * ntt + 2 * ntt * nit
                dS0.n += 16 * nit
                dS1.n += 16 * nit
                nc.scalar.wait_ge(dS0.h, dS0.n)
                nc.scalar.wait_ge(dS1.h, dS1.n)
                nc.sync.wait_ge(dL0.h, dL0.n)
                nc.sync.wait_ge(dL1.h, dL1.n)
            nc.all_engine_barrier()

        def emit_sweep2(n_k, stat_sc, mov_sc, mov_stride, nt, res_src, res_tot, res_off,
                        dst, dst_tot, dst_off, scale_dram):
            ntt = nt // 512
            dK0, dS = fresh_dma_sems()
            dK1, dDS = fresh_dma_sems()
            dP, _unused = fresh_dma_sems()
            dKp = [dK0, dK1]
            u = uid()
            with contextlib.ExitStack() as ps:
                DK = [ps.enter_context(nc.sbuf_tensor(f"s2d{h}_{u}", [128, 1024], BF16)) for h in range(2)]
                HK = [ps.enter_context(nc.sbuf_tensor(f"s2h{h}_{u}", [128, 512], BF16)) for h in range(2)]
                XR = ps.enter_context(nc.sbuf_tensor(f"s2xr_{u}", [128, 8 * 512], F32))
                DSC = ps.enter_context(nc.sbuf_tensor(f"s2ds_{u}", [128, 8], F32))
                XO = ps.enter_context(nc.sbuf_tensor(f"s2xo_{u}", [128, 8 * 512], F32))
                PSb = [ps.enter_context(nc.psum_tensor(f"s2p{oc}_{u}", [128, 512], F32)) for oc in range(8)]
                bT0, bV0, bS0 = sT.n, sV.n, dS.n
                nc.tensor.sem_inc(sT.h, 2)
                nc.vector.sem_inc(sV.h, 8)
                nc.scalar.sem_inc(dS.h, 16)
                tLd2 = [Rg(ps, nc.tensor, f"tLd2{h}_{u}", unroll) for h in range(2)]
                yTb = Rg(ps, nc.sync, f"yTb{u}", unroll)
                ySv = Rg(ps, nc.sync, f"ySv{u}", unroll)
                yDS = Rg(ps, nc.sync, f"yDS{u}", unroll)
                yVx = Rg(ps, nc.gpsimd, f"yVx{u}", unroll)
                yDK = Rg(ps, nc.sync, f"yDK{u}", unroll)
                yHK = Rg(ps, nc.sync, f"yHK{u}", unroll)
                yXR = Rg(ps, nc.gpsimd, f"yXR{u}", unroll)
                tVp = Rg(ps, nc.tensor, f"tVp{u}", unroll)
                vTm = Rg(ps, nc.vector, f"vTm{u}", unroll)
                vLd = Rg(ps, nc.vector, f"vLd{u}", unroll)
                vDs = Rg(ps, nc.vector, f"vDs{u}", unroll)
                vSx = Rg(ps, nc.vector, f"vSx{u}", unroll)
                aVe = Rg(ps, nc.scalar, f"aVe{u}", unroll)
                aXO = Rg(ps, nc.scalar, f"aXO{u}", unroll)

                def kstep(par, it, kk, k_first, k_last, og_i, tt_i):
                    nc.sync.wait_ge(sT.h, yTb.to(bT0 + 2 + n_k * it + kk - 1))
                    with free_temps(nc, mybir.EngineType.SP):
                        nc.sync.dma_start(
                            out=DK[par][:, :],
                            in_=bass.AP(stat_sc.tensor, yDK.to(kk * 128 * D + og_i * 1024),
                                        [[D, 128], [1, 1024]])).then_inc(dKp[par].h, 16)
                    with free_temps(nc, mybir.EngineType.SP):
                        nc.sync.dma_start(
                            out=HK[par][:, :],
                            in_=bass.AP(mov_sc.tensor, yHK.to(kk * 128 * mov_stride + tt_i * 512),
                                        [[mov_stride, 128], [1, 512]])).then_inc(dKp[par].h, 16)
                    nc.tensor.wait_ge(dKp[par].h, tLd2[par].to(16 * n_k * it + 32 * (kk // 2) + 32))
                    if k_first:
                        nc.tensor.wait_ge(sV.h, tVp.to(bV0 + 8 + 8 * it))
                    for oc in range(8):
                        i = nc.tensor.matmul(PSb[oc][:, :], DK[par][:, oc * 128:(oc + 1) * 128],
                                             HK[par][:, :], start=k_first, stop=k_last,
                                             skip_group_check=True)
                    i.then_inc(sT.h, 1)

                def body_tt(og_i, tt_i):
                    it = og_i * ntt + tt_i
                    nc.sync.wait_ge(sV.h, ySv.to(bV0 + 8 + 8 * it))
                    with free_temps(nc, mybir.EngineType.SP):
                        nc.sync.dma_start(out=DSC[:, :],
                                          in_=bass.AP(scale_dram.tensor, yDS.to(og_i * 8),
                                                      [[NKC, 128], [1, 8]])).then_inc(dDS.h, 16)
                    nc.gpsimd.wait_ge(sV.h, yVx.to(bV0 + 8 + 8 * it))
                    with free_temps(nc, mybir.EngineType.Pool):
                        nc.gpsimd.dma_start(
                            out=XR[:, :].rearrange("p (g n) -> p g n", n=512),
                            in_=bass.AP(res_src.tensor,
                                        yXR.to(og_i * 1024 * res_tot + res_off + tt_i * 512),
                                        [[res_tot, 128], [128 * res_tot, 8], [1, 512]]),
                        ).then_inc(dP.h, 16)
                    kstep(0, it, 0, True, False, og_i, tt_i)
                    kstep(1, it, 1, False, False, og_i, tt_i)

                    def kbody(m):
                        kstep(0, it, 2 * m + 2, False, False, og_i, tt_i)
                        kstep(1, it, 2 * m + 3, False, False, og_i, tt_i)

                    drive(nc, (n_k - 4) // 2, unroll,
                          [(yTb, 2), (yDK, 2 * 128 * D), (yHK, 2 * 128 * mov_stride),
                           (tLd2[0], 32), (tLd2[1], 32)],
                          kbody, f"s2k{u}_{it if unroll else 0}")
                    kstep(0, it, n_k - 2, False, False, og_i, tt_i)
                    kstep(1, it, n_k - 1, False, True, og_i, tt_i)
                    nc.vector.wait_ge(sT.h, vTm.to(bT0 + 2 + n_k * it + n_k))
                    nc.vector.wait_ge(dP.h, vLd.to(16 * it + 16))
                    nc.vector.wait_ge(dDS.h, vDs.to(16 * it + 16))
                    nc.vector.wait_ge(dS.h, vSx.to(bS0 + 16 + 16 * it))
                    for oc in range(8):
                        nc.vector.scalar_tensor_tensor(
                            XO[:, oc * 512:(oc + 1) * 512], PSb[oc][:, :], DSC[:, oc:oc + 1],
                            XR[:, oc * 512:(oc + 1) * 512],
                            ALU.mult, ALU.add).then_inc(sV.h, 1)
                    nc.scalar.wait_ge(sV.h, aVe.to(bV0 + 8 + 8 * it + 8))
                    with free_temps(nc, mybir.EngineType.Activation):
                        nc.scalar.dma_start(
                            out=bass.AP(dst.tensor,
                                        aXO.to(og_i * 1024 * dst_tot + dst_off + tt_i * 512),
                                        [[dst_tot, 128], [128 * dst_tot, 8], [1, 512]]),
                            in_=XO[:, :].rearrange("p (g n) -> p g n", n=512),
                        ).then_inc(dS.h, 16)

                def body_og(og_i):
                    drive(nc, ntt, unroll,
                          [(yTb, n_k), (ySv, 8), (yDS, 0), (yVx, 8), (yDK, 0), (yHK, 512),
                           (yXR, 512), (tLd2[0], 16 * n_k), (tLd2[1], 16 * n_k), (tVp, 8),
                           (vTm, n_k), (vLd, 16), (vDs, 16), (vSx, 16), (aVe, 8), (aXO, 512)],
                          lambda tt_i: body_tt(og_i, tt_i), f"s2t{u}_{og_i if unroll else 0}")

                drive(nc, 4, unroll,
                      [(yTb, n_k * ntt), (ySv, 8 * ntt), (yDS, 8), (yVx, 8 * ntt),
                       (yDK, 1024), (yHK, 0), (yXR, 1024 * res_tot),
                       (tLd2[0], 16 * n_k * ntt), (tLd2[1], 16 * n_k * ntt), (tVp, 8 * ntt),
                       (vTm, n_k * ntt), (vLd, 16 * ntt), (vDs, 16 * ntt),
                       (vSx, 16 * ntt), (aVe, 8 * ntt), (aXO, 1024 * dst_tot)],
                      body_og, f"s2o{u}")
                nit = 4 * ntt
                dK0.n += 16 * n_k * nit
                dK1.n += 16 * n_k * nit
                dDS.n += 16 * nit
                dP.n += 16 * nit
                sT.n += 2 + n_k * nit
                sV.n += 8 + 8 * nit
                dS.n += 16 + 16 * nit
                nc.scalar.wait_ge(dS.h, dS.n)
                nc.sync.wait_ge(dK0.h, dK0.n)
                nc.sync.wait_ge(dK1.h, dK1.n)
                nc.sync.wait_ge(dDS.h, dDS.n)
                nc.gpsimd.wait_ge(dP.h, dP.n)
            nc.all_engine_barrier()

        def emit_attn():
            u = uid()
            with contextlib.ExitStack() as ats:
                KRT = ats.enter_context(nc.sbuf_tensor(f"aKRT_{u}", [128, NEXT], BF16))
                VTM = ats.enter_context(nc.sbuf_tensor(f"aVTM_{u}", [128, 12 * 128], BF16))

                dL, dS = fresh_dma_sems()
                with contextlib.ExitStack() as ps:
                    WF = ps.enter_context(nc.sbuf_tensor(f"aWF_{u}", [128, D], F32))
                    WB = ps.enter_context(nc.sbuf_tensor(f"aWB_{u}", [128, D], BF16))
                    WTs = ps.enter_context(nc.sbuf_tensor(f"aWT_{u}", [128, D], BF16))
                    KF = ps.enter_context(nc.sbuf_tensor(f"aKF_{u}", [128, NEXT], F32))
                    VB = ps.enter_context(nc.sbuf_tensor(f"aVB_{u}", [128, NEXT], BF16))
                    R1 = ps.enter_context(nc.sbuf_tensor(f"aR1_{u}", [128, NEXT], F32))
                    R2 = ps.enter_context(nc.sbuf_tensor(f"aR2_{u}", [128, NEXT], F32))
                    PT = [ps.enter_context(nc.psum_tensor(f"aPT{h}_{u}", [128, 2048], BF16)) for h in range(2)]
                    PKV = ps.enter_context(nc.psum_tensor(f"aPKV_{u}", [128, 512], F32))

                    def kv_one(wmat, out_is_k):
                        bL, bT, bA, bP = dL.n, sT.n, sA.n, sP.n
                        with nc.Block() as bb:
                            @bb.sync
                            def _(s):
                                s.dma_start(out=WF[:, :], in_=wmat[:, :]).then_inc(dL.h, 16)

                            @bb.gpsimd
                            def _(g):
                                g.wait_ge(dL.h, bL + 16)
                                g.tensor_copy(WB[:, :], WF[:, :]).then_inc(sP.h, 1)

                            @bb.tensor
                            def _(t):
                                t.wait_ge(sP.h, bP + 1)
                                for k in range(NKC):
                                    i = t.transpose(PT[k // 16][:, (k % 16) * 128:(k % 16 + 1) * 128],
                                                    WB[:, k * 128:(k + 1) * 128], ident[:, :])
                                    if k in (15, 31):
                                        i.then_inc(sT.h, 1)
                                t.wait_ge(sA.h, bA + 2)
                                for t3 in range(3):
                                    if t3 > 0:
                                        t.wait_ge(sA.h, bA + 2 + t3)
                                    for k in range(NKC):
                                        i = t.matmul(PKV[:, :], WTs[:, k * 128:(k + 1) * 128],
                                                     XN[:, k * NEXT + t3 * 512:k * NEXT + t3 * 512 + 512],
                                                     start=(k == 0), stop=(k == NKC - 1),
                                                     skip_group_check=True)
                                    i.then_inc(sT.h, 1)

                            @bb.scalar
                            def _(sc):
                                sc.wait_ge(sT.h, bT + 1)
                                sc.copy(WTs[:, 0:2048], PT[0][:, :]).then_inc(sA.h, 1)
                                sc.wait_ge(sT.h, bT + 2)
                                sc.copy(WTs[:, 2048:4096], PT[1][:, :]).then_inc(sA.h, 1)
                                for t3 in range(3):
                                    sc.wait_ge(sT.h, bT + 2 + t3 + 1)
                                    if out_is_k:
                                        sc.copy(KF[:, t3 * 512:(t3 + 1) * 512], PKV[:, :]).then_inc(sA.h, 1)
                                    else:
                                        sc.copy(VB[:, t3 * 512:(t3 + 1) * 512], PKV[:, :]).then_inc(sA.h, 1)
                                sc.wait_ge(sA.h, bA + 5)
                        dL.n += 16
                        sP.n += 1
                        sT.n += 5
                        sA.n += 5
                        nc.all_engine_barrier()

                    kv_one(wk, True)
                    kv_one(wv, False)

                    bT, bA, bV = sT.n, sA.n, sV.n
                    with nc.Block() as b2:
                        @b2.vector
                        def _(v):
                            v.tensor_mul(R1[:, :], KF[:, :], cosE[:, :]).then_inc(sV.h, 1)
                            v.tensor_mul(R2[0:64, :], KF[64:128, :], sinE[64:128, :]).then_inc(sV.h, 1)
                            v.tensor_mul(R2[64:128, :], KF[0:64, :], sinE[0:64, :]).then_inc(sV.h, 1)
                            v.wait_ge(sV.h, bV + 3)
                            v.tensor_add(KRT[:, :], R1[:, :], R2[:, :]).then_inc(sV.h, 1)
                            v.wait_ge(sV.h, bV + 4)

                        @b2.tensor
                        def _(t):
                            for e in range(12):
                                i = t.transpose(PT[0][:, e * 128:(e + 1) * 128],
                                                VB[:, e * 128:(e + 1) * 128], ident[:, :])
                            i.then_inc(sT.h, 1)

                        @b2.scalar
                        def _(sc):
                            sc.wait_ge(sT.h, bT + 1)
                            sc.copy(VTM[:, 0:12 * 128], PT[0][:, 0:12 * 128]).then_inc(sA.h, 1)
                            sc.wait_ge(sA.h, bA + 1)
                    sV.n += 4
                    sT.n += 1
                    sA.n += 1
                    nc.sync.wait_ge(dL.h, dL.n)
                nc.all_engine_barrier()

                dL, dS = fresh_dma_sems()
                with contextlib.ExitStack() as ps:
                    WQB = ps.enter_context(nc.sbuf_tensor(f"hWQ_{u}", [128, D], BF16))
                    QF = ps.enter_context(nc.sbuf_tensor(f"hQF_{u}", [128, NOWN], F32))
                    QR = ps.enter_context(nc.sbuf_tensor(f"hQR_{u}", [128, NOWN], BF16))
                    R1 = ps.enter_context(nc.sbuf_tensor(f"hR1_{u}", [128, NOWN], F32))
                    R2 = ps.enter_context(nc.sbuf_tensor(f"hR2_{u}", [128, NOWN], F32))
                    SGT = [ps.enter_context(nc.sbuf_tensor(f"hSG{w}_{u}", [128, 2048], BF16)) for w in range(2)]
                    RC = ps.enter_context(nc.sbuf_tensor(f"hRC_{u}", [128, 512], F32))
                    RW = ps.enter_context(nc.sbuf_tensor(f"hRW_{u}", [128, 512], F32))
                    CBb = [ps.enter_context(nc.sbuf_tensor(f"hCB{w}_{u}", [128, 512], F32)) for w in range(2)]
                    ATH = ps.enter_context(nc.sbuf_tensor(f"hATH_{u}", [128, NOWN], BF16))
                    PQ2 = [ps.enter_context(nc.psum_tensor(f"hPQ{t2}_{u}", [128, 512], F32)) for t2 in range(2)]
                    PST = [ps.enter_context(nc.psum_tensor(f"hPS{w}_{u}", [128, 512], F32)) for w in range(2)]
                    PR = ps.enter_context(nc.psum_tensor(f"hPR_{u}", [128, 512], F32))
                    PW = ps.enter_context(nc.psum_tensor(f"hPW_{u}", [128, 512], F32))
                    bL0, bT0, bA0, bV0, bS0 = dL.n, sT.n, sA.n, sV.n, dS.n
                    rT, rA, rV = 32, 22, 23
                    nc.tensor.sem_inc(sT.h, rT)
                    nc.scalar.sem_inc(sA.h, rA)
                    nc.vector.sem_inc(sV.h, rV)
                    nc.scalar.sem_inc(dS.h, 16)
                    REC = [5, 8, 12, 16, 20]
                    RCW = [6, 9, 13, 17, 21]
                    CTR = [7, 10, 14, 18, 22]
                    yTq = Rg(ps, nc.sync, f"yTq{u}", unroll)
                    yWq = Rg(ps, nc.sync, f"yWq{u}", unroll)
                    tLd = Rg(ps, nc.tensor, f"tLdh{u}", unroll)
                    tAx = Rg(ps, nc.tensor, f"tAx{u}", unroll)
                    tVx = Rg(ps, nc.tensor, f"tVx{u}", unroll)
                    aTx = Rg(ps, nc.scalar, f"aTx{u}", unroll)
                    aVx = Rg(ps, nc.scalar, f"aVx{u}", unroll)
                    aAo = Rg(ps, nc.scalar, f"aAoh{u}", unroll)
                    vAx = Rg(ps, nc.vector, f"vAx{u}", unroll)
                    vTx = Rg(ps, nc.vector, f"vTxh{u}", unroll)
                    vVx = Rg(ps, nc.vector, f"vVx{u}", unroll)
                    vSs = Rg(ps, nc.vector, f"vSsh{u}", unroll)

                    def body(hh):
                        ctT = ctA = ctV = 0
                        nc.sync.wait_ge(sT.h, yTq.to(bT0 + rT * hh + 2))
                        with free_temps(nc, mybir.EngineType.SP):
                            nc.sync.dma_start(out=WQB[:, :],
                                              in_=bass.AP(qT_sc.tensor, yWq.to(hh * 128 * D),
                                                          [[D, 128], [1, D]])).then_inc(dL.h, 16)
                        nc.tensor.wait_ge(dL.h, tLd.to(bL0 + 16 * hh + 16))
                        for t2 in range(2):
                            nc.tensor.wait_ge(sA.h, tAx.to(bA0 + rA * hh + t2 + 1))
                            for k in range(NKC):
                                i = nc.tensor.matmul(
                                    PQ2[t2][:, :], WQB[:, k * 128:(k + 1) * 128],
                                    XN[:, k * NEXT + HALO + t2 * 512:k * NEXT + HALO + t2 * 512 + 512],
                                    start=(k == 0), stop=(k == NKC - 1), skip_group_check=True)
                            i.then_inc(sT.h, 1); ctT += 1
                        for t2 in range(2):
                            nc.scalar.wait_ge(sT.h, aTx.to(bT0 + rT + rT * hh + t2 + 1))
                            nc.scalar.wait_ge(sV.h, aVx.to(bV0 + rV * hh + 4))
                            nc.scalar.copy(QF[:, t2 * 512:(t2 + 1) * 512], PQ2[t2][:, :]).then_inc(sA.h, 1)
                            ctA += 1
                        nc.vector.wait_ge(sA.h, vAx.to(bA0 + rA + rA * hh + 2))
                        nc.vector.wait_ge(sT.h, vTx.to(bT0 + rT + rT * hh))
                        nc.vector.tensor_mul(R1[:, :], QF[:, :],
                                             cosE[:, HALO:HALO + NOWN]).then_inc(sV.h, 1); ctV += 1
                        nc.vector.tensor_mul(R2[0:64, :], QF[64:128, :],
                                             sinE[64:128, HALO:HALO + NOWN]).then_inc(sV.h, 1); ctV += 1
                        nc.vector.tensor_mul(R2[64:128, :], QF[0:64, :],
                                             sinE[0:64, HALO:HALO + NOWN]).then_inc(sV.h, 1); ctV += 1
                        nc.vector.wait_ge(sV.h, vVx.to(bV0 + rV + rV * hh + ctV))
                        nc.vector.tensor_add(QR[:, :], R1[:, :], R2[:, :]).then_inc(sV.h, 1); ctV += 1
                        for wi in range(5):
                            qoff, qw = WIN_Q[wi]
                            kb = 2 * wi
                            woff = WIN_WOFF[wi]
                            for kc in range(4):
                                g = 4 * wi + kc
                                if wi == 0 and kc == 0:
                                    nc.tensor.wait_ge(sV.h, tVx.to(bV0 + rV + rV * hh + 4))
                                nc.tensor.wait_ge(sA.h, tAx.to(bA0 + rA + rA * hh + 2 + g - 1))
                                i = nc.tensor.matmul(PST[kc % 2][:, 0:qw],
                                                     KRT[:, (kb + kc) * 128:(kb + kc + 1) * 128],
                                                     QR[:, qoff:qoff + qw],
                                                     start=True, stop=True, skip_group_check=True)
                                i.then_inc(sT.h, 1); ctT += 1
                                nc.scalar.wait_ge(sT.h, aTx.to(bT0 + rT + rT * hh + 2 + 6 * wi + kc + 1))
                                nc.scalar.activation(SGT[wi % 2][:, kc * 512:kc * 512 + qw],
                                                     PST[kc % 2][:, 0:qw], AF.Sigmoid,
                                                     scale=SCALE).then_inc(sA.h, 1); ctA += 1
                            for kc in range(4):
                                g = 4 * wi + kc
                                nc.tensor.wait_ge(sA.h, tAx.to(bA0 + rA + rA * hh + 2 + g + 1))
                                if kc == 0:
                                    if wi == 0:
                                        nc.tensor.wait_ge(sV.h, tVx.to(bV0 + rV + rV * hh))
                                    else:
                                        nc.tensor.wait_ge(sV.h, tVx.to(bV0 + rV + rV * hh + REC[wi - 1]))
                                i = nc.tensor.matmul(PR[:, 0:qw],
                                                     onesm[:, (kb + kc) * 128:(kb + kc + 1) * 128],
                                                     SGT[wi % 2][:, kc * 512:kc * 512 + qw],
                                                     start=(kc == 0), stop=(kc == 3), skip_group_check=True)
                            i.then_inc(sT.h, 1); ctT += 1
                            for kc in range(4):
                                if kc == 0:
                                    if wi == 0:
                                        nc.tensor.wait_ge(sV.h, tVx.to(bV0 + rV + rV * hh))
                                    else:
                                        nc.tensor.wait_ge(sV.h, tVx.to(bV0 + rV + rV * hh + CTR[wi - 1]))
                                i = nc.tensor.matmul(PW[:, 0:qw],
                                                     VTM[:, (kb + kc) * 128:(kb + kc + 1) * 128],
                                                     SGT[wi % 2][:, kc * 512:kc * 512 + qw],
                                                     start=(kc == 0), stop=(kc == 3), skip_group_check=True)
                            i.then_inc(sT.h, 1); ctT += 1
                            nc.vector.wait_ge(sT.h, vTx.to(bT0 + rT + rT * hh + 2 + 6 * wi + 5))
                            if wi > 0:
                                nc.vector.wait_ge(sV.h, vVx.to(bV0 + rV + rV * hh + RCW[wi - 1]))
                            nc.vector.reciprocal(RC[:, 0:qw], PR[:, 0:qw]).then_inc(sV.h, 1); ctV += 1
                            nc.vector.wait_ge(sV.h, vVx.to(bV0 + rV + rV * hh + ctV))
                            nc.vector.tensor_mul(RW[:, 0:qw], RC[:, 0:qw],
                                                 wgt[:, woff:woff + qw]).then_inc(sV.h, 1); ctV += 1
                            nc.vector.wait_ge(sT.h, vTx.to(bT0 + rT + rT * hh + 2 + 6 * wi + 6))
                            nc.vector.wait_ge(sV.h, vVx.to(bV0 + rV + rV * hh + ctV))
                            nc.vector.tensor_mul(CBb[wi % 2][:, 0:qw], PW[:, 0:qw],
                                                 RW[:, 0:qw]).then_inc(sV.h, 1); ctV += 1
                            if wi > 0:
                                aoff = 0 if wi - 1 == 0 else 256
                                if wi == 1:
                                    nc.vector.wait_ge(dS.h, vSs.to(bS0 + 16 + 16 * hh))
                                nc.vector.wait_ge(sV.h, vVx.to(bV0 + rV + rV * hh + ctV))
                                nc.vector.tensor_add(ATH[:, (wi - 1) * 256:wi * 256],
                                                     CBb[(wi - 1) % 2][:, aoff:aoff + 256],
                                                     CBb[wi % 2][:, 0:256]).then_inc(sV.h, 1); ctV += 1
                        assert ctT == rT and ctA == rA and ctV == rV, (ctT, ctA, ctV)
                        nc.scalar.wait_ge(sV.h, aVx.to(bV0 + rV + rV * hh + rV))
                        with free_temps(nc, mybir.EngineType.Activation):
                            nc.scalar.dma_start(out=bass.AP(attnsc.tensor, aAo.to(hh * 128 * NOWN),
                                                            [[NOWN, 128], [1, NOWN]]),
                                                in_=ATH[:, :]).then_inc(dS.h, 16)

                    drive(nc, NH, unroll,
                          [(yTq, rT), (yWq, 128 * D), (tLd, 16), (tAx, rA), (tVx, rV),
                           (aTx, rT), (aVx, rV), (aAo, 128 * NOWN), (vAx, rA), (vTx, rT),
                           (vVx, rV), (vSs, 16)],
                          body, f"hd{u}")
                    dL.n += 16 * NH
                    sT.n += rT + rT * NH
                    sA.n += rA + rA * NH
                    sV.n += rV + rV * NH
                    dS.n += 16 + 16 * NH
                    nc.scalar.wait_ge(dS.h, dS.n)
                    nc.sync.wait_ge(dL.h, dL.n)
                nc.all_engine_barrier()

        emit_ln(xe, NEXT, NEXT)
        if debug == 2:
            xnsc = nc.dram_tensor("xnsc", [128, NKC * NEXT], BF16, kind="ExternalOutput").ap()
            dbgL, _ = fresh_dma_sems()
            with nc.Block() as bdbg:
                @bdbg.sync
                def _(s):
                    s.dma_start(out=xnsc[:, :], in_=XN[:, :]).then_inc(dbgL.h, 16)
                    s.wait_ge(dbgL.h, 16)
            dbgL.n += 16
            nc.all_engine_barrier()
        emit_sweep1(NEXT, NEXT)
        emit_sweep2(NOC, dT_sc, hsc, NEXT, NEXT, xe, NEXT, 0, x1sc, NEXT, 0, dsh_dram)
        emit_ln(x1sc, NEXT, NEXT)
        emit_attn()
        emit_sweep2(NKC, oT_sc, attnsc, NOWN, NOWN, x1sc, NEXT, HALO, x2sc, NOWN, 0, oneI)
        emit_ln(x2sc, NOWN, NOWN)
        emit_sweep1(NOWN, NOWN)
        emit_sweep2(NOC, dT_sc, hsc, NEXT, NOWN, x2sc, NOWN, 0, yT, NOWN, 0, dsh_dram)
    return nc



def _rope_tables():
    base = 10000.0
    Dh = 128
    inv_freq = 1.0 / base ** (np.arange(0, Dh, 2, dtype=np.float32) / Dh)
    t = np.arange(S, dtype=np.float32)
    f = np.outer(t, inv_freq)
    cos = np.zeros((S, Dh), dtype=np.float32)
    sin = np.zeros((S, Dh), dtype=np.float32)
    cos[:, 0::2] = np.cos(f)
    cos[:, 1::2] = np.cos(f)
    sin[:, 0::2] = np.sin(f)
    sin[:, 1::2] = np.sin(f)
    return cos, sin


def _host_prep(inputs):
    x = np.asarray(inputs["L0_attn_x"])
    Wq = np.ascontiguousarray(np.asarray(inputs["L0_attn_Wq"], dtype=np.float32))
    Wk = np.ascontiguousarray(np.asarray(inputs["L0_attn_Wk"], dtype=np.float32))
    Wv = np.ascontiguousarray(np.asarray(inputs["L0_attn_Wv"], dtype=np.float32))
    Wo = np.ascontiguousarray(np.asarray(inputs["L0_attn_Wo"], dtype=np.float32))
    gw32 = np.asarray(inputs["L0_ffn_gate_wq"])
    uw32 = np.asarray(inputs["L0_ffn_up_wq"])
    dw32 = np.asarray(inputs["L0_ffn_down_wq"])
    gs = np.asarray(inputs["L0_ffn_gate_scale"], dtype=np.float32)
    us = np.asarray(inputs["L0_ffn_up_scale"], dtype=np.float32)
    dsc = np.asarray(inputs["L0_ffn_down_scale"], dtype=np.float32)

    def lo16(a):
        return np.ascontiguousarray(a.view(np.int16)[:, 0::2])

    gw16, uw16, dw16 = lo16(gw32), lo16(uw32), lo16(dw32)
    xT = np.ascontiguousarray(x[0].T.astype(np.float32))

    gsr = gs.reshape(NOC, 128).T
    usr = us.reshape(NOC, 128).T
    gusr = np.empty((128, 2 * NOC), dtype=np.float32)
    gusr[:, 0::2] = gsr
    gusr[:, 1::2] = usr
    gusr = np.ascontiguousarray(gusr)
    dsr = np.ascontiguousarray(dsc.reshape(NKC, 128).T)
    onev = np.ones((128, NKC), dtype=np.float32)

    cos, sin = _rope_tables()
    alpha = np.linspace(0.0, 1.0, 256, dtype=np.float32)
    idn = np.eye(128, dtype=ml_dtypes.bfloat16)

    in_maps = []
    for c in range(8):
        g0 = c * NOWN - HALO
        xev = np.zeros((D, NEXT), dtype=np.float32)
        lo, hi = max(0, g0), min(S, g0 + NEXT)
        xev[:, lo - g0:hi - g0] = xT[:, lo:hi]
        pos = np.clip(np.arange(g0, g0 + NEXT), 0, S - 1)
        cosEv = np.ascontiguousarray(cos[pos].T)
        sinv = sin[pos].T
        sintbl = np.empty_like(sinv)
        sintbl[64:128] = -sinv[0:64]
        sintbl[0:64] = sinv[64:128]
        sintbl = np.ascontiguousarray(sintbl)
        wgtbl = np.zeros((2048,), dtype=np.float32)
        for wi in range(5):
            off = WIN_WOFF[wi]
            if wi == 0:
                wgtbl[off:off + 256] = 0.0 if c == 0 else 1.0 - alpha
            elif wi == 4:
                wgtbl[off:off + 256] = alpha
            else:
                wgtbl[off:off + 256] = 1.0 if (c == 0 and wi == 1) else alpha
                wgtbl[off + 256:off + 512] = 1.0 - alpha
        wgtblB = np.ascontiguousarray(np.broadcast_to(wgtbl, (128, 2048)))
        onesmv = np.ones((12, 128, 128), dtype=np.float32)
        for jch in range(12):
            kk = np.arange(g0 + jch * 128, g0 + jch * 128 + 128)
            onesmv[jch, (kk < 0) | (kk >= S), :] = 0.0
        onesmv = np.ascontiguousarray(
            onesmv.transpose(1, 0, 2).reshape(128, 12 * 128)).astype(ml_dtypes.bfloat16)
        in_maps.append({
            "xe": xev, "gw": gw16, "uw": uw16, "dw": dw16,
            "wq": Wq, "wk": Wk, "wv": Wv, "wo": Wo,
            "gus": gusr, "ds": dsr, "onev": onev,
            "cosE": cosEv, "sinE": sintbl, "wgt": wgtblB,
            "onesm": onesmv, "ident": idn,
        })
    return in_maps


def kernel(**inputs):
    in_maps = _host_prep(inputs)
    nc = build_nc(debug=False, unroll=False)
    res = run_bass_kernel_spmd(nc, in_maps, core_ids=list(range(8)))
    out = np.empty((1, S, D), dtype=np.float32)
    for c in range(8):
        out[0, c * NOWN:(c + 1) * NOWN, :] = res.results[c]["yT"].T
    return out


# version 05
# speedup vs baseline: 1.0872x; 1.0872x over previous
import contextlib
import numpy as np
import ml_dtypes
import concourse.bass as bass
import concourse.mybir as mybir
from concourse.bass_utils import run_bass_kernel_spmd

F32 = mybir.dt.float32
BF16 = mybir.dt.bfloat16
I16 = mybir.dt.int16
AF = mybir.ActivationFunctionType
ALU = mybir.AluOpType

D = 4096
DI = 14336
NKC = D // 128
NOC = DI // 128
NH = 32
S = 8192
NOWN = 1024
HALO = 256
NEXT = NOWN + 2 * HALO
SCALE = 1.0 / float(np.sqrt(128.0))
EPS = 1e-5

WIN_Q = [(0, 256), (0, 512), (256, 512), (512, 512), (768, 256)]
WIN_WOFF = [0, 256, 768, 1280, 1792]

_ENG_ALL = list(mybir.ALL_ENGINES)


class SemC:
    def __init__(self, stk, nc, name):
        self.h = stk.enter_context(nc.semaphore(name))
        self.n = 0


class Rg:
    def __init__(self, ps, eng, name, unroll):
        self.unroll = unroll
        self.eng = eng
        if not unroll:
            self.r = ps.enter_context(eng.register(name))
            eng.reg_mov(self.r, 0)
            self.cur0 = 0

    def to(self, v):
        if self.unroll:
            return int(v)
        d = int(v) - self.cur0
        if d > 0:
            self.eng.reg_add(self.r, self.r, d)
        elif d < 0:
            self.eng.reg_sub(self.r, self.r, -d)
        self.cur0 = int(v)
        return self.r


@contextlib.contextmanager
def raw_fori(nc, n, tag):
    regs = nc.alloc_registers(f"fl{tag}_i", engines=_ENG_ALL)
    nc.regs_mov(regs, 0)
    lbl, end = f"fl{tag}_L", f"fl{tag}_E"
    nc.br(lbl, engines=_ENG_ALL)
    with nc.body(lbl, valid_engines=_ENG_ALL):
        yield
        nc.regs_alu(regs, regs, 1, op=ALU.add)
        nc.br_lt(regs, n, on_true=lbl, on_false=end, engines=_ENG_ALL)
    nc.switch_bb(end)
    for hdl in regs.handles:
        nc.free_register(hdl)


def _free_lowering_temps(nc, eng_enum, n0):
    from bass_rust import RegisterHandle
    for nm in nc.inst_names()[n0:]:
        inst = nc.find_inst_by_name(nm)
        if inst is None:
            continue
        if type(inst).__name__ in ("InstRegisterMove", "InstRegisterAlu"):
            for a in list(inst.outs):
                r = getattr(a, "regref", None)
                if isinstance(r, str) and ("_snap_" in r or "_tmp_" in r):
                    try:
                        nc.free_register(RegisterHandle(r, eng_enum))
                    except Exception:
                        pass


@contextlib.contextmanager
def free_temps(nc, eng_enum):
    n0 = nc.n_instructions()
    yield
    _free_lowering_temps(nc, eng_enum, n0)


def drive(nc, n, unroll, bumps, body, tag):
    if unroll:
        for j in range(n):
            body(j)
    else:
        entry = [rg.cur0 for rg, _ in bumps]
        with raw_fori(nc, n, tag):
            body(0)
            for (rg, rate), e in zip(bumps, entry):
                rg.to(e + rate)
        for (rg, rate), e in zip(bumps, entry):
            rg.cur0 = e + rate * n


def build_nc(debug=False, unroll=False):
    nc = bass.Bass("TRN2", target_bir_lowering=False,
                   detect_race_conditions=bool(unroll))

    xe = nc.dram_tensor("xe", [D, NEXT], F32, kind="ExternalInput").ap()
    gw = nc.dram_tensor("gw", [DI, D], I16, kind="ExternalInput").ap()
    uw = nc.dram_tensor("uw", [DI, D], I16, kind="ExternalInput").ap()
    dw = nc.dram_tensor("dw", [D, DI], I16, kind="ExternalInput").ap()
    wq = nc.dram_tensor("wq", [D, D], F32, kind="ExternalInput").ap()
    wk = nc.dram_tensor("wk", [128, D], F32, kind="ExternalInput").ap()
    wv = nc.dram_tensor("wv", [128, D], F32, kind="ExternalInput").ap()
    wo = nc.dram_tensor("wo", [D, D], F32, kind="ExternalInput").ap()
    gusI = nc.dram_tensor("gus", [128, 2 * NOC], F32, kind="ExternalInput").ap()
    dsI = nc.dram_tensor("ds", [128, NKC], F32, kind="ExternalInput").ap()
    oneI = nc.dram_tensor("onev", [128, NKC], F32, kind="ExternalInput").ap()
    cosI = nc.dram_tensor("cosE", [128, NEXT], F32, kind="ExternalInput").ap()
    sinI = nc.dram_tensor("sinE", [128, NEXT], F32, kind="ExternalInput").ap()
    wgtI = nc.dram_tensor("wgt", [128, 2048], F32, kind="ExternalInput").ap()
    onesmI = nc.dram_tensor("onesm", [128, 12 * 128], BF16, kind="ExternalInput").ap()
    identI = nc.dram_tensor("ident", [128, 128], BF16, kind="ExternalInput").ap()

    yT = nc.dram_tensor("yT", [D, NOWN], F32, kind="ExternalOutput").ap()

    SCK = "ExternalOutput" if debug else "Internal"
    SCK2 = "ExternalOutput" if debug == 2 else "Internal"
    x1sc = nc.dram_tensor("x1sc", [D, NEXT], F32, kind=SCK).ap()
    x2sc = nc.dram_tensor("x2sc", [D, NOWN], F32, kind=SCK).ap()
    guT_sc = nc.dram_tensor("guT_sc", [2 * NOC * 128, D], BF16, kind=SCK2).ap()
    qT_sc = nc.dram_tensor("qT_sc", [NH * 128, D], BF16, kind=SCK2).ap()
    dT_sc = nc.dram_tensor("dT_sc", [DI, D], BF16, kind=SCK2).ap()
    oT_sc = nc.dram_tensor("oT_sc", [D, D], BF16, kind=SCK2).ap()
    hsc = nc.dram_tensor("hsc", [NOC * 128, NEXT], BF16, kind=SCK2).ap()
    attnsc = nc.dram_tensor("attnsc", [NH * 128, NOWN], BF16, kind=SCK2).ap()

    stk = contextlib.ExitStack()
    with stk:
        sT = SemC(stk, nc, "sT")
        sA = SemC(stk, nc, "sA")
        sV = SemC(stk, nc, "sV")
        sP = SemC(stk, nc, "sP")
        _ctr = [0]

        def fresh_dma_sems():
            _ctr[0] += 1
            return SemC(stk, nc, f"dL{_ctr[0]}"), SemC(stk, nc, f"dS{_ctr[0]}")

        def uid():
            _ctr[0] += 1
            return _ctr[0]

        cosE = stk.enter_context(nc.sbuf_tensor("cosE_s", [128, NEXT], F32))
        sinE = stk.enter_context(nc.sbuf_tensor("sinE_s", [128, NEXT], F32))
        wgt = stk.enter_context(nc.sbuf_tensor("wgt_s", [128, 2048], F32))
        onesm = stk.enter_context(nc.sbuf_tensor("onesm_s", [128, 12 * 128], BF16))
        ident = stk.enter_context(nc.sbuf_tensor("ident_s", [128, 128], BF16))
        dshS = stk.enter_context(nc.sbuf_tensor("dsh_s", [128, NKC], F32))
        onebf = stk.enter_context(nc.sbuf_tensor("onebf_s", [128, 128], BF16))
        VDUM = stk.enter_context(nc.sbuf_tensor("vdum_s", [128, 32], F32))

        def vfence(region_ap):
            return nc.vector.tensor_copy(VDUM[:, :], region_ap)
        dsh_dram = nc.dram_tensor("dsh_dram", [128, NKC], F32, kind="Internal").ap()

        cL0, cS0 = fresh_dma_sems()
        with nc.Block() as b0:
            @b0.sync
            def _(s):
                for dst_, src_ in ((cosE, cosI), (sinE, sinI), (wgt, wgtI), (onesm, onesmI),
                                   (ident, identI), (dshS, dsI)):
                    s.dma_start(out=dst_[:, :], in_=src_[:, :]).then_inc(cL0.h, 16)
                s.wait_ge(cL0.h, 96)

            @b0.vector
            def _(v):
                v.wait_ge(cL0.h, 96)
                v.tensor_scalar_mul(dshS[:, :], dshS[:, :], 0.5).then_inc(sV.h, 1)
                v.wait_ge(sV.h, sV.n + 1)

            @b0.scalar
            def _(sc):
                sc.wait_ge(sV.h, sV.n + 1)
                sc.dma_start(out=dsh_dram[:, :], in_=dshS[:, :]).then_inc(cS0.h, 16)
                sc.wait_ge(cS0.h, 16)

            @b0.gpsimd
            def _(g):
                g.memset(onebf[:, :], 1.0).then_inc(sP.h, 1)
        cL0.n += 96
        cS0.n += 16
        sV.n += 1
        sP.n += 1
        nc.all_engine_barrier()

        def prolog_slab(src, n_o, dst, src_f32, slab_base=0, slab_step=1):
            dL0, dS0 = fresh_dma_sems()
            dL1, dS1 = fresh_dma_sems()
            dLp, dSp = [dL0, dL1], [dS0, dS1]
            sdt = F32 if src_f32 else I16
            u = uid()
            W = D
            with contextlib.ExitStack() as ps:
                NI = [ps.enter_context(nc.sbuf_tensor(f"pni{h}_{u}", [128, W], sdt)) for h in range(2)]
                CB = [ps.enter_context(nc.sbuf_tensor(f"pcb{h}_{u}", [128, W], BF16)) for h in range(2)]
                WT = [ps.enter_context(nc.sbuf_tensor(f"pwt{h}_{u}", [128, W], BF16)) for h in range(2)]
                PT = [ps.enter_context(nc.psum_tensor(f"ppt{h}_{u}", [128, 2048], BF16)) for h in range(4)]
                bP0, bT0, bA0 = sP.n, sT.n, sA.n
                nc.gpsimd.sem_inc(sP.h, 2)
                nc.tensor.sem_inc(sT.h, 4)
                nc.scalar.sem_inc(sA.h, 6)
                nc.scalar.sem_inc(dS0.h, 16)
                nc.scalar.sem_inc(dS1.h, 16)
                dS0.n += 16
                dS1.n += 16
                gLd2 = [Rg(ps, nc.gpsimd, f"gLd2{h}_{u}", unroll) for h in range(2)]
                aLd2 = [Rg(ps, nc.scalar, f"aLd2{h}_{u}", unroll) for h in range(2)]
                aTc = Rg(ps, nc.scalar, f"aTc{u}", unroll)
                aSs2 = [Rg(ps, nc.scalar, f"aSs2{h}_{u}", unroll) for h in range(2)]
                ySp = Rg(ps, nc.sync, f"ySp{u}", unroll)
                ySo = Rg(ps, nc.sync, f"ySo{u}", unroll)
                gTt = Rg(ps, nc.gpsimd, f"gTt{u}", unroll)
                tPc = Rg(ps, nc.tensor, f"tPc{u}", unroll)
                tAe = Rg(ps, nc.tensor, f"tAe{u}", unroll)
                aTt = Rg(ps, nc.scalar, f"aTt{u}", unroll)
                aAo = Rg(ps, nc.scalar, f"aAo{u}", unroll)
                aOf = Rg(ps, nc.scalar, f"aOf{u}", unroll)
                ySp2 = Rg(ps, nc.sync, f"ySp2{u}", unroll)
                tAe0 = Rg(ps, nc.tensor, f"tAe0{u}", unroll)

                def body(j):
                    for h in range(2):
                        o = 2 * j + h
                        pt0, pt1 = PT[2 * h], PT[2 * h + 1]
                        nc.sync.wait_ge(sP.h, ySp.to(bP0 + 2 + o - 1))
                        nc.sync.wait_ge(sA.h, ySp2.to(bA0 + 6 + 3 * o - 6 + 1))
                        with free_temps(nc, mybir.EngineType.SP):
                            nc.sync.dma_start(out=NI[h][:, :],
                                              in_=bass.AP(src.tensor, ySo.to(o * 128 * W),
                                                          [[W, 128], [1, W]])).then_inc(dLp[h].h, 16)
                        W2 = W // 2
                        nc.gpsimd.wait_ge(dLp[h].h, gLd2[h].to(16 * j + 16))
                        nc.gpsimd.wait_ge(sT.h, gTt.to(bT0 + 4 + 2 * o - 2))
                        nc.gpsimd.tensor_copy(CB[h][:, 0:W2], NI[h][:, 0:W2]).then_inc(sP.h, 1)
                        nc.scalar.wait_ge(dLp[h].h, aLd2[h].to(16 * j + 16))
                        nc.scalar.wait_ge(sT.h, aTc.to(bT0 + 4 + 2 * o - 2))
                        nc.scalar.copy(CB[h][:, W2:W], NI[h][:, W2:W]).then_inc(sA.h, 1)
                        nc.tensor.wait_ge(sP.h, tPc.to(bP0 + 2 + o + 1))
                        nc.tensor.wait_ge(sA.h, tAe0.to(bA0 + 6 + 3 * o + 1))
                        nc.tensor.wait_ge(sA.h, tAe.to(bA0 + 6 + 3 * o - 6 + 3))
                        for k in range(NKC):
                            pt = pt0 if k < 16 else pt1
                            i = nc.tensor.transpose(pt[:, (k % 16) * 128:(k % 16 + 1) * 128],
                                                    CB[h][:, k * 128:(k + 1) * 128], ident[:, :])
                            if k in (15, 31):
                                i.then_inc(sT.h, 1)
                        nc.scalar.wait_ge(dSp[h].h, aSs2[h].to(16 * j))
                        nc.scalar.wait_ge(sT.h, aTt.to(bT0 + 4 + 2 * o + 1))
                        nc.scalar.copy(WT[h][:, 0:2048], pt0[:, :]).then_inc(sA.h, 1)
                        nc.scalar.wait_ge(sT.h, aTt.to(bT0 + 4 + 2 * o + 2))
                        nc.scalar.copy(WT[h][:, 2048:4096], pt1[:, :]).then_inc(sA.h, 1)
                        nc.scalar.wait_ge(sA.h, aAo.to(bA0 + 6 + 3 * o + 3))
                        with free_temps(nc, mybir.EngineType.Activation):
                            nc.scalar.dma_start(out=bass.AP(dst.tensor,
                                                            aOf.to((slab_base + slab_step * o) * 128 * W),
                                                            [[W, 128], [1, W]]),
                                                in_=WT[h][:, :]).then_inc(dSp[h].h, 16)

                drive(nc, n_o // 2, unroll,
                      [(ySp, 2), (ySp2, 6), (ySo, 2 * 128 * W), (gLd2[0], 16), (gLd2[1], 16),
                       (aLd2[0], 16), (aLd2[1], 16), (aTc, 4), (gTt, 4), (tPc, 2),
                       (tAe0, 6), (tAe, 6), (aSs2[0], 16), (aSs2[1], 16),
                       (aTt, 4), (aAo, 6), (aOf, slab_step * 2 * 128 * W)],
                      body, f"ps{u}")
                nit = n_o // 2
                dL0.n += 16 * nit
                dL1.n += 16 * nit
                sP.n += 2 + 2 * nit
                sT.n += 4 + 4 * nit
                sA.n += 6 + 6 * nit
                dS0.n += 16 * nit
                dS1.n += 16 * nit
                nc.scalar.wait_ge(dS0.h, dS0.n)
                nc.scalar.wait_ge(dS1.h, dS1.n)
                nc.sync.wait_ge(dL0.h, dL0.n)
                nc.sync.wait_ge(dL1.h, dL1.n)
            nc.all_engine_barrier()

        def prolog_rowmajor(src, n_or, n_kc, dst, src_f32):
            dL0, dS = fresh_dma_sems()
            dL1, _u2 = fresh_dma_sems()
            dLp = [dL0, dL1]
            sdt = F32 if src_f32 else I16
            W = n_kc * 128
            M = n_or * 128
            nsub = (n_kc + 15) // 16
            u = uid()
            with contextlib.ExitStack() as ps:
                NI = [ps.enter_context(nc.sbuf_tensor(f"rni{h}_{u}", [128, W], sdt)) for h in range(2)]
                CB = [ps.enter_context(nc.sbuf_tensor(f"rcb{h}_{u}", [128, W], BF16)) for h in range(2)]
                DT = ps.enter_context(nc.sbuf_tensor(f"rdt_{u}", [128, W], BF16))
                PT = [ps.enter_context(nc.psum_tensor(f"rpt{h}_{u}", [128, 2048], BF16)) for h in range(2)]
                bV0, bT0, bA0, bS0 = sV.n, sT.n, sA.n, dS.n
                nc.vector.sem_inc(sV.h, 2)
                nc.tensor.sem_inc(sT.h, 2 * nsub)
                nc.scalar.sem_inc(sA.h, 2 * nsub)
                nc.scalar.sem_inc(dS.h, 16)
                dS.n += 16
                bS1 = dS.n
                ySp = Rg(ps, nc.sync, f"ySp{u}", unroll)
                ySo = Rg(ps, nc.sync, f"ySo{u}", unroll)
                gLd2 = [Rg(ps, nc.vector, f"gLd2{h}_{u}", unroll) for h in range(2)]
                gTt = Rg(ps, nc.vector, f"gTt{u}", unroll)
                tPc = Rg(ps, nc.tensor, f"tPc{u}", unroll)
                tAe = Rg(ps, nc.tensor, f"tAe{u}", unroll)
                aSs = Rg(ps, nc.scalar, f"aSs{u}", unroll)
                aTt = Rg(ps, nc.scalar, f"aTt{u}", unroll)
                aAo = Rg(ps, nc.scalar, f"aAo{u}", unroll)
                aOf = Rg(ps, nc.scalar, f"aOf{u}", unroll)

                def body(j):
                    for h in range(2):
                        o = 2 * j + h
                        nc.sync.wait_ge(sV.h, ySp.to(bV0 + 2 + o - 1))
                        with free_temps(nc, mybir.EngineType.SP):
                            nc.sync.dma_start(out=NI[h][:, :],
                                              in_=bass.AP(src.tensor, ySo.to(o * 128 * W),
                                                          [[W, 128], [1, W]])).then_inc(dLp[h].h, 16)
                        nc.vector.wait_ge(dLp[h].h, gLd2[h].to(16 * j + 16))
                        nc.vector.wait_ge(sT.h, gTt.to(bT0 + 2 * nsub + nsub * (o - 1)))
                        nc.vector.tensor_copy(CB[h][:, :], NI[h][:, :])
                        vfence(CB[h][:, W - 32:W]).then_inc(sV.h, 1)
                        for sb in range(nsub):
                            k0, k1 = sb * 16, min((sb + 1) * 16, n_kc)
                            pt = PT[sb % 2]
                            if sb < 2:
                                nc.tensor.wait_ge(sV.h, tPc.to(bV0 + 2 + o + 1))
                            nc.tensor.wait_ge(sA.h, tAe.to(bA0 + 2 * nsub + nsub * o + max(sb - 1, 0)))
                            for k in range(k0, k1):
                                i = nc.tensor.transpose(pt[:, (k - k0) * 128:(k - k0 + 1) * 128],
                                                        CB[h][:, k * 128:(k + 1) * 128], ident[:, :])
                            i.then_inc(sT.h, 1)
                            if sb == 0:
                                nc.scalar.wait_ge(dS.h, aSs.to(bS1 + 16 * o))
                            nc.scalar.wait_ge(sT.h, aTt.to(bT0 + 2 * nsub + nsub * o + sb + 1))
                            nc.scalar.copy(DT[:, k0 * 128:k1 * 128],
                                           pt[:, 0:(k1 - k0) * 128]).then_inc(sA.h, 1)
                        nc.scalar.wait_ge(sA.h, aAo.to(bA0 + 2 * nsub + nsub * o + nsub))
                        with free_temps(nc, mybir.EngineType.Activation):
                            nc.scalar.dma_start(
                                out=bass.AP(dst.tensor, aOf.to(o * 128),
                                            [[M, 128], [128 * M, n_kc], [1, 128]]),
                                in_=DT[:, :].rearrange("p (k c) -> p k c", c=128),
                            ).then_inc(dS.h, 16)

                drive(nc, n_or // 2, unroll,
                      [(ySp, 2), (ySo, 2 * 128 * W), (gLd2[0], 16), (gLd2[1], 16),
                       (gTt, 2 * nsub), (tPc, 2), (tAe, 2 * nsub), (aSs, 32),
                       (aTt, 2 * nsub), (aAo, 2 * nsub), (aOf, 2 * 128)],
                      body, f"pr{u}")
                nit = n_or // 2
                dL0.n += 16 * nit
                dL1.n += 16 * nit
                sV.n += 2 + 2 * nit
                sT.n += 2 * nsub + 2 * nsub * nit
                sA.n += 2 * nsub + 2 * nsub * nit
                dS.n += 32 * nit
                nc.scalar.wait_ge(dS.h, dS.n)
                nc.sync.wait_ge(dL0.h, dL0.n)
                nc.sync.wait_ge(dL1.h, dL1.n)
            nc.all_engine_barrier()

        prolog_slab(gw, NOC, guT_sc, False, slab_base=0, slab_step=2)
        prolog_slab(uw, NOC, guT_sc, False, slab_base=1, slab_step=2)
        prolog_slab(wq, NH, qT_sc, True)
        prolog_rowmajor(dw, NKC, NOC, dT_sc, False)
        prolog_rowmajor(wo, NKC, NKC, oT_sc, True)

        XN = stk.enter_context(nc.sbuf_tensor("XN", [128, NKC * NEXT], BF16))

        def emit_ln(src, nt, xn_stride):
            dL0, _u1 = fresh_dma_sems()
            dL1, _u2 = fresh_dma_sems()
            dLp = [dL0, dL1]
            srcv = src.rearrange("(g p) n -> p g n", p=128)
            ntt = nt // 512
            u = uid()
            rL, rT, rA, rV = 16 * 8, 8, 7, 71
            with contextlib.ExitStack() as ps:
                XF = [ps.enter_context(nc.sbuf_tensor(f"lxf{h}_{u}", [128, 4096], F32)) for h in range(2)]
                XB = ps.enter_context(nc.sbuf_tensor(f"lxb_{u}", [128, 4096], BF16))
                XQ = ps.enter_context(nc.sbuf_tensor(f"lxq_{u}", [128, 4096], BF16))
                MB = ps.enter_context(nc.sbuf_tensor(f"lmb_{u}", [128, 512], F32))
                T2 = ps.enter_context(nc.sbuf_tensor(f"lt2_{u}", [128, 512], F32))
                VV = ps.enter_context(nc.sbuf_tensor(f"lvv_{u}", [128, 512], F32))
                RS = ps.enter_context(nc.sbuf_tensor(f"lrs_{u}", [128, 512], F32))
                DDb = [ps.enter_context(nc.sbuf_tensor(f"ldd{h}_{u}", [128, 512], F32)) for h in range(2)]
                PM = ps.enter_context(nc.psum_tensor(f"lpm_{u}", [128, 512], F32))
                PQ = ps.enter_context(nc.psum_tensor(f"lpq_{u}", [128, 512], F32))
                bT0, bA0, bV0 = sT.n, sA.n, sV.n
                for tt in range(ntt):
                    cv = ca = ct = 0
                    for cg in range(4):
                        par = cg % 2
                        nc.sync.wait_ge(sV.h, bV0 + rV * tt)
                        if cg >= 2:
                            nc.sync.wait_ge(sV.h, bV0 + rV * tt + cg - 1)
                            nc.sync.wait_ge(sA.h, bA0 + rA * tt + cg - 1)
                        nc.sync.dma_start(
                            out=XF[par][:, :].rearrange("p (g n) -> p g n", n=512),
                            in_=srcv[:, cg * 8:(cg + 1) * 8, tt * 512:tt * 512 + 512],
                        ).then_inc(dLp[par].h, 16)
                        ldv = 64 * tt + 16 * (cg // 2 + 1)
                        nc.vector.wait_ge(dLp[par].h, ldv)
                        nc.vector.wait_ge(sT.h, bT0 + rT * tt + 2 * cg)
                        nc.vector.tensor_copy(XB[:, :], XF[par][:, :]).then_inc(sV.h, 1); cv += 1
                        nc.scalar.wait_ge(dLp[par].h, ldv)
                        nc.scalar.wait_ge(sT.h, bT0 + rT * tt + 2 * cg)
                        nc.scalar.activation(XQ[:, :], XF[par][:, :], AF.Square).then_inc(sA.h, 1); ca += 1
                        nc.tensor.wait_ge(sV.h, bV0 + rV * tt + cv)
                        for c in range(8):
                            i = nc.tensor.matmul(PM[:, :], onebf[:, :], XB[:, c * 512:(c + 1) * 512],
                                                 start=(cg == 0 and c == 0), stop=(cg == 3 and c == 7),
                                                 skip_group_check=True)
                        i.then_inc(sT.h, 1); ct += 1
                        nc.tensor.wait_ge(sA.h, bA0 + rA * tt + ca)
                        for c in range(8):
                            i = nc.tensor.matmul(PQ[:, :], onebf[:, :], XQ[:, c * 512:(c + 1) * 512],
                                                 start=(cg == 0 and c == 0), stop=(cg == 3 and c == 7),
                                                 skip_group_check=True)
                        i.then_inc(sT.h, 1); ct += 1
                    nc.vector.wait_ge(sT.h, bT0 + rT * tt + 8)
                    nc.vector.tensor_scalar_mul(MB[:, :], PM[:, :], 1.0 / D).then_inc(sV.h, 1); cv += 1
                    nc.scalar.wait_ge(sT.h, bT0 + rT * tt + 8)
                    nc.scalar.mul(T2[:, :], PQ[:, :], 1.0 / D).then_inc(sA.h, 1); ca += 1
                    nc.vector.wait_ge(sV.h, bV0 + rV * tt + cv)
                    nc.vector.scalar_tensor_tensor(VV[:, :], MB[:, :], -1.0, MB[:, :],
                                                   ALU.mult, ALU.mult).then_inc(sV.h, 1); cv += 1
                    nc.vector.wait_ge(sV.h, bV0 + rV * tt + cv)
                    nc.vector.wait_ge(sA.h, bA0 + rA * tt + ca)
                    nc.vector.scalar_tensor_tensor(VV[:, :], VV[:, :], EPS, T2[:, :],
                                                   ALU.add, ALU.add).then_inc(sV.h, 1); cv += 1
                    nc.scalar.wait_ge(sV.h, bV0 + rV * tt + cv)
                    nc.scalar.activation(RS[:, :], VV[:, :], AF.Ln).then_inc(sA.h, 1); ca += 1
                    nc.scalar.wait_ge(sA.h, bA0 + rA * tt + ca)
                    nc.scalar.activation(RS[:, :], RS[:, :], AF.Exp, scale=-0.5).then_inc(sA.h, 1); ca += 1
                    assert ca == rA and ct == rT
                    for cg in range(4):
                        par = cg % 2
                        nc.sync.wait_ge(sV.h, bV0 + rV * tt + 7 + 16 * cg)
                        nc.sync.dma_start(
                            out=XF[par][:, :].rearrange("p (g n) -> p g n", n=512),
                            in_=srcv[:, cg * 8:(cg + 1) * 8, tt * 512:tt * 512 + 512],
                        ).then_inc(dLp[par].h, 16)
                        nc.vector.wait_ge(dLp[par].h, 64 * tt + 32 + 16 * (cg // 2 + 1))
                        nc.vector.wait_ge(sA.h, bA0 + rA * tt + rA)
                        for c2 in range(4):
                            nc.vector.wait_ge(sV.h, bV0 + rV * tt + cv)
                            for c, dp in ((2 * c2, 0), (2 * c2 + 1, 1)):
                                nc.vector.tensor_sub(DDb[dp][:, :], XF[par][:, c * 512:(c + 1) * 512],
                                                     MB[:, :]).then_inc(sV.h, 1); cv += 1
                            nc.vector.wait_ge(sV.h, bV0 + rV * tt + cv)
                            for c, dp in ((2 * c2, 0), (2 * c2 + 1, 1)):
                                cc = cg * 8 + c
                                nc.vector.tensor_mul(
                                    XN[:, cc * xn_stride + tt * 512:cc * xn_stride + tt * 512 + 512],
                                    DDb[dp][:, :], RS[:, :]).then_inc(sV.h, 1); cv += 1
                    assert cv == rV, (cv, rV)
                dL0.n += 64 * ntt
                dL1.n += 64 * ntt
                sT.n += rT * ntt
                sA.n += rA * ntt
                sV.n += rV * ntt
                nc.sync.wait_ge(dL0.h, dL0.n)
                nc.sync.wait_ge(dL1.h, dL1.n)
            nc.all_engine_barrier()

        def emit_sweep1(nt, xn_stride):
            ntt = nt // 512
            dL0, dS0 = fresh_dma_sems()
            dL1, dS1 = fresh_dma_sems()
            dLp, dSp = [dL0, dL1], [dS0, dS1]
            u = uid()
            with contextlib.ExitStack() as ps:
                GUB = [ps.enter_context(nc.sbuf_tensor(f"s1gu{h}_{u}", [128, 2 * D], BF16)) for h in range(2)]
                GUC = [ps.enter_context(nc.sbuf_tensor(f"s1gc{h}_{u}", [128, 2], F32)) for h in range(2)]
                SG = ps.enter_context(nc.sbuf_tensor(f"s1sg_{u}", [128, ntt * 512], BF16))
                HH = [ps.enter_context(nc.sbuf_tensor(f"s1h{h}_{u}", [128, ntt * 512], BF16)) for h in range(2)]
                PG = [ps.enter_context(nc.psum_tensor(f"s1pg{t}_{u}", [128, 512], F32)) for t in range(ntt)]
                PU = [ps.enter_context(nc.psum_tensor(f"s1pu{t}_{u}", [128, 512], F32)) for t in range(ntt)]
                bT0, bA0, bV0 = sT.n, sA.n, sV.n
                nc.tensor.sem_inc(sT.h, 4 * ntt)
                nc.scalar.sem_inc(sA.h, 2 * ntt)
                nc.vector.sem_inc(sV.h, 2 * ntt)
                nc.scalar.sem_inc(dS0.h, 16)
                nc.scalar.sem_inc(dS1.h, 16)
                dS0.n += 16
                dS1.n += 16
                tLd2 = [Rg(ps, nc.tensor, f"tLd2{h}_{u}", unroll) for h in range(2)]
                aLd2 = [Rg(ps, nc.scalar, f"aLd2{h}_{u}", unroll) for h in range(2)]
                vLd2 = [Rg(ps, nc.vector, f"vLd2{h}_{u}", unroll) for h in range(2)]
                vSs2 = [Rg(ps, nc.vector, f"vSs2{h}_{u}", unroll) for h in range(2)]
                yTm = Rg(ps, nc.sync, f"yTm{u}", unroll)
                ySv = Rg(ps, nc.sync, f"ySv{u}", unroll)
                yOf = Rg(ps, nc.sync, f"yOf{u}", unroll)
                yGc = Rg(ps, nc.sync, f"yGc{u}", unroll)
                tAs = Rg(ps, nc.tensor, f"tAs{u}", unroll)
                tVh = Rg(ps, nc.tensor, f"tVh{u}", unroll)
                aTg = Rg(ps, nc.scalar, f"aTg{u}", unroll)
                aVh = Rg(ps, nc.scalar, f"aVh{u}", unroll)
                aHo = Rg(ps, nc.scalar, f"aHo{u}", unroll)
                vTu = Rg(ps, nc.vector, f"vTu{u}", unroll)
                vAs = Rg(ps, nc.vector, f"vAs{u}", unroll)
                vL2 = Rg(ps, nc.vector, f"vL2{u}", unroll)

                def body(j):
                    for h in range(2):
                        o = 2 * j + h
                        nc.sync.wait_ge(sT.h, yTm.to(bT0 + 4 * ntt + 2 * ntt * (o - 1)))
                        nc.sync.wait_ge(sV.h, ySv.to(bV0 + 2 * ntt + ntt * (o - 1)))
                        with free_temps(nc, mybir.EngineType.SP):
                            nc.sync.dma_start(
                                out=GUB[h][:, :].rearrange("p (g n) -> p g n", n=D),
                                in_=bass.AP(guT_sc.tensor, yOf.to(2 * o * 128 * D),
                                            [[D, 128], [128 * D, 2], [1, D]])).then_inc(dLp[h].h, 16)
                        with free_temps(nc, mybir.EngineType.SP):
                            nc.sync.dma_start(
                                out=GUC[h][:, :],
                                in_=bass.AP(gusI.tensor, yGc.to(2 * o),
                                            [[2 * NOC, 128], [1, 2]])).then_inc(dLp[h].h, 16)
                        nc.tensor.wait_ge(dLp[h].h, tLd2[h].to(32 * j + 32))
                        nc.tensor.wait_ge(sA.h, tAs.to(bA0 + 2 * ntt + ntt * o))
                        for t in range(ntt):
                            for k in range(NKC):
                                i = nc.tensor.matmul(
                                    PG[t][:, :], GUB[h][:, k * 128:(k + 1) * 128],
                                    XN[:, k * xn_stride + t * 512:k * xn_stride + t * 512 + 512],
                                    start=(k == 0), stop=(k == NKC - 1), skip_group_check=True)
                            i.then_inc(sT.h, 1)
                        nc.tensor.wait_ge(sV.h, tVh.to(bV0 + 2 * ntt + ntt * o))
                        for t in range(ntt):
                            for k in range(NKC):
                                i = nc.tensor.matmul(
                                    PU[t][:, :], GUB[h][:, D + k * 128:D + (k + 1) * 128],
                                    XN[:, k * xn_stride + t * 512:k * xn_stride + t * 512 + 512],
                                    start=(k == 0), stop=(k == NKC - 1), skip_group_check=True)
                            i.then_inc(sT.h, 1)
                        nc.scalar.wait_ge(sT.h, aTg.to(bT0 + 4 * ntt + 2 * ntt * o + ntt))
                        nc.scalar.wait_ge(sV.h, aVh.to(bV0 + 2 * ntt + ntt * o))
                        nc.scalar.wait_ge(dLp[h].h, aLd2[h].to(32 * j + 32))
                        for t in range(ntt):
                            nc.scalar.activation(SG[:, t * 512:(t + 1) * 512], PG[t][:, :], AF.Silu,
                                                 scale=GUC[h][:, 0:1]).then_inc(sA.h, 1)
                        nc.vector.wait_ge(sT.h, vTu.to(bT0 + 4 * ntt + 2 * ntt * o + 2 * ntt))
                        nc.vector.wait_ge(sA.h, vAs.to(bA0 + 2 * ntt + ntt * o + ntt))
                        nc.vector.wait_ge(dSp[h].h, vSs2[h].to(16 * j))
                        nc.vector.wait_ge(dLp[h].h, vLd2[h].to(32 * j + 32))
                        for t in range(ntt):
                            nc.vector.scalar_tensor_tensor(
                                HH[h][:, t * 512:(t + 1) * 512], PU[t][:, :], GUC[h][:, 1:2],
                                SG[:, t * 512:(t + 1) * 512], ALU.mult, ALU.mult).then_inc(sV.h, 1)
                        nc.scalar.wait_ge(sV.h, aVh.to(bV0 + 2 * ntt + ntt * o + ntt))
                        with free_temps(nc, mybir.EngineType.Activation):
                            nc.scalar.dma_start(out=bass.AP(hsc.tensor, aHo.to(o * 128 * NEXT),
                                                            [[NEXT, 128], [1, nt]]),
                                                in_=HH[h][:, :]).then_inc(dSp[h].h, 16)

                drive(nc, NOC // 2, unroll,
                      [(yTm, 4 * ntt), (ySv, 2 * ntt), (yOf, 4 * 128 * D), (yGc, 4),
                       (tLd2[0], 32), (tLd2[1], 32), (tAs, 2 * ntt), (tVh, 2 * ntt),
                       (aTg, 4 * ntt), (aVh, 2 * ntt), (aLd2[0], 32), (aLd2[1], 32),
                       (aHo, 2 * 128 * NEXT), (vTu, 4 * ntt), (vAs, 2 * ntt),
                       (vSs2[0], 16), (vSs2[1], 16), (vLd2[0], 32), (vLd2[1], 32)],
                      body, f"s1_{u}")
                nit = NOC // 2
                dL0.n += 32 * nit
                dL1.n += 32 * nit
                sT.n += 4 * ntt + 4 * ntt * nit
                sA.n += 2 * ntt + 2 * ntt * nit
                sV.n += 2 * ntt + 2 * ntt * nit
                dS0.n += 16 * nit
                dS1.n += 16 * nit
                nc.scalar.wait_ge(dS0.h, dS0.n)
                nc.scalar.wait_ge(dS1.h, dS1.n)
                nc.sync.wait_ge(dL0.h, dL0.n)
                nc.sync.wait_ge(dL1.h, dL1.n)
            nc.all_engine_barrier()

        def emit_sweep2(n_k, stat_sc, mov_sc, mov_stride, nt, res_src, res_tot, res_off,
                        dst, dst_tot, dst_off, scale_dram):
            ntt = nt // 512
            SL = 4
            ns = n_k // SL
            assert n_k % SL == 0 and ns % 2 == 0 and ns >= 4
            dK0, dS = fresh_dma_sems()
            dK1, dDS = fresh_dma_sems()
            dP, _unused = fresh_dma_sems()
            dKp = [dK0, dK1]
            u = uid()
            with contextlib.ExitStack() as ps:
                DK = [ps.enter_context(nc.sbuf_tensor(f"s2d{h}_{u}", [128, SL * 1024], BF16)) for h in range(2)]
                HK = [ps.enter_context(nc.sbuf_tensor(f"s2h{h}_{u}", [128, SL * 512], BF16)) for h in range(2)]
                XR = ps.enter_context(nc.sbuf_tensor(f"s2xr_{u}", [128, 8 * 512], F32))
                DSC = ps.enter_context(nc.sbuf_tensor(f"s2ds_{u}", [128, 8], F32))
                XO = ps.enter_context(nc.sbuf_tensor(f"s2xo_{u}", [128, 8 * 512], F32))
                PSb = [ps.enter_context(nc.psum_tensor(f"s2p{oc}_{u}", [128, 512], F32)) for oc in range(8)]
                bT0, bV0, bS0 = sT.n, sV.n, dS.n
                nc.tensor.sem_inc(sT.h, 2)
                nc.vector.sem_inc(sV.h, 8)
                nc.scalar.sem_inc(dS.h, 16)
                tLd2 = [Rg(ps, nc.tensor, f"tLd2{h}_{u}", unroll) for h in range(2)]
                yTb = Rg(ps, nc.sync, f"yTb{u}", unroll)
                ySv = Rg(ps, nc.sync, f"ySv{u}", unroll)
                yDS = Rg(ps, nc.sync, f"yDS{u}", unroll)
                yVx = Rg(ps, nc.gpsimd, f"yVx{u}", unroll)
                yDK = Rg(ps, nc.sync, f"yDK{u}", unroll)
                yHK = Rg(ps, nc.sync, f"yHK{u}", unroll)
                yXR = Rg(ps, nc.gpsimd, f"yXR{u}", unroll)
                tVp = Rg(ps, nc.tensor, f"tVp{u}", unroll)
                vTm = Rg(ps, nc.vector, f"vTm{u}", unroll)
                vLd = Rg(ps, nc.vector, f"vLd{u}", unroll)
                vDs = Rg(ps, nc.vector, f"vDs{u}", unroll)
                vSx = Rg(ps, nc.vector, f"vSx{u}", unroll)
                aVe = Rg(ps, nc.scalar, f"aVe{u}", unroll)
                aXO = Rg(ps, nc.scalar, f"aXO{u}", unroll)

                def kslab(par, it, s, s_first, s_last, og_i, tt_i):
                    nc.sync.wait_ge(sT.h, yTb.to(bT0 + 2 + ns * it + s - 1))
                    with free_temps(nc, mybir.EngineType.SP):
                        nc.sync.dma_start(
                            out=DK[par][:, :].rearrange("p (g n) -> p g n", n=1024),
                            in_=bass.AP(stat_sc.tensor, yDK.to(s * SL * 128 * D + og_i * 1024),
                                        [[D, 128], [128 * D, SL], [1, 1024]])).then_inc(dKp[par].h, 16)
                    with free_temps(nc, mybir.EngineType.SP):
                        nc.sync.dma_start(
                            out=HK[par][:, :].rearrange("p (g n) -> p g n", n=512),
                            in_=bass.AP(mov_sc.tensor,
                                        yHK.to(s * SL * 128 * mov_stride + tt_i * 512),
                                        [[mov_stride, 128], [128 * mov_stride, SL], [1, 512]])
                        ).then_inc(dKp[par].h, 16)
                    nc.tensor.wait_ge(dKp[par].h, tLd2[par].to(16 * ns * it + 32 * (s // 2) + 32))
                    if s_first:
                        nc.tensor.wait_ge(sV.h, tVp.to(bV0 + 8 + 8 * it))
                    for ks in range(SL):
                        for oc in range(8):
                            i = nc.tensor.matmul(
                                PSb[oc][:, :],
                                DK[par][:, ks * 1024 + oc * 128:ks * 1024 + (oc + 1) * 128],
                                HK[par][:, ks * 512:(ks + 1) * 512],
                                start=(s_first and ks == 0), stop=(s_last and ks == SL - 1),
                                skip_group_check=True)
                    i.then_inc(sT.h, 1)

                def body_tt(og_i, tt_i):
                    it = og_i * ntt + tt_i
                    nc.sync.wait_ge(sV.h, ySv.to(bV0 + 8 + 8 * it))
                    with free_temps(nc, mybir.EngineType.SP):
                        nc.sync.dma_start(out=DSC[:, :],
                                          in_=bass.AP(scale_dram.tensor, yDS.to(og_i * 8),
                                                      [[NKC, 128], [1, 8]])).then_inc(dDS.h, 16)
                    nc.gpsimd.wait_ge(sV.h, yVx.to(bV0 + 8 + 8 * it))
                    with free_temps(nc, mybir.EngineType.Pool):
                        nc.gpsimd.dma_start(
                            out=XR[:, :].rearrange("p (g n) -> p g n", n=512),
                            in_=bass.AP(res_src.tensor,
                                        yXR.to(og_i * 1024 * res_tot + res_off + tt_i * 512),
                                        [[res_tot, 128], [128 * res_tot, 8], [1, 512]]),
                        ).then_inc(dP.h, 16)
                    kslab(0, it, 0, True, False, og_i, tt_i)
                    kslab(1, it, 1, False, False, og_i, tt_i)

                    def kbody(m):
                        kslab(0, it, 2 * m + 2, False, False, og_i, tt_i)
                        kslab(1, it, 2 * m + 3, False, False, og_i, tt_i)

                    drive(nc, (ns - 4) // 2, unroll,
                          [(yTb, 2), (yDK, 2 * SL * 128 * D), (yHK, 2 * SL * 128 * mov_stride),
                           (tLd2[0], 32), (tLd2[1], 32)],
                          kbody, f"s2k{u}_{it if unroll else 0}")
                    kslab(0, it, ns - 2, False, False, og_i, tt_i)
                    kslab(1, it, ns - 1, False, True, og_i, tt_i)
                    nc.vector.wait_ge(sT.h, vTm.to(bT0 + 2 + ns * it + ns))
                    nc.vector.wait_ge(dP.h, vLd.to(16 * it + 16))
                    nc.vector.wait_ge(dDS.h, vDs.to(16 * it + 16))
                    nc.vector.wait_ge(dS.h, vSx.to(bS0 + 16 + 16 * it))
                    for oc in range(8):
                        nc.vector.scalar_tensor_tensor(
                            XO[:, oc * 512:(oc + 1) * 512], PSb[oc][:, :], DSC[:, oc:oc + 1],
                            XR[:, oc * 512:(oc + 1) * 512],
                            ALU.mult, ALU.add).then_inc(sV.h, 1)
                    nc.scalar.wait_ge(sV.h, aVe.to(bV0 + 8 + 8 * it + 8))
                    with free_temps(nc, mybir.EngineType.Activation):
                        nc.scalar.dma_start(
                            out=bass.AP(dst.tensor,
                                        aXO.to(og_i * 1024 * dst_tot + dst_off + tt_i * 512),
                                        [[dst_tot, 128], [128 * dst_tot, 8], [1, 512]]),
                            in_=XO[:, :].rearrange("p (g n) -> p g n", n=512),
                        ).then_inc(dS.h, 16)

                def body_og(og_i):
                    drive(nc, ntt, unroll,
                          [(yTb, ns), (ySv, 8), (yDS, 0), (yVx, 8), (yDK, 0), (yHK, 512),
                           (yXR, 512), (tLd2[0], 16 * ns), (tLd2[1], 16 * ns), (tVp, 8),
                           (vTm, ns), (vLd, 16), (vDs, 16), (vSx, 16), (aVe, 8), (aXO, 512)],
                          lambda tt_i: body_tt(og_i, tt_i), f"s2t{u}_{og_i if unroll else 0}")

                drive(nc, 4, unroll,
                      [(yTb, ns * ntt), (ySv, 8 * ntt), (yDS, 8), (yVx, 8 * ntt),
                       (yDK, 1024), (yHK, 0), (yXR, 1024 * res_tot),
                       (tLd2[0], 16 * ns * ntt), (tLd2[1], 16 * ns * ntt), (tVp, 8 * ntt),
                       (vTm, ns * ntt), (vLd, 16 * ntt), (vDs, 16 * ntt),
                       (vSx, 16 * ntt), (aVe, 8 * ntt), (aXO, 1024 * dst_tot)],
                      body_og, f"s2o{u}")
                nit = 4 * ntt
                dK0.n += 16 * ns * nit
                dK1.n += 16 * ns * nit
                dDS.n += 16 * nit
                dP.n += 16 * nit
                sT.n += 2 + ns * nit
                sV.n += 8 + 8 * nit
                dS.n += 16 + 16 * nit
                nc.scalar.wait_ge(dS.h, dS.n)
                nc.sync.wait_ge(dK0.h, dK0.n)
                nc.sync.wait_ge(dK1.h, dK1.n)
                nc.sync.wait_ge(dDS.h, dDS.n)
                nc.gpsimd.wait_ge(dP.h, dP.n)
            nc.all_engine_barrier()

        def emit_attn():
            u = uid()
            with contextlib.ExitStack() as ats:
                KRT = ats.enter_context(nc.sbuf_tensor(f"aKRT_{u}", [128, NEXT], BF16))
                VTM = ats.enter_context(nc.sbuf_tensor(f"aVTM_{u}", [128, 12 * 128], BF16))

                dL, dS = fresh_dma_sems()
                with contextlib.ExitStack() as ps:
                    WF = ps.enter_context(nc.sbuf_tensor(f"aWF_{u}", [128, D], F32))
                    WB = ps.enter_context(nc.sbuf_tensor(f"aWB_{u}", [128, D], BF16))
                    WTs = ps.enter_context(nc.sbuf_tensor(f"aWT_{u}", [128, D], BF16))
                    KF = ps.enter_context(nc.sbuf_tensor(f"aKF_{u}", [128, NEXT], F32))
                    VB = ps.enter_context(nc.sbuf_tensor(f"aVB_{u}", [128, NEXT], BF16))
                    R1 = ps.enter_context(nc.sbuf_tensor(f"aR1_{u}", [128, NEXT], F32))
                    R2 = ps.enter_context(nc.sbuf_tensor(f"aR2_{u}", [128, NEXT], F32))
                    PT = [ps.enter_context(nc.psum_tensor(f"aPT{h}_{u}", [128, 2048], BF16)) for h in range(2)]
                    PKV = ps.enter_context(nc.psum_tensor(f"aPKV_{u}", [128, 512], F32))

                    def kv_one(wmat, out_is_k):
                        bL, bT, bA, bV = dL.n, sT.n, sA.n, sV.n
                        with nc.Block() as bb:
                            @bb.sync
                            def _(s):
                                s.dma_start(out=WF[:, :], in_=wmat[:, :]).then_inc(dL.h, 16)

                            @bb.vector
                            def _(v):
                                v.wait_ge(dL.h, bL + 16)
                                v.tensor_copy(WB[:, :], WF[:, :])
                                vfence(WB[:, D - 32:D]).then_inc(sV.h, 1)

                            @bb.tensor
                            def _(t):
                                t.wait_ge(sV.h, bV + 1)
                                for k in range(NKC):
                                    i = t.transpose(PT[k // 16][:, (k % 16) * 128:(k % 16 + 1) * 128],
                                                    WB[:, k * 128:(k + 1) * 128], ident[:, :])
                                    if k in (15, 31):
                                        i.then_inc(sT.h, 1)
                                t.wait_ge(sA.h, bA + 2)
                                for t3 in range(3):
                                    if t3 > 0:
                                        t.wait_ge(sA.h, bA + 2 + t3)
                                    for k in range(NKC):
                                        i = t.matmul(PKV[:, :], WTs[:, k * 128:(k + 1) * 128],
                                                     XN[:, k * NEXT + t3 * 512:k * NEXT + t3 * 512 + 512],
                                                     start=(k == 0), stop=(k == NKC - 1),
                                                     skip_group_check=True)
                                    i.then_inc(sT.h, 1)

                            @bb.scalar
                            def _(sc):
                                sc.wait_ge(sT.h, bT + 1)
                                sc.copy(WTs[:, 0:2048], PT[0][:, :]).then_inc(sA.h, 1)
                                sc.wait_ge(sT.h, bT + 2)
                                sc.copy(WTs[:, 2048:4096], PT[1][:, :]).then_inc(sA.h, 1)
                                for t3 in range(3):
                                    sc.wait_ge(sT.h, bT + 2 + t3 + 1)
                                    if out_is_k:
                                        sc.copy(KF[:, t3 * 512:(t3 + 1) * 512], PKV[:, :]).then_inc(sA.h, 1)
                                    else:
                                        sc.copy(VB[:, t3 * 512:(t3 + 1) * 512], PKV[:, :]).then_inc(sA.h, 1)
                                sc.wait_ge(sA.h, bA + 5)
                        dL.n += 16
                        sV.n += 1
                        sT.n += 5
                        sA.n += 5
                        nc.all_engine_barrier()

                    kv_one(wk, True)
                    kv_one(wv, False)

                    bT, bA, bV = sT.n, sA.n, sV.n
                    with nc.Block() as b2:
                        @b2.vector
                        def _(v):
                            v.tensor_mul(R1[:, :], KF[:, :], cosE[:, :]).then_inc(sV.h, 1)
                            v.tensor_mul(R2[0:64, :], KF[64:128, :], sinE[64:128, :]).then_inc(sV.h, 1)
                            v.tensor_mul(R2[64:128, :], KF[0:64, :], sinE[0:64, :]).then_inc(sV.h, 1)
                            v.wait_ge(sV.h, bV + 3)
                            v.tensor_add(KRT[:, :], R1[:, :], R2[:, :]).then_inc(sV.h, 1)
                            v.wait_ge(sV.h, bV + 4)

                        @b2.tensor
                        def _(t):
                            for e in range(12):
                                i = t.transpose(PT[0][:, e * 128:(e + 1) * 128],
                                                VB[:, e * 128:(e + 1) * 128], ident[:, :])
                            i.then_inc(sT.h, 1)

                        @b2.scalar
                        def _(sc):
                            sc.wait_ge(sT.h, bT + 1)
                            sc.copy(VTM[:, 0:12 * 128], PT[0][:, 0:12 * 128]).then_inc(sA.h, 1)
                            sc.wait_ge(sA.h, bA + 1)
                    sV.n += 4
                    sT.n += 1
                    sA.n += 1
                    nc.sync.wait_ge(dL.h, dL.n)
                nc.all_engine_barrier()

                dL, dS = fresh_dma_sems()
                with contextlib.ExitStack() as ps:
                    WQB = ps.enter_context(nc.sbuf_tensor(f"hWQ_{u}", [128, D], BF16))
                    QF = ps.enter_context(nc.sbuf_tensor(f"hQF_{u}", [128, NOWN], F32))
                    QR = ps.enter_context(nc.sbuf_tensor(f"hQR_{u}", [128, NOWN], BF16))
                    R1 = ps.enter_context(nc.sbuf_tensor(f"hR1_{u}", [128, NOWN], F32))
                    R2 = ps.enter_context(nc.sbuf_tensor(f"hR2_{u}", [128, NOWN], F32))
                    SGT = [ps.enter_context(nc.sbuf_tensor(f"hSG{w}_{u}", [128, 2048], BF16)) for w in range(2)]
                    RC = ps.enter_context(nc.sbuf_tensor(f"hRC_{u}", [128, 512], F32))
                    RW = ps.enter_context(nc.sbuf_tensor(f"hRW_{u}", [128, 512], F32))
                    CBb = [ps.enter_context(nc.sbuf_tensor(f"hCB{w}_{u}", [128, 512], F32)) for w in range(2)]
                    ATH = ps.enter_context(nc.sbuf_tensor(f"hATH_{u}", [128, NOWN], BF16))
                    PQ2 = [ps.enter_context(nc.psum_tensor(f"hPQ{t2}_{u}", [128, 512], F32)) for t2 in range(2)]
                    PST = [ps.enter_context(nc.psum_tensor(f"hPS{w}_{u}", [128, 512], F32)) for w in range(2)]
                    PR = ps.enter_context(nc.psum_tensor(f"hPR_{u}", [128, 512], F32))
                    PW = ps.enter_context(nc.psum_tensor(f"hPW_{u}", [128, 512], F32))
                    bL0, bT0, bA0, bV0, bS0 = dL.n, sT.n, sA.n, sV.n, dS.n
                    rT, rA, rV = 32, 22, 23
                    nc.tensor.sem_inc(sT.h, rT)
                    nc.scalar.sem_inc(sA.h, rA)
                    nc.vector.sem_inc(sV.h, rV)
                    nc.scalar.sem_inc(dS.h, 16)
                    REC = [5, 8, 12, 16, 20]
                    RCW = [6, 9, 13, 17, 21]
                    CTR = [7, 10, 14, 18, 22]
                    yTq = Rg(ps, nc.sync, f"yTq{u}", unroll)
                    yWq = Rg(ps, nc.sync, f"yWq{u}", unroll)
                    tLd = Rg(ps, nc.tensor, f"tLdh{u}", unroll)
                    tAx = Rg(ps, nc.tensor, f"tAx{u}", unroll)
                    tVx = Rg(ps, nc.tensor, f"tVx{u}", unroll)
                    aTx = Rg(ps, nc.scalar, f"aTx{u}", unroll)
                    aVx = Rg(ps, nc.scalar, f"aVx{u}", unroll)
                    aAo = Rg(ps, nc.scalar, f"aAoh{u}", unroll)
                    vAx = Rg(ps, nc.vector, f"vAx{u}", unroll)
                    vTx = Rg(ps, nc.vector, f"vTxh{u}", unroll)
                    vVx = Rg(ps, nc.vector, f"vVx{u}", unroll)
                    vSs = Rg(ps, nc.vector, f"vSsh{u}", unroll)

                    def body(hh):
                        ctT = ctA = ctV = 0
                        nc.sync.wait_ge(sT.h, yTq.to(bT0 + rT * hh + 2))
                        with free_temps(nc, mybir.EngineType.SP):
                            nc.sync.dma_start(out=WQB[:, :],
                                              in_=bass.AP(qT_sc.tensor, yWq.to(hh * 128 * D),
                                                          [[D, 128], [1, D]])).then_inc(dL.h, 16)
                        nc.tensor.wait_ge(dL.h, tLd.to(bL0 + 16 * hh + 16))
                        for t2 in range(2):
                            nc.tensor.wait_ge(sA.h, tAx.to(bA0 + rA * hh + t2 + 1))
                            for k in range(NKC):
                                i = nc.tensor.matmul(
                                    PQ2[t2][:, :], WQB[:, k * 128:(k + 1) * 128],
                                    XN[:, k * NEXT + HALO + t2 * 512:k * NEXT + HALO + t2 * 512 + 512],
                                    start=(k == 0), stop=(k == NKC - 1), skip_group_check=True)
                            i.then_inc(sT.h, 1); ctT += 1
                        for t2 in range(2):
                            nc.scalar.wait_ge(sT.h, aTx.to(bT0 + rT + rT * hh + t2 + 1))
                            nc.scalar.wait_ge(sV.h, aVx.to(bV0 + rV * hh + 4))
                            nc.scalar.copy(QF[:, t2 * 512:(t2 + 1) * 512], PQ2[t2][:, :]).then_inc(sA.h, 1)
                            ctA += 1
                        nc.vector.wait_ge(sA.h, vAx.to(bA0 + rA + rA * hh + 2))
                        nc.vector.wait_ge(sT.h, vTx.to(bT0 + rT + rT * hh))
                        nc.vector.tensor_mul(R1[:, :], QF[:, :],
                                             cosE[:, HALO:HALO + NOWN]).then_inc(sV.h, 1); ctV += 1
                        nc.vector.tensor_mul(R2[0:64, :], QF[64:128, :],
                                             sinE[64:128, HALO:HALO + NOWN]).then_inc(sV.h, 1); ctV += 1
                        nc.vector.tensor_mul(R2[64:128, :], QF[0:64, :],
                                             sinE[0:64, HALO:HALO + NOWN]).then_inc(sV.h, 1); ctV += 1
                        nc.vector.wait_ge(sV.h, vVx.to(bV0 + rV + rV * hh + ctV))
                        nc.vector.tensor_add(QR[:, :], R1[:, :], R2[:, :]).then_inc(sV.h, 1); ctV += 1
                        for wi in range(5):
                            qoff, qw = WIN_Q[wi]
                            kb = 2 * wi
                            woff = WIN_WOFF[wi]
                            for kc in range(4):
                                g = 4 * wi + kc
                                if wi == 0 and kc == 0:
                                    nc.tensor.wait_ge(sV.h, tVx.to(bV0 + rV + rV * hh + 4))
                                nc.tensor.wait_ge(sA.h, tAx.to(bA0 + rA + rA * hh + 2 + g - 1))
                                i = nc.tensor.matmul(PST[kc % 2][:, 0:qw],
                                                     KRT[:, (kb + kc) * 128:(kb + kc + 1) * 128],
                                                     QR[:, qoff:qoff + qw],
                                                     start=True, stop=True, skip_group_check=True)
                                i.then_inc(sT.h, 1); ctT += 1
                                nc.scalar.wait_ge(sT.h, aTx.to(bT0 + rT + rT * hh + 2 + 6 * wi + kc + 1))
                                nc.scalar.activation(SGT[wi % 2][:, kc * 512:kc * 512 + qw],
                                                     PST[kc % 2][:, 0:qw], AF.Sigmoid,
                                                     scale=SCALE).then_inc(sA.h, 1); ctA += 1
                            for kc in range(4):
                                g = 4 * wi + kc
                                nc.tensor.wait_ge(sA.h, tAx.to(bA0 + rA + rA * hh + 2 + g + 1))
                                if kc == 0:
                                    if wi == 0:
                                        nc.tensor.wait_ge(sV.h, tVx.to(bV0 + rV + rV * hh))
                                    else:
                                        nc.tensor.wait_ge(sV.h, tVx.to(bV0 + rV + rV * hh + REC[wi - 1]))
                                i = nc.tensor.matmul(PR[:, 0:qw],
                                                     onesm[:, (kb + kc) * 128:(kb + kc + 1) * 128],
                                                     SGT[wi % 2][:, kc * 512:kc * 512 + qw],
                                                     start=(kc == 0), stop=(kc == 3), skip_group_check=True)
                            i.then_inc(sT.h, 1); ctT += 1
                            for kc in range(4):
                                if kc == 0:
                                    if wi == 0:
                                        nc.tensor.wait_ge(sV.h, tVx.to(bV0 + rV + rV * hh))
                                    else:
                                        nc.tensor.wait_ge(sV.h, tVx.to(bV0 + rV + rV * hh + CTR[wi - 1]))
                                i = nc.tensor.matmul(PW[:, 0:qw],
                                                     VTM[:, (kb + kc) * 128:(kb + kc + 1) * 128],
                                                     SGT[wi % 2][:, kc * 512:kc * 512 + qw],
                                                     start=(kc == 0), stop=(kc == 3), skip_group_check=True)
                            i.then_inc(sT.h, 1); ctT += 1
                            nc.vector.wait_ge(sT.h, vTx.to(bT0 + rT + rT * hh + 2 + 6 * wi + 5))
                            if wi > 0:
                                nc.vector.wait_ge(sV.h, vVx.to(bV0 + rV + rV * hh + RCW[wi - 1]))
                            nc.vector.reciprocal(RC[:, 0:qw], PR[:, 0:qw]).then_inc(sV.h, 1); ctV += 1
                            nc.vector.wait_ge(sV.h, vVx.to(bV0 + rV + rV * hh + ctV))
                            nc.vector.tensor_mul(RW[:, 0:qw], RC[:, 0:qw],
                                                 wgt[:, woff:woff + qw]).then_inc(sV.h, 1); ctV += 1
                            nc.vector.wait_ge(sT.h, vTx.to(bT0 + rT + rT * hh + 2 + 6 * wi + 6))
                            nc.vector.wait_ge(sV.h, vVx.to(bV0 + rV + rV * hh + ctV))
                            nc.vector.tensor_mul(CBb[wi % 2][:, 0:qw], PW[:, 0:qw],
                                                 RW[:, 0:qw]).then_inc(sV.h, 1); ctV += 1
                            if wi > 0:
                                aoff = 0 if wi - 1 == 0 else 256
                                if wi == 1:
                                    nc.vector.wait_ge(dS.h, vSs.to(bS0 + 16 + 16 * hh))
                                nc.vector.wait_ge(sV.h, vVx.to(bV0 + rV + rV * hh + ctV))
                                nc.vector.tensor_add(ATH[:, (wi - 1) * 256:wi * 256],
                                                     CBb[(wi - 1) % 2][:, aoff:aoff + 256],
                                                     CBb[wi % 2][:, 0:256]).then_inc(sV.h, 1); ctV += 1
                        assert ctT == rT and ctA == rA and ctV == rV, (ctT, ctA, ctV)
                        nc.scalar.wait_ge(sV.h, aVx.to(bV0 + rV + rV * hh + rV))
                        with free_temps(nc, mybir.EngineType.Activation):
                            nc.scalar.dma_start(out=bass.AP(attnsc.tensor, aAo.to(hh * 128 * NOWN),
                                                            [[NOWN, 128], [1, NOWN]]),
                                                in_=ATH[:, :]).then_inc(dS.h, 16)

                    drive(nc, NH, unroll,
                          [(yTq, rT), (yWq, 128 * D), (tLd, 16), (tAx, rA), (tVx, rV),
                           (aTx, rT), (aVx, rV), (aAo, 128 * NOWN), (vAx, rA), (vTx, rT),
                           (vVx, rV), (vSs, 16)],
                          body, f"hd{u}")
                    dL.n += 16 * NH
                    sT.n += rT + rT * NH
                    sA.n += rA + rA * NH
                    sV.n += rV + rV * NH
                    dS.n += 16 + 16 * NH
                    nc.scalar.wait_ge(dS.h, dS.n)
                    nc.sync.wait_ge(dL.h, dL.n)
                nc.all_engine_barrier()

        emit_ln(xe, NEXT, NEXT)
        if debug == 2:
            xnsc = nc.dram_tensor("xnsc", [128, NKC * NEXT], BF16, kind="ExternalOutput").ap()
            dbgL, _ = fresh_dma_sems()
            with nc.Block() as bdbg:
                @bdbg.sync
                def _(s):
                    s.dma_start(out=xnsc[:, :], in_=XN[:, :]).then_inc(dbgL.h, 16)
                    s.wait_ge(dbgL.h, 16)
            dbgL.n += 16
            nc.all_engine_barrier()
        emit_sweep1(NEXT, NEXT)
        emit_sweep2(NOC, dT_sc, hsc, NEXT, NEXT, xe, NEXT, 0, x1sc, NEXT, 0, dsh_dram)
        emit_ln(x1sc, NEXT, NEXT)
        emit_attn()
        emit_sweep2(NKC, oT_sc, attnsc, NOWN, NOWN, x1sc, NEXT, HALO, x2sc, NOWN, 0, oneI)
        emit_ln(x2sc, NOWN, NOWN)
        emit_sweep1(NOWN, NOWN)
        emit_sweep2(NOC, dT_sc, hsc, NEXT, NOWN, x2sc, NOWN, 0, yT, NOWN, 0, dsh_dram)
    return nc



def _rope_tables():
    base = 10000.0
    Dh = 128
    inv_freq = 1.0 / base ** (np.arange(0, Dh, 2, dtype=np.float32) / Dh)
    t = np.arange(S, dtype=np.float32)
    f = np.outer(t, inv_freq)
    cos = np.zeros((S, Dh), dtype=np.float32)
    sin = np.zeros((S, Dh), dtype=np.float32)
    cos[:, 0::2] = np.cos(f)
    cos[:, 1::2] = np.cos(f)
    sin[:, 0::2] = np.sin(f)
    sin[:, 1::2] = np.sin(f)
    return cos, sin


def _host_prep(inputs):
    x = np.asarray(inputs["L0_attn_x"])
    Wq = np.ascontiguousarray(np.asarray(inputs["L0_attn_Wq"], dtype=np.float32))
    Wk = np.ascontiguousarray(np.asarray(inputs["L0_attn_Wk"], dtype=np.float32))
    Wv = np.ascontiguousarray(np.asarray(inputs["L0_attn_Wv"], dtype=np.float32))
    Wo = np.ascontiguousarray(np.asarray(inputs["L0_attn_Wo"], dtype=np.float32))
    gw32 = np.asarray(inputs["L0_ffn_gate_wq"])
    uw32 = np.asarray(inputs["L0_ffn_up_wq"])
    dw32 = np.asarray(inputs["L0_ffn_down_wq"])
    gs = np.asarray(inputs["L0_ffn_gate_scale"], dtype=np.float32)
    us = np.asarray(inputs["L0_ffn_up_scale"], dtype=np.float32)
    dsc = np.asarray(inputs["L0_ffn_down_scale"], dtype=np.float32)

    def lo16(a):
        return np.ascontiguousarray(a.view(np.int16)[:, 0::2])

    gw16, uw16, dw16 = lo16(gw32), lo16(uw32), lo16(dw32)
    xT = np.ascontiguousarray(x[0].T.astype(np.float32))

    gsr = gs.reshape(NOC, 128).T
    usr = us.reshape(NOC, 128).T
    gusr = np.empty((128, 2 * NOC), dtype=np.float32)
    gusr[:, 0::2] = gsr
    gusr[:, 1::2] = usr
    gusr = np.ascontiguousarray(gusr)
    dsr = np.ascontiguousarray(dsc.reshape(NKC, 128).T)
    onev = np.ones((128, NKC), dtype=np.float32)

    cos, sin = _rope_tables()
    alpha = np.linspace(0.0, 1.0, 256, dtype=np.float32)
    idn = np.eye(128, dtype=ml_dtypes.bfloat16)

    in_maps = []
    for c in range(8):
        g0 = c * NOWN - HALO
        xev = np.zeros((D, NEXT), dtype=np.float32)
        lo, hi = max(0, g0), min(S, g0 + NEXT)
        xev[:, lo - g0:hi - g0] = xT[:, lo:hi]
        pos = np.clip(np.arange(g0, g0 + NEXT), 0, S - 1)
        cosEv = np.ascontiguousarray(cos[pos].T)
        sinv = sin[pos].T
        sintbl = np.empty_like(sinv)
        sintbl[64:128] = -sinv[0:64]
        sintbl[0:64] = sinv[64:128]
        sintbl = np.ascontiguousarray(sintbl)
        wgtbl = np.zeros((2048,), dtype=np.float32)
        for wi in range(5):
            off = WIN_WOFF[wi]
            if wi == 0:
                wgtbl[off:off + 256] = 0.0 if c == 0 else 1.0 - alpha
            elif wi == 4:
                wgtbl[off:off + 256] = alpha
            else:
                wgtbl[off:off + 256] = 1.0 if (c == 0 and wi == 1) else alpha
                wgtbl[off + 256:off + 512] = 1.0 - alpha
        wgtblB = np.ascontiguousarray(np.broadcast_to(wgtbl, (128, 2048)))
        onesmv = np.ones((12, 128, 128), dtype=np.float32)
        for jch in range(12):
            kk = np.arange(g0 + jch * 128, g0 + jch * 128 + 128)
            onesmv[jch, (kk < 0) | (kk >= S), :] = 0.0
        onesmv = np.ascontiguousarray(
            onesmv.transpose(1, 0, 2).reshape(128, 12 * 128)).astype(ml_dtypes.bfloat16)
        in_maps.append({
            "xe": xev, "gw": gw16, "uw": uw16, "dw": dw16,
            "wq": Wq, "wk": Wk, "wv": Wv, "wo": Wo,
            "gus": gusr, "ds": dsr, "onev": onev,
            "cosE": cosEv, "sinE": sintbl, "wgt": wgtblB,
            "onesm": onesmv, "ident": idn,
        })
    return in_maps


def kernel(**inputs):
    in_maps = _host_prep(inputs)
    nc = build_nc(debug=False, unroll=False)
    res = run_bass_kernel_spmd(nc, in_maps, core_ids=list(range(8)))
    out = np.empty((1, S, D), dtype=np.float32)
    for c in range(8):
        out[0, c * NOWN:(c + 1) * NOWN, :] = res.results[c]["yT"].T
    return out


# version 06 (final)
# speedup vs baseline: 1.2178x; 1.1201x over previous
import contextlib
import numpy as np
import ml_dtypes
import concourse.bass as bass
import concourse.mybir as mybir
from concourse.bass_utils import run_bass_kernel_spmd

F32 = mybir.dt.float32
BF16 = mybir.dt.bfloat16
I16 = mybir.dt.int16
AF = mybir.ActivationFunctionType
ALU = mybir.AluOpType

D = 4096
DI = 14336
NKC = D // 128
NOC = DI // 128
NH = 32
S = 8192
NOWN = 1024
HALO = 256
NEXT = NOWN + 2 * HALO
SCALE = 1.0 / float(np.sqrt(128.0))
EPS = 1e-5

WIN_Q = [(0, 256), (0, 512), (256, 512), (512, 512), (768, 256)]
WIN_WOFF = [0, 256, 768, 1280, 1792]

_ENG_ALL = list(mybir.ALL_ENGINES)


class SemC:
    def __init__(self, stk, nc, name):
        self.h = stk.enter_context(nc.semaphore(name))
        self.n = 0


class Rg:
    def __init__(self, ps, eng, name, unroll):
        self.unroll = unroll
        self.eng = eng
        if not unroll:
            self.r = ps.enter_context(eng.register(name))
            eng.reg_mov(self.r, 0)
            self.cur0 = 0

    def to(self, v):
        if self.unroll:
            return int(v)
        d = int(v) - self.cur0
        if d > 0:
            self.eng.reg_add(self.r, self.r, d)
        elif d < 0:
            self.eng.reg_sub(self.r, self.r, -d)
        self.cur0 = int(v)
        return self.r


@contextlib.contextmanager
def raw_fori(nc, n, tag):
    regs = nc.alloc_registers(f"fl{tag}_i", engines=_ENG_ALL)
    nc.regs_mov(regs, 0)
    lbl, end = f"fl{tag}_L", f"fl{tag}_E"
    nc.br(lbl, engines=_ENG_ALL)
    with nc.body(lbl, valid_engines=_ENG_ALL):
        yield
        nc.regs_alu(regs, regs, 1, op=ALU.add)
        nc.br_lt(regs, n, on_true=lbl, on_false=end, engines=_ENG_ALL)
    nc.switch_bb(end)
    for hdl in regs.handles:
        nc.free_register(hdl)


def _free_lowering_temps(nc, eng_enum, n0):
    from bass_rust import RegisterHandle
    for nm in nc.inst_names()[n0:]:
        inst = nc.find_inst_by_name(nm)
        if inst is None:
            continue
        if type(inst).__name__ in ("InstRegisterMove", "InstRegisterAlu"):
            for a in list(inst.outs):
                r = getattr(a, "regref", None)
                if isinstance(r, str) and ("_snap_" in r or "_tmp_" in r):
                    try:
                        nc.free_register(RegisterHandle(r, eng_enum))
                    except Exception:
                        pass


@contextlib.contextmanager
def free_temps(nc, eng_enum):
    n0 = nc.n_instructions()
    yield
    _free_lowering_temps(nc, eng_enum, n0)


def drive(nc, n, unroll, bumps, body, tag):
    if unroll:
        for j in range(n):
            body(j)
    else:
        entry = [rg.cur0 for rg, _ in bumps]
        with raw_fori(nc, n, tag):
            body(0)
            for (rg, rate), e in zip(bumps, entry):
                rg.to(e + rate)
        for (rg, rate), e in zip(bumps, entry):
            rg.cur0 = e + rate * n


def build_nc(debug=False, unroll=False):
    nc = bass.Bass("TRN2", target_bir_lowering=False,
                   detect_race_conditions=bool(unroll))

    xe = nc.dram_tensor("xe", [D, NEXT], F32, kind="ExternalInput").ap()
    gw = nc.dram_tensor("gw", [DI, D], I16, kind="ExternalInput").ap()
    uw = nc.dram_tensor("uw", [DI, D], I16, kind="ExternalInput").ap()
    dw = nc.dram_tensor("dw", [D, DI], I16, kind="ExternalInput").ap()
    wq = nc.dram_tensor("wq", [D, D], F32, kind="ExternalInput").ap()
    wk = nc.dram_tensor("wk", [128, D], F32, kind="ExternalInput").ap()
    wv = nc.dram_tensor("wv", [128, D], F32, kind="ExternalInput").ap()
    wo = nc.dram_tensor("wo", [D, D], F32, kind="ExternalInput").ap()
    gusI = nc.dram_tensor("gus", [128, 2 * NOC], F32, kind="ExternalInput").ap()
    dsI = nc.dram_tensor("ds", [128, NKC], F32, kind="ExternalInput").ap()
    oneI = nc.dram_tensor("onev", [128, NKC], F32, kind="ExternalInput").ap()
    cosI = nc.dram_tensor("cosE", [128, NEXT], F32, kind="ExternalInput").ap()
    sinI = nc.dram_tensor("sinE", [128, NEXT], F32, kind="ExternalInput").ap()
    wgtI = nc.dram_tensor("wgt", [128, 2048], F32, kind="ExternalInput").ap()
    onesmI = nc.dram_tensor("onesm", [128, 12 * 128], BF16, kind="ExternalInput").ap()
    identI = nc.dram_tensor("ident", [128, 128], BF16, kind="ExternalInput").ap()

    yT = nc.dram_tensor("yT", [D, NOWN], F32, kind="ExternalOutput").ap()

    SCK = "ExternalOutput" if debug else "Internal"
    SCK2 = "ExternalOutput" if debug == 2 else "Internal"
    x1sc = nc.dram_tensor("x1sc", [D, NEXT], F32, kind=SCK).ap()
    x2sc = nc.dram_tensor("x2sc", [D, NOWN], F32, kind=SCK).ap()
    guT_sc = nc.dram_tensor("guT_sc", [2 * NOC * 128, D], BF16, kind=SCK2).ap()
    qT_sc = nc.dram_tensor("qT_sc", [NH * 128, D], BF16, kind=SCK2).ap()
    dT_sc = nc.dram_tensor("dT_sc", [DI, D], BF16, kind=SCK2).ap()
    oT_sc = nc.dram_tensor("oT_sc", [D, D], BF16, kind=SCK2).ap()
    hsc = nc.dram_tensor("hsc", [NOC * 128, NEXT], BF16, kind=SCK2).ap()
    attnsc = nc.dram_tensor("attnsc", [NH * 128, NOWN], BF16, kind=SCK2).ap()

    stk = contextlib.ExitStack()
    with stk:
        sT = SemC(stk, nc, "sT")
        sA = SemC(stk, nc, "sA")
        sV = SemC(stk, nc, "sV")
        sP = SemC(stk, nc, "sP")
        _ctr = [0]

        def fresh_dma_sems():
            _ctr[0] += 1
            return SemC(stk, nc, f"dL{_ctr[0]}"), SemC(stk, nc, f"dS{_ctr[0]}")

        def uid():
            _ctr[0] += 1
            return _ctr[0]

        cosE = stk.enter_context(nc.sbuf_tensor("cosE_s", [128, NEXT], F32))
        sinE = stk.enter_context(nc.sbuf_tensor("sinE_s", [128, NEXT], F32))
        wgt = stk.enter_context(nc.sbuf_tensor("wgt_s", [128, 2048], F32))
        onesm = stk.enter_context(nc.sbuf_tensor("onesm_s", [128, 12 * 128], BF16))
        ident = stk.enter_context(nc.sbuf_tensor("ident_s", [128, 128], BF16))
        dshS = stk.enter_context(nc.sbuf_tensor("dsh_s", [128, NKC], F32))
        onebf = stk.enter_context(nc.sbuf_tensor("onebf_s", [128, 128], BF16))
        VDUM = stk.enter_context(nc.sbuf_tensor("vdum_s", [128, 32], F32))

        def vfence(region_ap):
            return nc.vector.tensor_copy(VDUM[:, :], region_ap)
        dsh_dram = nc.dram_tensor("dsh_dram", [128, NKC], F32, kind="Internal").ap()

        cL0, cS0 = fresh_dma_sems()
        with nc.Block() as b0:
            @b0.sync
            def _(s):
                for dst_, src_ in ((cosE, cosI), (sinE, sinI), (wgt, wgtI), (onesm, onesmI),
                                   (ident, identI), (dshS, dsI)):
                    s.dma_start(out=dst_[:, :], in_=src_[:, :]).then_inc(cL0.h, 16)
                s.wait_ge(cL0.h, 96)

            @b0.vector
            def _(v):
                v.wait_ge(cL0.h, 96)
                v.tensor_scalar_mul(dshS[:, :], dshS[:, :], 0.5).then_inc(sV.h, 1)
                v.wait_ge(sV.h, sV.n + 1)

            @b0.scalar
            def _(sc):
                sc.wait_ge(sV.h, sV.n + 1)
                sc.dma_start(out=dsh_dram[:, :], in_=dshS[:, :]).then_inc(cS0.h, 16)
                sc.wait_ge(cS0.h, 16)

            @b0.gpsimd
            def _(g):
                g.memset(onebf[:, :], 1.0).then_inc(sP.h, 1)
        cL0.n += 96
        cS0.n += 16
        sV.n += 1
        sP.n += 1
        nc.all_engine_barrier()

        def prolog_slab(src, n_o, dst, src_f32, slab_base=0, slab_step=1):
            dL0, dS0 = fresh_dma_sems()
            dL1, dS1 = fresh_dma_sems()
            dLp, dSp = [dL0, dL1], [dS0, dS1]
            sdt = F32 if src_f32 else I16
            u = uid()
            W = D
            W2 = 1536
            with contextlib.ExitStack() as ps:
                NI = [ps.enter_context(nc.sbuf_tensor(f"pni{h}_{u}", [128, W], sdt)) for h in range(2)]
                CB = [ps.enter_context(nc.sbuf_tensor(f"pcb{h}_{u}", [128, W], BF16)) for h in range(2)]
                WT = [ps.enter_context(nc.sbuf_tensor(f"pwt{h}_{u}", [128, W], BF16)) for h in range(2)]
                PT = [ps.enter_context(nc.psum_tensor(f"ppt{h}_{u}", [128, 2048], BF16)) for h in range(4)]
                bP0, bT0, bA0 = sP.n, sT.n, sA.n
                nc.gpsimd.sem_inc(sP.h, 2)
                nc.tensor.sem_inc(sT.h, 4)
                nc.scalar.sem_inc(dS0.h, 16)
                nc.scalar.sem_inc(dS1.h, 16)
                dS0.n += 16
                dS1.n += 16
                gLd2 = [Rg(ps, nc.gpsimd, f"gLd2{h}_{u}", unroll) for h in range(2)]
                aLd2 = [Rg(ps, nc.scalar, f"aLd2{h}_{u}", unroll) for h in range(2)]
                aTc = Rg(ps, nc.scalar, f"aTc{u}", unroll)
                aSs2 = [Rg(ps, nc.scalar, f"aSs2{h}_{u}", unroll) for h in range(2)]
                ySp = Rg(ps, nc.sync, f"ySp{u}", unroll)
                ySp2 = Rg(ps, nc.sync, f"ySp2{u}", unroll)
                ySo = Rg(ps, nc.sync, f"ySo{u}", unroll)
                gTt = Rg(ps, nc.gpsimd, f"gTt{u}", unroll)
                tPc = Rg(ps, nc.tensor, f"tPc{u}", unroll)
                tAc = Rg(ps, nc.tensor, f"tAc{u}", unroll)
                tAe = Rg(ps, nc.tensor, f"tAe{u}", unroll)
                aTt = Rg(ps, nc.scalar, f"aTt{u}", unroll)
                aAo = Rg(ps, nc.scalar, f"aAo{u}", unroll)
                aOf = Rg(ps, nc.scalar, f"aOf{u}", unroll)

                def front(o, h, peel):
                    if not peel:
                        nc.sync.wait_ge(sP.h, ySp.to(bP0 + 2 + o - 1))
                        nc.sync.wait_ge(sA.h, ySp2.to(bA0 + 3 * (o - 2) + 1))
                    with free_temps(nc, mybir.EngineType.SP):
                        nc.sync.dma_start(out=NI[h][:, :],
                                          in_=bass.AP(src.tensor, ySo.to(o * 128 * W),
                                                      [[W, 128], [1, W]])).then_inc(dLp[h].h, 16)
                    nc.gpsimd.wait_ge(dLp[h].h, gLd2[h].to(16 * (o // 2) + 16))
                    if not peel:
                        nc.gpsimd.wait_ge(sT.h, gTt.to(bT0 + 4 + 2 * o - 2))
                    nc.gpsimd.tensor_copy(CB[h][:, 0:W2], NI[h][:, 0:W2]).then_inc(sP.h, 1)
                    nc.scalar.wait_ge(dLp[h].h, aLd2[h].to(16 * (o // 2) + 16))
                    if not peel:
                        nc.scalar.wait_ge(sT.h, aTc.to(bT0 + 4 + 2 * o - 2))
                    i = nc.scalar.copy(CB[h][:, W2:W], NI[h][:, W2:W])
                    i.then_inc(sA.h, 1)
                    if peel:
                        nc.scalar.sem_inc(sA.h, 2)
                    nc.tensor.wait_ge(sP.h, tPc.to(bP0 + 2 + o + 1))
                    nc.tensor.wait_ge(sA.h, tAc.to(bA0 + 3 * o + 1))
                    if not peel:
                        nc.tensor.wait_ge(sA.h, tAe.to(bA0 + 3 * (o - 2) + 9))
                    pt0, pt1 = PT[2 * h], PT[2 * h + 1]
                    for k in range(NKC):
                        pt = pt0 if k < 16 else pt1
                        i = nc.tensor.transpose(pt[:, (k % 16) * 128:(k % 16 + 1) * 128],
                                                CB[h][:, k * 128:(k + 1) * 128], ident[:, :])
                        if k in (15, 31):
                            i.then_inc(sT.h, 1)

                def back(o, h):
                    pt0, pt1 = PT[2 * h], PT[2 * h + 1]
                    nc.scalar.wait_ge(dSp[h].h, aSs2[h].to(16 * (o // 2) + 16))
                    nc.scalar.wait_ge(sT.h, aTt.to(bT0 + 4 + 2 * o + 1))
                    nc.scalar.copy(WT[h][:, 0:2048], pt0[:, :]).then_inc(sA.h, 1)
                    nc.scalar.wait_ge(sT.h, aTt.to(bT0 + 4 + 2 * o + 2))
                    nc.scalar.copy(WT[h][:, 2048:4096], pt1[:, :]).then_inc(sA.h, 1)
                    nc.scalar.wait_ge(sA.h, aAo.to(bA0 + 3 * o + 9))
                    with free_temps(nc, mybir.EngineType.Activation):
                        nc.scalar.dma_start(out=bass.AP(dst.tensor,
                                                        aOf.to((slab_base + slab_step * o) * 128 * W),
                                                        [[W, 128], [1, W]]),
                                            in_=WT[h][:, :]).then_inc(dSp[h].h, 16)

                front(0, 0, True)
                front(1, 1, True)

                def body(jj):
                    j = jj + 1
                    for h in range(2):
                        o = 2 * j + h
                        front(o, h, False)
                        back(o - 2, h)

                drive(nc, n_o // 2 - 1, unroll,
                      [(ySp, 2), (ySp2, 6), (ySo, 2 * 128 * W), (gLd2[0], 16), (gLd2[1], 16),
                       (aLd2[0], 16), (aLd2[1], 16), (aTc, 4), (gTt, 4), (tPc, 2),
                       (tAc, 6), (tAe, 6), (aSs2[0], 16), (aSs2[1], 16),
                       (aTt, 4), (aAo, 6), (aOf, slab_step * 2 * 128 * W)],
                      body, f"ps{u}")
                nc.scalar.sem_inc(sA.h, 1)
                back(n_o - 2, 0)
                nc.scalar.sem_inc(sA.h, 1)
                back(n_o - 1, 1)
                nit = n_o // 2
                dL0.n += 16 * nit
                dL1.n += 16 * nit
                sP.n += 2 + 2 * nit
                sT.n += 4 + 4 * nit
                sA.n += 3 * n_o + 6
                dS0.n += 16 * nit
                dS1.n += 16 * nit
                nc.scalar.wait_ge(dS0.h, dS0.n)
                nc.scalar.wait_ge(dS1.h, dS1.n)
                nc.sync.wait_ge(dL0.h, dL0.n)
                nc.sync.wait_ge(dL1.h, dL1.n)
            nc.all_engine_barrier()

        def prolog_rowmajor(src, n_or, n_kc, dst, src_f32):
            dL0, dS = fresh_dma_sems()
            dL1, _u2 = fresh_dma_sems()
            dLp = [dL0, dL1]
            sdt = F32 if src_f32 else I16
            W = n_kc * 128
            M = n_or * 128
            nsub = (n_kc + 15) // 16
            u = uid()
            with contextlib.ExitStack() as ps:
                NI = [ps.enter_context(nc.sbuf_tensor(f"rni{h}_{u}", [128, W], sdt)) for h in range(2)]
                CB = [ps.enter_context(nc.sbuf_tensor(f"rcb{h}_{u}", [128, W], BF16)) for h in range(2)]
                DT = ps.enter_context(nc.sbuf_tensor(f"rdt_{u}", [128, W], BF16))
                PT = [ps.enter_context(nc.psum_tensor(f"rpt{h}_{u}", [128, 2048], BF16)) for h in range(2)]
                bV0, bT0, bA0, bS0 = sV.n, sT.n, sA.n, dS.n
                nc.vector.sem_inc(sV.h, 2)
                nc.tensor.sem_inc(sT.h, 2 * nsub)
                nc.scalar.sem_inc(sA.h, 2 * nsub)
                nc.scalar.sem_inc(dS.h, 16)
                dS.n += 16
                bS1 = dS.n
                ySp = Rg(ps, nc.sync, f"ySp{u}", unroll)
                ySo = Rg(ps, nc.sync, f"ySo{u}", unroll)
                gLd2 = [Rg(ps, nc.vector, f"gLd2{h}_{u}", unroll) for h in range(2)]
                gTt = Rg(ps, nc.vector, f"gTt{u}", unroll)
                tPc = Rg(ps, nc.tensor, f"tPc{u}", unroll)
                tAe = Rg(ps, nc.tensor, f"tAe{u}", unroll)
                aSs = Rg(ps, nc.scalar, f"aSs{u}", unroll)
                aTt = Rg(ps, nc.scalar, f"aTt{u}", unroll)
                aAo = Rg(ps, nc.scalar, f"aAo{u}", unroll)
                aOf = Rg(ps, nc.scalar, f"aOf{u}", unroll)

                def body(j):
                    for h in range(2):
                        o = 2 * j + h
                        nc.sync.wait_ge(sV.h, ySp.to(bV0 + 2 + o - 1))
                        with free_temps(nc, mybir.EngineType.SP):
                            nc.sync.dma_start(out=NI[h][:, :],
                                              in_=bass.AP(src.tensor, ySo.to(o * 128 * W),
                                                          [[W, 128], [1, W]])).then_inc(dLp[h].h, 16)
                        nc.vector.wait_ge(dLp[h].h, gLd2[h].to(16 * j + 16))
                        nc.vector.wait_ge(sT.h, gTt.to(bT0 + 2 * nsub + nsub * (o - 1)))
                        nc.vector.tensor_copy(CB[h][:, :], NI[h][:, :])
                        vfence(CB[h][:, W - 32:W]).then_inc(sV.h, 1)
                        for sb in range(nsub):
                            k0, k1 = sb * 16, min((sb + 1) * 16, n_kc)
                            pt = PT[sb % 2]
                            if sb < 2:
                                nc.tensor.wait_ge(sV.h, tPc.to(bV0 + 2 + o + 1))
                            nc.tensor.wait_ge(sA.h, tAe.to(bA0 + 2 * nsub + nsub * o + max(sb - 1, 0)))
                            for k in range(k0, k1):
                                i = nc.tensor.transpose(pt[:, (k - k0) * 128:(k - k0 + 1) * 128],
                                                        CB[h][:, k * 128:(k + 1) * 128], ident[:, :])
                            i.then_inc(sT.h, 1)
                            if sb == 0:
                                nc.scalar.wait_ge(dS.h, aSs.to(bS1 + 16 * o))
                            nc.scalar.wait_ge(sT.h, aTt.to(bT0 + 2 * nsub + nsub * o + sb + 1))
                            nc.scalar.copy(DT[:, k0 * 128:k1 * 128],
                                           pt[:, 0:(k1 - k0) * 128]).then_inc(sA.h, 1)
                        nc.scalar.wait_ge(sA.h, aAo.to(bA0 + 2 * nsub + nsub * o + nsub))
                        with free_temps(nc, mybir.EngineType.Activation):
                            nc.scalar.dma_start(
                                out=bass.AP(dst.tensor, aOf.to(o * 128),
                                            [[M, 128], [128 * M, n_kc], [1, 128]]),
                                in_=DT[:, :].rearrange("p (k c) -> p k c", c=128),
                            ).then_inc(dS.h, 16)

                drive(nc, n_or // 2, unroll,
                      [(ySp, 2), (ySo, 2 * 128 * W), (gLd2[0], 16), (gLd2[1], 16),
                       (gTt, 2 * nsub), (tPc, 2), (tAe, 2 * nsub), (aSs, 32),
                       (aTt, 2 * nsub), (aAo, 2 * nsub), (aOf, 2 * 128)],
                      body, f"pr{u}")
                nit = n_or // 2
                dL0.n += 16 * nit
                dL1.n += 16 * nit
                sV.n += 2 + 2 * nit
                sT.n += 2 * nsub + 2 * nsub * nit
                sA.n += 2 * nsub + 2 * nsub * nit
                dS.n += 32 * nit
                nc.scalar.wait_ge(dS.h, dS.n)
                nc.sync.wait_ge(dL0.h, dL0.n)
                nc.sync.wait_ge(dL1.h, dL1.n)
            nc.all_engine_barrier()

        prolog_slab(gw, NOC, guT_sc, False, slab_base=0, slab_step=2)
        prolog_slab(uw, NOC, guT_sc, False, slab_base=1, slab_step=2)
        prolog_slab(wq, NH, qT_sc, True)
        prolog_rowmajor(dw, NKC, NOC, dT_sc, False)
        prolog_rowmajor(wo, NKC, NKC, oT_sc, True)

        XN = stk.enter_context(nc.sbuf_tensor("XN", [128, NKC * NEXT], BF16))

        def emit_ln(src, nt, xn_stride):
            dL0, _u1 = fresh_dma_sems()
            dL1, _u2 = fresh_dma_sems()
            dLp = [dL0, dL1]
            srcv = src.rearrange("(g p) n -> p g n", p=128)
            ntt = nt // 512
            u = uid()
            rL, rT, rA, rV = 16 * 8, 8, 7, 71
            with contextlib.ExitStack() as ps:
                XF = [ps.enter_context(nc.sbuf_tensor(f"lxf{h}_{u}", [128, 4096], F32)) for h in range(2)]
                XB = ps.enter_context(nc.sbuf_tensor(f"lxb_{u}", [128, 4096], BF16))
                XQ = ps.enter_context(nc.sbuf_tensor(f"lxq_{u}", [128, 4096], BF16))
                MB = ps.enter_context(nc.sbuf_tensor(f"lmb_{u}", [128, 512], F32))
                T2 = ps.enter_context(nc.sbuf_tensor(f"lt2_{u}", [128, 512], F32))
                VV = ps.enter_context(nc.sbuf_tensor(f"lvv_{u}", [128, 512], F32))
                RS = ps.enter_context(nc.sbuf_tensor(f"lrs_{u}", [128, 512], F32))
                DDb = [ps.enter_context(nc.sbuf_tensor(f"ldd{h}_{u}", [128, 512], F32)) for h in range(2)]
                PM = ps.enter_context(nc.psum_tensor(f"lpm_{u}", [128, 512], F32))
                PQ = ps.enter_context(nc.psum_tensor(f"lpq_{u}", [128, 512], F32))
                bT0, bA0, bV0 = sT.n, sA.n, sV.n
                for tt in range(ntt):
                    cv = ca = ct = 0
                    for cg in range(4):
                        par = cg % 2
                        nc.sync.wait_ge(sV.h, bV0 + rV * tt)
                        if cg >= 2:
                            nc.sync.wait_ge(sV.h, bV0 + rV * tt + cg - 1)
                            nc.sync.wait_ge(sA.h, bA0 + rA * tt + cg - 1)
                        nc.sync.dma_start(
                            out=XF[par][:, :].rearrange("p (g n) -> p g n", n=512),
                            in_=srcv[:, cg * 8:(cg + 1) * 8, tt * 512:tt * 512 + 512],
                        ).then_inc(dLp[par].h, 16)
                        ldv = 64 * tt + 16 * (cg // 2 + 1)
                        nc.vector.wait_ge(dLp[par].h, ldv)
                        nc.vector.wait_ge(sT.h, bT0 + rT * tt + 2 * cg)
                        nc.vector.tensor_copy(XB[:, :], XF[par][:, :]).then_inc(sV.h, 1); cv += 1
                        nc.scalar.wait_ge(dLp[par].h, ldv)
                        nc.scalar.wait_ge(sT.h, bT0 + rT * tt + 2 * cg)
                        nc.scalar.activation(XQ[:, :], XF[par][:, :], AF.Square).then_inc(sA.h, 1); ca += 1
                        nc.tensor.wait_ge(sV.h, bV0 + rV * tt + cv)
                        for c in range(8):
                            i = nc.tensor.matmul(PM[:, :], onebf[:, :], XB[:, c * 512:(c + 1) * 512],
                                                 start=(cg == 0 and c == 0), stop=(cg == 3 and c == 7),
                                                 skip_group_check=True)
                        i.then_inc(sT.h, 1); ct += 1
                        nc.tensor.wait_ge(sA.h, bA0 + rA * tt + ca)
                        for c in range(8):
                            i = nc.tensor.matmul(PQ[:, :], onebf[:, :], XQ[:, c * 512:(c + 1) * 512],
                                                 start=(cg == 0 and c == 0), stop=(cg == 3 and c == 7),
                                                 skip_group_check=True)
                        i.then_inc(sT.h, 1); ct += 1
                    nc.vector.wait_ge(sT.h, bT0 + rT * tt + 8)
                    nc.vector.tensor_scalar_mul(MB[:, :], PM[:, :], 1.0 / D).then_inc(sV.h, 1); cv += 1
                    nc.scalar.wait_ge(sT.h, bT0 + rT * tt + 8)
                    nc.scalar.mul(T2[:, :], PQ[:, :], 1.0 / D).then_inc(sA.h, 1); ca += 1
                    nc.vector.wait_ge(sV.h, bV0 + rV * tt + cv)
                    nc.vector.scalar_tensor_tensor(VV[:, :], MB[:, :], -1.0, MB[:, :],
                                                   ALU.mult, ALU.mult).then_inc(sV.h, 1); cv += 1
                    nc.vector.wait_ge(sV.h, bV0 + rV * tt + cv)
                    nc.vector.wait_ge(sA.h, bA0 + rA * tt + ca)
                    nc.vector.scalar_tensor_tensor(VV[:, :], VV[:, :], EPS, T2[:, :],
                                                   ALU.add, ALU.add).then_inc(sV.h, 1); cv += 1
                    nc.scalar.wait_ge(sV.h, bV0 + rV * tt + cv)
                    nc.scalar.activation(RS[:, :], VV[:, :], AF.Ln).then_inc(sA.h, 1); ca += 1
                    nc.scalar.wait_ge(sA.h, bA0 + rA * tt + ca)
                    nc.scalar.activation(RS[:, :], RS[:, :], AF.Exp, scale=-0.5).then_inc(sA.h, 1); ca += 1
                    assert ca == rA and ct == rT
                    for cg in range(4):
                        par = cg % 2
                        nc.sync.wait_ge(sV.h, bV0 + rV * tt + 7 + 16 * cg)
                        nc.sync.dma_start(
                            out=XF[par][:, :].rearrange("p (g n) -> p g n", n=512),
                            in_=srcv[:, cg * 8:(cg + 1) * 8, tt * 512:tt * 512 + 512],
                        ).then_inc(dLp[par].h, 16)
                        nc.vector.wait_ge(dLp[par].h, 64 * tt + 32 + 16 * (cg // 2 + 1))
                        nc.vector.wait_ge(sA.h, bA0 + rA * tt + rA)
                        for c2 in range(4):
                            nc.vector.wait_ge(sV.h, bV0 + rV * tt + cv)
                            for c, dp in ((2 * c2, 0), (2 * c2 + 1, 1)):
                                nc.vector.tensor_sub(DDb[dp][:, :], XF[par][:, c * 512:(c + 1) * 512],
                                                     MB[:, :]).then_inc(sV.h, 1); cv += 1
                            nc.vector.wait_ge(sV.h, bV0 + rV * tt + cv)
                            for c, dp in ((2 * c2, 0), (2 * c2 + 1, 1)):
                                cc = cg * 8 + c
                                nc.vector.tensor_mul(
                                    XN[:, cc * xn_stride + tt * 512:cc * xn_stride + tt * 512 + 512],
                                    DDb[dp][:, :], RS[:, :]).then_inc(sV.h, 1); cv += 1
                    assert cv == rV, (cv, rV)
                dL0.n += 64 * ntt
                dL1.n += 64 * ntt
                sT.n += rT * ntt
                sA.n += rA * ntt
                sV.n += rV * ntt
                nc.sync.wait_ge(dL0.h, dL0.n)
                nc.sync.wait_ge(dL1.h, dL1.n)
            nc.all_engine_barrier()

        def emit_sweep1(nt, xn_stride):
            ntt = nt // 512
            dL0, dS0 = fresh_dma_sems()
            dL1, dS1 = fresh_dma_sems()
            dLp, dSp = [dL0, dL1], [dS0, dS1]
            u = uid()
            with contextlib.ExitStack() as ps:
                GUB = [ps.enter_context(nc.sbuf_tensor(f"s1gu{h}_{u}", [128, 2 * D], BF16)) for h in range(2)]
                GUC = [ps.enter_context(nc.sbuf_tensor(f"s1gc{h}_{u}", [128, 2], F32)) for h in range(2)]
                SG = ps.enter_context(nc.sbuf_tensor(f"s1sg_{u}", [128, ntt * 512], BF16))
                HH = [ps.enter_context(nc.sbuf_tensor(f"s1h{h}_{u}", [128, ntt * 512], BF16)) for h in range(2)]
                PG = [ps.enter_context(nc.psum_tensor(f"s1pg{t}_{u}", [128, 512], F32)) for t in range(ntt)]
                PU = [ps.enter_context(nc.psum_tensor(f"s1pu{t}_{u}", [128, 512], F32)) for t in range(ntt)]
                bT0, bA0, bV0 = sT.n, sA.n, sV.n
                nc.tensor.sem_inc(sT.h, 4 * ntt)
                nc.scalar.sem_inc(sA.h, 2 * ntt)
                nc.vector.sem_inc(sV.h, 2 * ntt)
                nc.scalar.sem_inc(dS0.h, 16)
                nc.scalar.sem_inc(dS1.h, 16)
                dS0.n += 16
                dS1.n += 16
                tLd2 = [Rg(ps, nc.tensor, f"tLd2{h}_{u}", unroll) for h in range(2)]
                aLd2 = [Rg(ps, nc.scalar, f"aLd2{h}_{u}", unroll) for h in range(2)]
                vLd2 = [Rg(ps, nc.vector, f"vLd2{h}_{u}", unroll) for h in range(2)]
                vSs2 = [Rg(ps, nc.vector, f"vSs2{h}_{u}", unroll) for h in range(2)]
                yTm = Rg(ps, nc.sync, f"yTm{u}", unroll)
                ySv = Rg(ps, nc.sync, f"ySv{u}", unroll)
                yOf = Rg(ps, nc.sync, f"yOf{u}", unroll)
                yGc = Rg(ps, nc.sync, f"yGc{u}", unroll)
                tAs = Rg(ps, nc.tensor, f"tAs{u}", unroll)
                tVh = Rg(ps, nc.tensor, f"tVh{u}", unroll)
                aTg = Rg(ps, nc.scalar, f"aTg{u}", unroll)
                aVh = Rg(ps, nc.scalar, f"aVh{u}", unroll)
                aHo = Rg(ps, nc.scalar, f"aHo{u}", unroll)
                vTu = Rg(ps, nc.vector, f"vTu{u}", unroll)
                vAs = Rg(ps, nc.vector, f"vAs{u}", unroll)
                vL2 = Rg(ps, nc.vector, f"vL2{u}", unroll)

                def body(j):
                    for h in range(2):
                        o = 2 * j + h
                        nc.sync.wait_ge(sT.h, yTm.to(bT0 + 4 * ntt + 2 * ntt * (o - 1)))
                        nc.sync.wait_ge(sV.h, ySv.to(bV0 + 2 * ntt + ntt * (o - 1)))
                        with free_temps(nc, mybir.EngineType.SP):
                            nc.sync.dma_start(
                                out=GUB[h][:, :].rearrange("p (g n) -> p g n", n=D),
                                in_=bass.AP(guT_sc.tensor, yOf.to(2 * o * 128 * D),
                                            [[D, 128], [128 * D, 2], [1, D]])).then_inc(dLp[h].h, 16)
                        with free_temps(nc, mybir.EngineType.SP):
                            nc.sync.dma_start(
                                out=GUC[h][:, :],
                                in_=bass.AP(gusI.tensor, yGc.to(2 * o),
                                            [[2 * NOC, 128], [1, 2]])).then_inc(dLp[h].h, 16)
                        nc.tensor.wait_ge(dLp[h].h, tLd2[h].to(32 * j + 32))
                        nc.tensor.wait_ge(sA.h, tAs.to(bA0 + 2 * ntt + ntt * o))
                        for t in range(ntt):
                            for k in range(NKC):
                                i = nc.tensor.matmul(
                                    PG[t][:, :], GUB[h][:, k * 128:(k + 1) * 128],
                                    XN[:, k * xn_stride + t * 512:k * xn_stride + t * 512 + 512],
                                    start=(k == 0), stop=(k == NKC - 1), skip_group_check=True)
                            i.then_inc(sT.h, 1)
                        nc.tensor.wait_ge(sV.h, tVh.to(bV0 + 2 * ntt + ntt * o))
                        for t in range(ntt):
                            for k in range(NKC):
                                i = nc.tensor.matmul(
                                    PU[t][:, :], GUB[h][:, D + k * 128:D + (k + 1) * 128],
                                    XN[:, k * xn_stride + t * 512:k * xn_stride + t * 512 + 512],
                                    start=(k == 0), stop=(k == NKC - 1), skip_group_check=True)
                            i.then_inc(sT.h, 1)
                        nc.scalar.wait_ge(sT.h, aTg.to(bT0 + 4 * ntt + 2 * ntt * o + ntt))
                        nc.scalar.wait_ge(sV.h, aVh.to(bV0 + 2 * ntt + ntt * o))
                        nc.scalar.wait_ge(dLp[h].h, aLd2[h].to(32 * j + 32))
                        for t in range(ntt):
                            nc.scalar.activation(SG[:, t * 512:(t + 1) * 512], PG[t][:, :], AF.Silu,
                                                 scale=GUC[h][:, 0:1]).then_inc(sA.h, 1)
                        nc.vector.wait_ge(sT.h, vTu.to(bT0 + 4 * ntt + 2 * ntt * o + 2 * ntt))
                        nc.vector.wait_ge(sA.h, vAs.to(bA0 + 2 * ntt + ntt * o + ntt))
                        nc.vector.wait_ge(dSp[h].h, vSs2[h].to(16 * j))
                        nc.vector.wait_ge(dLp[h].h, vLd2[h].to(32 * j + 32))
                        for t in range(ntt):
                            nc.vector.scalar_tensor_tensor(
                                HH[h][:, t * 512:(t + 1) * 512], PU[t][:, :], GUC[h][:, 1:2],
                                SG[:, t * 512:(t + 1) * 512], ALU.mult, ALU.mult).then_inc(sV.h, 1)
                        nc.scalar.wait_ge(sV.h, aVh.to(bV0 + 2 * ntt + ntt * o + ntt))
                        with free_temps(nc, mybir.EngineType.Activation):
                            nc.scalar.dma_start(out=bass.AP(hsc.tensor, aHo.to(o * 128 * NEXT),
                                                            [[NEXT, 128], [1, nt]]),
                                                in_=HH[h][:, :]).then_inc(dSp[h].h, 16)

                drive(nc, NOC // 2, unroll,
                      [(yTm, 4 * ntt), (ySv, 2 * ntt), (yOf, 4 * 128 * D), (yGc, 4),
                       (tLd2[0], 32), (tLd2[1], 32), (tAs, 2 * ntt), (tVh, 2 * ntt),
                       (aTg, 4 * ntt), (aVh, 2 * ntt), (aLd2[0], 32), (aLd2[1], 32),
                       (aHo, 2 * 128 * NEXT), (vTu, 4 * ntt), (vAs, 2 * ntt),
                       (vSs2[0], 16), (vSs2[1], 16), (vLd2[0], 32), (vLd2[1], 32)],
                      body, f"s1_{u}")
                nit = NOC // 2
                dL0.n += 32 * nit
                dL1.n += 32 * nit
                sT.n += 4 * ntt + 4 * ntt * nit
                sA.n += 2 * ntt + 2 * ntt * nit
                sV.n += 2 * ntt + 2 * ntt * nit
                dS0.n += 16 * nit
                dS1.n += 16 * nit
                nc.scalar.wait_ge(dS0.h, dS0.n)
                nc.scalar.wait_ge(dS1.h, dS1.n)
                nc.sync.wait_ge(dL0.h, dL0.n)
                nc.sync.wait_ge(dL1.h, dL1.n)
            nc.all_engine_barrier()

        def emit_sweep2(n_k, stat_sc, mov_sc, mov_stride, nt, res_src, res_tot, res_off,
                        dst, dst_tot, dst_off, scale_dram):
            ntt = nt // 512
            SL = 4
            ns = n_k // SL
            assert n_k % SL == 0 and ns % 2 == 0 and ns >= 4
            dK0, dS = fresh_dma_sems()
            dK1, dDS = fresh_dma_sems()
            dP, _unused = fresh_dma_sems()
            dKp = [dK0, dK1]
            u = uid()
            with contextlib.ExitStack() as ps:
                DK = [ps.enter_context(nc.sbuf_tensor(f"s2d{h}_{u}", [128, SL * 1024], BF16)) for h in range(2)]
                HK = [ps.enter_context(nc.sbuf_tensor(f"s2h{h}_{u}", [128, SL * 512], BF16)) for h in range(2)]
                XR = ps.enter_context(nc.sbuf_tensor(f"s2xr_{u}", [128, 8 * 512], F32))
                DSC = ps.enter_context(nc.sbuf_tensor(f"s2ds_{u}", [128, 8], F32))
                XO = ps.enter_context(nc.sbuf_tensor(f"s2xo_{u}", [128, 8 * 512], F32))
                PSb = [ps.enter_context(nc.psum_tensor(f"s2p{oc}_{u}", [128, 512], F32)) for oc in range(8)]
                bT0, bV0, bS0 = sT.n, sV.n, dS.n
                nc.tensor.sem_inc(sT.h, 2)
                nc.vector.sem_inc(sV.h, 8)
                nc.scalar.sem_inc(dS.h, 16)
                tLd2 = [Rg(ps, nc.tensor, f"tLd2{h}_{u}", unroll) for h in range(2)]
                yTb = Rg(ps, nc.sync, f"yTb{u}", unroll)
                ySv = Rg(ps, nc.sync, f"ySv{u}", unroll)
                yDS = Rg(ps, nc.sync, f"yDS{u}", unroll)
                yVx = Rg(ps, nc.gpsimd, f"yVx{u}", unroll)
                yDK = Rg(ps, nc.sync, f"yDK{u}", unroll)
                yHK = Rg(ps, nc.sync, f"yHK{u}", unroll)
                yXR = Rg(ps, nc.gpsimd, f"yXR{u}", unroll)
                tVp = Rg(ps, nc.tensor, f"tVp{u}", unroll)
                vTm = Rg(ps, nc.vector, f"vTm{u}", unroll)
                vLd = Rg(ps, nc.vector, f"vLd{u}", unroll)
                vDs = Rg(ps, nc.vector, f"vDs{u}", unroll)
                vSx = Rg(ps, nc.vector, f"vSx{u}", unroll)
                aVe = Rg(ps, nc.scalar, f"aVe{u}", unroll)
                aXO = Rg(ps, nc.scalar, f"aXO{u}", unroll)

                def kslab(par, it, s, s_first, s_last, og_i, tt_i):
                    nc.sync.wait_ge(sT.h, yTb.to(bT0 + 2 + ns * it + s - 1))
                    with free_temps(nc, mybir.EngineType.SP):
                        nc.sync.dma_start(
                            out=DK[par][:, :].rearrange("p (g n) -> p g n", n=1024),
                            in_=bass.AP(stat_sc.tensor, yDK.to(s * SL * 128 * D + og_i * 1024),
                                        [[D, 128], [128 * D, SL], [1, 1024]])).then_inc(dKp[par].h, 16)
                    with free_temps(nc, mybir.EngineType.SP):
                        nc.sync.dma_start(
                            out=HK[par][:, :].rearrange("p (g n) -> p g n", n=512),
                            in_=bass.AP(mov_sc.tensor,
                                        yHK.to(s * SL * 128 * mov_stride + tt_i * 512),
                                        [[mov_stride, 128], [128 * mov_stride, SL], [1, 512]])
                        ).then_inc(dKp[par].h, 16)
                    nc.tensor.wait_ge(dKp[par].h, tLd2[par].to(16 * ns * it + 32 * (s // 2) + 32))
                    if s_first:
                        nc.tensor.wait_ge(sV.h, tVp.to(bV0 + 8 + 8 * it))
                    for ks in range(SL):
                        for oc in range(8):
                            i = nc.tensor.matmul(
                                PSb[oc][:, :],
                                DK[par][:, ks * 1024 + oc * 128:ks * 1024 + (oc + 1) * 128],
                                HK[par][:, ks * 512:(ks + 1) * 512],
                                start=(s_first and ks == 0), stop=(s_last and ks == SL - 1),
                                skip_group_check=True)
                    i.then_inc(sT.h, 1)

                def body_tt(og_i, tt_i):
                    it = og_i * ntt + tt_i
                    nc.sync.wait_ge(sV.h, ySv.to(bV0 + 8 + 8 * it))
                    with free_temps(nc, mybir.EngineType.SP):
                        nc.sync.dma_start(out=DSC[:, :],
                                          in_=bass.AP(scale_dram.tensor, yDS.to(og_i * 8),
                                                      [[NKC, 128], [1, 8]])).then_inc(dDS.h, 16)
                    nc.gpsimd.wait_ge(sV.h, yVx.to(bV0 + 8 + 8 * it))
                    with free_temps(nc, mybir.EngineType.Pool):
                        nc.gpsimd.dma_start(
                            out=XR[:, :].rearrange("p (g n) -> p g n", n=512),
                            in_=bass.AP(res_src.tensor,
                                        yXR.to(og_i * 1024 * res_tot + res_off + tt_i * 512),
                                        [[res_tot, 128], [128 * res_tot, 8], [1, 512]]),
                        ).then_inc(dP.h, 16)
                    kslab(0, it, 0, True, False, og_i, tt_i)
                    kslab(1, it, 1, False, False, og_i, tt_i)

                    def kbody(m):
                        kslab(0, it, 2 * m + 2, False, False, og_i, tt_i)
                        kslab(1, it, 2 * m + 3, False, False, og_i, tt_i)

                    drive(nc, (ns - 4) // 2, unroll,
                          [(yTb, 2), (yDK, 2 * SL * 128 * D), (yHK, 2 * SL * 128 * mov_stride),
                           (tLd2[0], 32), (tLd2[1], 32)],
                          kbody, f"s2k{u}_{it if unroll else 0}")
                    kslab(0, it, ns - 2, False, False, og_i, tt_i)
                    kslab(1, it, ns - 1, False, True, og_i, tt_i)
                    nc.vector.wait_ge(sT.h, vTm.to(bT0 + 2 + ns * it + ns))
                    nc.vector.wait_ge(dP.h, vLd.to(16 * it + 16))
                    nc.vector.wait_ge(dDS.h, vDs.to(16 * it + 16))
                    nc.vector.wait_ge(dS.h, vSx.to(bS0 + 16 + 16 * it))
                    for oc in range(8):
                        nc.vector.scalar_tensor_tensor(
                            XO[:, oc * 512:(oc + 1) * 512], PSb[oc][:, :], DSC[:, oc:oc + 1],
                            XR[:, oc * 512:(oc + 1) * 512],
                            ALU.mult, ALU.add).then_inc(sV.h, 1)
                    nc.scalar.wait_ge(sV.h, aVe.to(bV0 + 8 + 8 * it + 8))
                    with free_temps(nc, mybir.EngineType.Activation):
                        nc.scalar.dma_start(
                            out=bass.AP(dst.tensor,
                                        aXO.to(og_i * 1024 * dst_tot + dst_off + tt_i * 512),
                                        [[dst_tot, 128], [128 * dst_tot, 8], [1, 512]]),
                            in_=XO[:, :].rearrange("p (g n) -> p g n", n=512),
                        ).then_inc(dS.h, 16)

                def body_og(og_i):
                    drive(nc, ntt, unroll,
                          [(yTb, ns), (ySv, 8), (yDS, 0), (yVx, 8), (yDK, 0), (yHK, 512),
                           (yXR, 512), (tLd2[0], 16 * ns), (tLd2[1], 16 * ns), (tVp, 8),
                           (vTm, ns), (vLd, 16), (vDs, 16), (vSx, 16), (aVe, 8), (aXO, 512)],
                          lambda tt_i: body_tt(og_i, tt_i), f"s2t{u}_{og_i if unroll else 0}")

                drive(nc, 4, unroll,
                      [(yTb, ns * ntt), (ySv, 8 * ntt), (yDS, 8), (yVx, 8 * ntt),
                       (yDK, 1024), (yHK, 0), (yXR, 1024 * res_tot),
                       (tLd2[0], 16 * ns * ntt), (tLd2[1], 16 * ns * ntt), (tVp, 8 * ntt),
                       (vTm, ns * ntt), (vLd, 16 * ntt), (vDs, 16 * ntt),
                       (vSx, 16 * ntt), (aVe, 8 * ntt), (aXO, 1024 * dst_tot)],
                      body_og, f"s2o{u}")
                nit = 4 * ntt
                dK0.n += 16 * ns * nit
                dK1.n += 16 * ns * nit
                dDS.n += 16 * nit
                dP.n += 16 * nit
                sT.n += 2 + ns * nit
                sV.n += 8 + 8 * nit
                dS.n += 16 + 16 * nit
                nc.scalar.wait_ge(dS.h, dS.n)
                nc.sync.wait_ge(dK0.h, dK0.n)
                nc.sync.wait_ge(dK1.h, dK1.n)
                nc.sync.wait_ge(dDS.h, dDS.n)
                nc.gpsimd.wait_ge(dP.h, dP.n)
            nc.all_engine_barrier()

        def emit_attn():
            u = uid()
            with contextlib.ExitStack() as ats:
                KRT = ats.enter_context(nc.sbuf_tensor(f"aKRT_{u}", [128, NEXT], BF16))
                VTM = ats.enter_context(nc.sbuf_tensor(f"aVTM_{u}", [128, 12 * 128], BF16))

                dL, dS = fresh_dma_sems()
                with contextlib.ExitStack() as ps:
                    WF = ps.enter_context(nc.sbuf_tensor(f"aWF_{u}", [128, D], F32))
                    WB = ps.enter_context(nc.sbuf_tensor(f"aWB_{u}", [128, D], BF16))
                    WTs = ps.enter_context(nc.sbuf_tensor(f"aWT_{u}", [128, D], BF16))
                    KF = ps.enter_context(nc.sbuf_tensor(f"aKF_{u}", [128, NEXT], F32))
                    VB = ps.enter_context(nc.sbuf_tensor(f"aVB_{u}", [128, NEXT], BF16))
                    R1 = ps.enter_context(nc.sbuf_tensor(f"aR1_{u}", [128, NEXT], F32))
                    R2 = ps.enter_context(nc.sbuf_tensor(f"aR2_{u}", [128, NEXT], F32))
                    PT = [ps.enter_context(nc.psum_tensor(f"aPT{h}_{u}", [128, 2048], BF16)) for h in range(2)]
                    PKV = ps.enter_context(nc.psum_tensor(f"aPKV_{u}", [128, 512], F32))

                    def kv_one(wmat, out_is_k):
                        bL, bT, bA, bV = dL.n, sT.n, sA.n, sV.n
                        with nc.Block() as bb:
                            @bb.sync
                            def _(s):
                                s.dma_start(out=WF[:, :], in_=wmat[:, :]).then_inc(dL.h, 16)

                            @bb.vector
                            def _(v):
                                v.wait_ge(dL.h, bL + 16)
                                v.tensor_copy(WB[:, :], WF[:, :])
                                vfence(WB[:, D - 32:D]).then_inc(sV.h, 1)

                            @bb.tensor
                            def _(t):
                                t.wait_ge(sV.h, bV + 1)
                                for k in range(NKC):
                                    i = t.transpose(PT[k // 16][:, (k % 16) * 128:(k % 16 + 1) * 128],
                                                    WB[:, k * 128:(k + 1) * 128], ident[:, :])
                                    if k in (15, 31):
                                        i.then_inc(sT.h, 1)
                                t.wait_ge(sA.h, bA + 2)
                                for t3 in range(3):
                                    if t3 > 0:
                                        t.wait_ge(sA.h, bA + 2 + t3)
                                    for k in range(NKC):
                                        i = t.matmul(PKV[:, :], WTs[:, k * 128:(k + 1) * 128],
                                                     XN[:, k * NEXT + t3 * 512:k * NEXT + t3 * 512 + 512],
                                                     start=(k == 0), stop=(k == NKC - 1),
                                                     skip_group_check=True)
                                    i.then_inc(sT.h, 1)

                            @bb.scalar
                            def _(sc):
                                sc.wait_ge(sT.h, bT + 1)
                                sc.copy(WTs[:, 0:2048], PT[0][:, :]).then_inc(sA.h, 1)
                                sc.wait_ge(sT.h, bT + 2)
                                sc.copy(WTs[:, 2048:4096], PT[1][:, :]).then_inc(sA.h, 1)
                                for t3 in range(3):
                                    sc.wait_ge(sT.h, bT + 2 + t3 + 1)
                                    if out_is_k:
                                        sc.copy(KF[:, t3 * 512:(t3 + 1) * 512], PKV[:, :]).then_inc(sA.h, 1)
                                    else:
                                        sc.copy(VB[:, t3 * 512:(t3 + 1) * 512], PKV[:, :]).then_inc(sA.h, 1)
                                sc.wait_ge(sA.h, bA + 5)
                        dL.n += 16
                        sV.n += 1
                        sT.n += 5
                        sA.n += 5
                        nc.all_engine_barrier()

                    kv_one(wk, True)
                    kv_one(wv, False)

                    bT, bA, bV = sT.n, sA.n, sV.n
                    with nc.Block() as b2:
                        @b2.vector
                        def _(v):
                            v.tensor_mul(R1[:, :], KF[:, :], cosE[:, :]).then_inc(sV.h, 1)
                            v.tensor_mul(R2[0:64, :], KF[64:128, :], sinE[64:128, :]).then_inc(sV.h, 1)
                            v.tensor_mul(R2[64:128, :], KF[0:64, :], sinE[0:64, :]).then_inc(sV.h, 1)
                            v.wait_ge(sV.h, bV + 3)
                            v.tensor_add(KRT[:, :], R1[:, :], R2[:, :]).then_inc(sV.h, 1)
                            v.wait_ge(sV.h, bV + 4)

                        @b2.tensor
                        def _(t):
                            for e in range(12):
                                i = t.transpose(PT[0][:, e * 128:(e + 1) * 128],
                                                VB[:, e * 128:(e + 1) * 128], ident[:, :])
                            i.then_inc(sT.h, 1)

                        @b2.scalar
                        def _(sc):
                            sc.wait_ge(sT.h, bT + 1)
                            sc.copy(VTM[:, 0:12 * 128], PT[0][:, 0:12 * 128]).then_inc(sA.h, 1)
                            sc.wait_ge(sA.h, bA + 1)
                    sV.n += 4
                    sT.n += 1
                    sA.n += 1
                    nc.sync.wait_ge(dL.h, dL.n)
                nc.all_engine_barrier()

                dL, dS = fresh_dma_sems()
                with contextlib.ExitStack() as ps:
                    WQB = ps.enter_context(nc.sbuf_tensor(f"hWQ_{u}", [128, D], BF16))
                    QF = ps.enter_context(nc.sbuf_tensor(f"hQF_{u}", [128, NOWN], F32))
                    QR = ps.enter_context(nc.sbuf_tensor(f"hQR_{u}", [128, NOWN], BF16))
                    R1 = ps.enter_context(nc.sbuf_tensor(f"hR1_{u}", [128, NOWN], F32))
                    R2 = ps.enter_context(nc.sbuf_tensor(f"hR2_{u}", [128, NOWN], F32))
                    SGT = [ps.enter_context(nc.sbuf_tensor(f"hSG{w}_{u}", [128, 2048], BF16)) for w in range(2)]
                    RC = ps.enter_context(nc.sbuf_tensor(f"hRC_{u}", [128, 512], F32))
                    RW = ps.enter_context(nc.sbuf_tensor(f"hRW_{u}", [128, 512], F32))
                    CBb = [ps.enter_context(nc.sbuf_tensor(f"hCB{w}_{u}", [128, 512], F32)) for w in range(2)]
                    ATH = ps.enter_context(nc.sbuf_tensor(f"hATH_{u}", [128, NOWN], BF16))
                    PQ2 = [ps.enter_context(nc.psum_tensor(f"hPQ{t2}_{u}", [128, 512], F32)) for t2 in range(2)]
                    PST = [ps.enter_context(nc.psum_tensor(f"hPS{w}_{u}", [128, 512], F32)) for w in range(2)]
                    PR = ps.enter_context(nc.psum_tensor(f"hPR_{u}", [128, 512], F32))
                    PW = ps.enter_context(nc.psum_tensor(f"hPW_{u}", [128, 512], F32))
                    bL0, bT0, bA0, bV0, bS0 = dL.n, sT.n, sA.n, sV.n, dS.n
                    rT, rA, rV = 32, 22, 23
                    nc.tensor.sem_inc(sT.h, rT)
                    nc.scalar.sem_inc(sA.h, rA)
                    nc.vector.sem_inc(sV.h, rV)
                    nc.scalar.sem_inc(dS.h, 16)
                    REC = [5, 8, 12, 16, 20]
                    RCW = [6, 9, 13, 17, 21]
                    CTR = [7, 10, 14, 18, 22]
                    yTq = Rg(ps, nc.sync, f"yTq{u}", unroll)
                    yWq = Rg(ps, nc.sync, f"yWq{u}", unroll)
                    tLd = Rg(ps, nc.tensor, f"tLdh{u}", unroll)
                    tAx = Rg(ps, nc.tensor, f"tAx{u}", unroll)
                    tVx = Rg(ps, nc.tensor, f"tVx{u}", unroll)
                    aTx = Rg(ps, nc.scalar, f"aTx{u}", unroll)
                    aVx = Rg(ps, nc.scalar, f"aVx{u}", unroll)
                    aAo = Rg(ps, nc.scalar, f"aAoh{u}", unroll)
                    vAx = Rg(ps, nc.vector, f"vAx{u}", unroll)
                    vTx = Rg(ps, nc.vector, f"vTxh{u}", unroll)
                    vVx = Rg(ps, nc.vector, f"vVx{u}", unroll)
                    vSs = Rg(ps, nc.vector, f"vSsh{u}", unroll)

                    def body(hh):
                        ctT = ctA = ctV = 0
                        nc.sync.wait_ge(sT.h, yTq.to(bT0 + rT * hh + 2))
                        with free_temps(nc, mybir.EngineType.SP):
                            nc.sync.dma_start(out=WQB[:, :],
                                              in_=bass.AP(qT_sc.tensor, yWq.to(hh * 128 * D),
                                                          [[D, 128], [1, D]])).then_inc(dL.h, 16)
                        nc.tensor.wait_ge(dL.h, tLd.to(bL0 + 16 * hh + 16))
                        for t2 in range(2):
                            nc.tensor.wait_ge(sA.h, tAx.to(bA0 + rA * hh + t2 + 1))
                            for k in range(NKC):
                                i = nc.tensor.matmul(
                                    PQ2[t2][:, :], WQB[:, k * 128:(k + 1) * 128],
                                    XN[:, k * NEXT + HALO + t2 * 512:k * NEXT + HALO + t2 * 512 + 512],
                                    start=(k == 0), stop=(k == NKC - 1), skip_group_check=True)
                            i.then_inc(sT.h, 1); ctT += 1
                        for t2 in range(2):
                            nc.scalar.wait_ge(sT.h, aTx.to(bT0 + rT + rT * hh + t2 + 1))
                            nc.scalar.wait_ge(sV.h, aVx.to(bV0 + rV * hh + 4))
                            nc.scalar.copy(QF[:, t2 * 512:(t2 + 1) * 512], PQ2[t2][:, :]).then_inc(sA.h, 1)
                            ctA += 1
                        nc.vector.wait_ge(sA.h, vAx.to(bA0 + rA + rA * hh + 2))
                        nc.vector.wait_ge(sT.h, vTx.to(bT0 + rT + rT * hh))
                        nc.vector.tensor_mul(R1[:, :], QF[:, :],
                                             cosE[:, HALO:HALO + NOWN]).then_inc(sV.h, 1); ctV += 1
                        nc.vector.tensor_mul(R2[0:64, :], QF[64:128, :],
                                             sinE[64:128, HALO:HALO + NOWN]).then_inc(sV.h, 1); ctV += 1
                        nc.vector.tensor_mul(R2[64:128, :], QF[0:64, :],
                                             sinE[0:64, HALO:HALO + NOWN]).then_inc(sV.h, 1); ctV += 1
                        nc.vector.wait_ge(sV.h, vVx.to(bV0 + rV + rV * hh + ctV))
                        nc.vector.tensor_add(QR[:, :], R1[:, :], R2[:, :]).then_inc(sV.h, 1); ctV += 1
                        for wi in range(5):
                            qoff, qw = WIN_Q[wi]
                            kb = 2 * wi
                            woff = WIN_WOFF[wi]
                            for kc in range(4):
                                g = 4 * wi + kc
                                if wi == 0 and kc == 0:
                                    nc.tensor.wait_ge(sV.h, tVx.to(bV0 + rV + rV * hh + 4))
                                nc.tensor.wait_ge(sA.h, tAx.to(bA0 + rA + rA * hh + 2 + g - 1))
                                i = nc.tensor.matmul(PST[kc % 2][:, 0:qw],
                                                     KRT[:, (kb + kc) * 128:(kb + kc + 1) * 128],
                                                     QR[:, qoff:qoff + qw],
                                                     start=True, stop=True, skip_group_check=True)
                                i.then_inc(sT.h, 1); ctT += 1
                                nc.scalar.wait_ge(sT.h, aTx.to(bT0 + rT + rT * hh + 2 + 6 * wi + kc + 1))
                                nc.scalar.activation(SGT[wi % 2][:, kc * 512:kc * 512 + qw],
                                                     PST[kc % 2][:, 0:qw], AF.Sigmoid,
                                                     scale=SCALE).then_inc(sA.h, 1); ctA += 1
                            for kc in range(4):
                                g = 4 * wi + kc
                                nc.tensor.wait_ge(sA.h, tAx.to(bA0 + rA + rA * hh + 2 + g + 1))
                                if kc == 0:
                                    if wi == 0:
                                        nc.tensor.wait_ge(sV.h, tVx.to(bV0 + rV + rV * hh))
                                    else:
                                        nc.tensor.wait_ge(sV.h, tVx.to(bV0 + rV + rV * hh + REC[wi - 1]))
                                i = nc.tensor.matmul(PR[:, 0:qw],
                                                     onesm[:, (kb + kc) * 128:(kb + kc + 1) * 128],
                                                     SGT[wi % 2][:, kc * 512:kc * 512 + qw],
                                                     start=(kc == 0), stop=(kc == 3), skip_group_check=True)
                            i.then_inc(sT.h, 1); ctT += 1
                            for kc in range(4):
                                if kc == 0:
                                    if wi == 0:
                                        nc.tensor.wait_ge(sV.h, tVx.to(bV0 + rV + rV * hh))
                                    else:
                                        nc.tensor.wait_ge(sV.h, tVx.to(bV0 + rV + rV * hh + CTR[wi - 1]))
                                i = nc.tensor.matmul(PW[:, 0:qw],
                                                     VTM[:, (kb + kc) * 128:(kb + kc + 1) * 128],
                                                     SGT[wi % 2][:, kc * 512:kc * 512 + qw],
                                                     start=(kc == 0), stop=(kc == 3), skip_group_check=True)
                            i.then_inc(sT.h, 1); ctT += 1
                            nc.vector.wait_ge(sT.h, vTx.to(bT0 + rT + rT * hh + 2 + 6 * wi + 5))
                            if wi > 0:
                                nc.vector.wait_ge(sV.h, vVx.to(bV0 + rV + rV * hh + RCW[wi - 1]))
                            nc.vector.reciprocal(RC[:, 0:qw], PR[:, 0:qw]).then_inc(sV.h, 1); ctV += 1
                            nc.vector.wait_ge(sV.h, vVx.to(bV0 + rV + rV * hh + ctV))
                            nc.vector.tensor_mul(RW[:, 0:qw], RC[:, 0:qw],
                                                 wgt[:, woff:woff + qw]).then_inc(sV.h, 1); ctV += 1
                            nc.vector.wait_ge(sT.h, vTx.to(bT0 + rT + rT * hh + 2 + 6 * wi + 6))
                            nc.vector.wait_ge(sV.h, vVx.to(bV0 + rV + rV * hh + ctV))
                            nc.vector.tensor_mul(CBb[wi % 2][:, 0:qw], PW[:, 0:qw],
                                                 RW[:, 0:qw]).then_inc(sV.h, 1); ctV += 1
                            if wi > 0:
                                aoff = 0 if wi - 1 == 0 else 256
                                if wi == 1:
                                    nc.vector.wait_ge(dS.h, vSs.to(bS0 + 16 + 16 * hh))
                                nc.vector.wait_ge(sV.h, vVx.to(bV0 + rV + rV * hh + ctV))
                                nc.vector.tensor_add(ATH[:, (wi - 1) * 256:wi * 256],
                                                     CBb[(wi - 1) % 2][:, aoff:aoff + 256],
                                                     CBb[wi % 2][:, 0:256]).then_inc(sV.h, 1); ctV += 1
                        assert ctT == rT and ctA == rA and ctV == rV, (ctT, ctA, ctV)
                        nc.scalar.wait_ge(sV.h, aVx.to(bV0 + rV + rV * hh + rV))
                        with free_temps(nc, mybir.EngineType.Activation):
                            nc.scalar.dma_start(out=bass.AP(attnsc.tensor, aAo.to(hh * 128 * NOWN),
                                                            [[NOWN, 128], [1, NOWN]]),
                                                in_=ATH[:, :]).then_inc(dS.h, 16)

                    drive(nc, NH, unroll,
                          [(yTq, rT), (yWq, 128 * D), (tLd, 16), (tAx, rA), (tVx, rV),
                           (aTx, rT), (aVx, rV), (aAo, 128 * NOWN), (vAx, rA), (vTx, rT),
                           (vVx, rV), (vSs, 16)],
                          body, f"hd{u}")
                    dL.n += 16 * NH
                    sT.n += rT + rT * NH
                    sA.n += rA + rA * NH
                    sV.n += rV + rV * NH
                    dS.n += 16 + 16 * NH
                    nc.scalar.wait_ge(dS.h, dS.n)
                    nc.sync.wait_ge(dL.h, dL.n)
                nc.all_engine_barrier()

        emit_ln(xe, NEXT, NEXT)
        if debug == 2:
            xnsc = nc.dram_tensor("xnsc", [128, NKC * NEXT], BF16, kind="ExternalOutput").ap()
            dbgL, _ = fresh_dma_sems()
            with nc.Block() as bdbg:
                @bdbg.sync
                def _(s):
                    s.dma_start(out=xnsc[:, :], in_=XN[:, :]).then_inc(dbgL.h, 16)
                    s.wait_ge(dbgL.h, 16)
            dbgL.n += 16
            nc.all_engine_barrier()
        emit_sweep1(NEXT, NEXT)
        emit_sweep2(NOC, dT_sc, hsc, NEXT, NEXT, xe, NEXT, 0, x1sc, NEXT, 0, dsh_dram)
        emit_ln(x1sc, NEXT, NEXT)
        emit_attn()
        emit_sweep2(NKC, oT_sc, attnsc, NOWN, NOWN, x1sc, NEXT, HALO, x2sc, NOWN, 0, oneI)
        emit_ln(x2sc, NOWN, NOWN)
        emit_sweep1(NOWN, NOWN)
        emit_sweep2(NOC, dT_sc, hsc, NEXT, NOWN, x2sc, NOWN, 0, yT, NOWN, 0, dsh_dram)
    return nc



def _rope_tables():
    base = 10000.0
    Dh = 128
    inv_freq = 1.0 / base ** (np.arange(0, Dh, 2, dtype=np.float32) / Dh)
    t = np.arange(S, dtype=np.float32)
    f = np.outer(t, inv_freq)
    cos = np.zeros((S, Dh), dtype=np.float32)
    sin = np.zeros((S, Dh), dtype=np.float32)
    cos[:, 0::2] = np.cos(f)
    cos[:, 1::2] = np.cos(f)
    sin[:, 0::2] = np.sin(f)
    sin[:, 1::2] = np.sin(f)
    return cos, sin


def _host_prep(inputs):
    x = np.asarray(inputs["L0_attn_x"])
    Wq = np.ascontiguousarray(np.asarray(inputs["L0_attn_Wq"], dtype=np.float32))
    Wk = np.ascontiguousarray(np.asarray(inputs["L0_attn_Wk"], dtype=np.float32))
    Wv = np.ascontiguousarray(np.asarray(inputs["L0_attn_Wv"], dtype=np.float32))
    Wo = np.ascontiguousarray(np.asarray(inputs["L0_attn_Wo"], dtype=np.float32))
    gw32 = np.asarray(inputs["L0_ffn_gate_wq"])
    uw32 = np.asarray(inputs["L0_ffn_up_wq"])
    dw32 = np.asarray(inputs["L0_ffn_down_wq"])
    gs = np.asarray(inputs["L0_ffn_gate_scale"], dtype=np.float32)
    us = np.asarray(inputs["L0_ffn_up_scale"], dtype=np.float32)
    dsc = np.asarray(inputs["L0_ffn_down_scale"], dtype=np.float32)

    def lo16(a):
        return np.ascontiguousarray(a.view(np.int16)[:, 0::2])

    gw16, uw16, dw16 = lo16(gw32), lo16(uw32), lo16(dw32)
    xT = np.ascontiguousarray(x[0].T.astype(np.float32))

    gsr = gs.reshape(NOC, 128).T
    usr = us.reshape(NOC, 128).T
    gusr = np.empty((128, 2 * NOC), dtype=np.float32)
    gusr[:, 0::2] = gsr
    gusr[:, 1::2] = usr
    gusr = np.ascontiguousarray(gusr)
    dsr = np.ascontiguousarray(dsc.reshape(NKC, 128).T)
    onev = np.ones((128, NKC), dtype=np.float32)

    cos, sin = _rope_tables()
    alpha = np.linspace(0.0, 1.0, 256, dtype=np.float32)
    idn = np.eye(128, dtype=ml_dtypes.bfloat16)

    in_maps = []
    for c in range(8):
        g0 = c * NOWN - HALO
        xev = np.zeros((D, NEXT), dtype=np.float32)
        lo, hi = max(0, g0), min(S, g0 + NEXT)
        xev[:, lo - g0:hi - g0] = xT[:, lo:hi]
        pos = np.clip(np.arange(g0, g0 + NEXT), 0, S - 1)
        cosEv = np.ascontiguousarray(cos[pos].T)
        sinv = sin[pos].T
        sintbl = np.empty_like(sinv)
        sintbl[64:128] = -sinv[0:64]
        sintbl[0:64] = sinv[64:128]
        sintbl = np.ascontiguousarray(sintbl)
        wgtbl = np.zeros((2048,), dtype=np.float32)
        for wi in range(5):
            off = WIN_WOFF[wi]
            if wi == 0:
                wgtbl[off:off + 256] = 0.0 if c == 0 else 1.0 - alpha
            elif wi == 4:
                wgtbl[off:off + 256] = alpha
            else:
                wgtbl[off:off + 256] = 1.0 if (c == 0 and wi == 1) else alpha
                wgtbl[off + 256:off + 512] = 1.0 - alpha
        wgtblB = np.ascontiguousarray(np.broadcast_to(wgtbl, (128, 2048)))
        onesmv = np.ones((12, 128, 128), dtype=np.float32)
        for jch in range(12):
            kk = np.arange(g0 + jch * 128, g0 + jch * 128 + 128)
            onesmv[jch, (kk < 0) | (kk >= S), :] = 0.0
        onesmv = np.ascontiguousarray(
            onesmv.transpose(1, 0, 2).reshape(128, 12 * 128)).astype(ml_dtypes.bfloat16)
        in_maps.append({
            "xe": xev, "gw": gw16, "uw": uw16, "dw": dw16,
            "wq": Wq, "wk": Wk, "wv": Wv, "wo": Wo,
            "gus": gusr, "ds": dsr, "onev": onev,
            "cosE": cosEv, "sinE": sintbl, "wgt": wgtblB,
            "onesm": onesmv, "ident": idn,
        })
    return in_maps


def kernel(**inputs):
    in_maps = _host_prep(inputs)
    nc = build_nc(debug=False, unroll=False)
    res = run_bass_kernel_spmd(nc, in_maps, core_ids=list(range(8)))
    out = np.empty((1, S, D), dtype=np.float32)
    for c in range(8):
        out[0, c * NOWN:(c + 1) * NOWN, :] = res.results[c]["yT"].T
    return out
